# version 03
# baseline (speedup 1.0000x reference)
import numpy as np
import ml_dtypes
import concourse.bass as bass
import concourse.mybir as mybir
from concourse.bass_utils import run_bass_kernel_spmd

F32 = mybir.dt.float32
BF16 = mybir.dt.bfloat16
I32 = mybir.dt.int32
AF = mybir.ActivationFunctionType
ALU = mybir.AluOpType
AX = mybir.AxisListType
BF = ml_dtypes.bfloat16

NCORES = 8
TOK = 1024
D = 2048
NF = D // 128
LN_EPS = 1e-5


class Pr:

    def __init__(self):
        self.nc = bass.Bass("TRN2", target_bir_lowering=False, debug=False)
        self.map = {}
        self.cnt = {}
        self.free = []
        self.prefix = ""

    def _resolve(self, name):
        return name[1:] if name.startswith("/") else self.prefix + name

    def _virgin(self, name):
        return ("_d_" in name or name.startswith("d_") or "_p_d" in name
                or name.endswith("cc"))

    def sem(self, name):
        name = self._resolve(name)
        if name not in self.map:
            if self.free and not self._virgin(name):
                h, base = self.free.pop()
            else:
                h, base = self.nc.alloc_semaphore(name), 0
            self.map[name] = (h, base)
            self.cnt[name] = 0
        return self.map[name][0]

    def wge(self, eng, name, v):
        h = self.sem(name)
        base = self.map[self._resolve(name)][1]
        eng.wait_ge(h, base + v)

    def inc(self, name, inst, n=1):
        inst.then_inc(self.sem(name), n)
        self.cnt[self._resolve(name)] += n
        return inst

    def wait(self, eng, name, val=None):
        n = self._resolve(name)
        self.sem(name)
        v = self.cnt[n] if val is None else val
        eng.wait_ge(self.map[n][0], self.map[n][1] + v)

    def val(self, name):
        self.sem(name)
        return self.cnt[self._resolve(name)]

    def chain(self, eng, name, inst):
        self.inc(name, inst)
        self.wait(eng, name)

    def release_prefix(self):
        pfx = self.prefix
        for n in [k for k in self.map if k.startswith(pfx) and pfx]:
            h, base = self.map.pop(n)
            c = self.cnt.pop(n)
            if not self._virgin(n):
                self.free.append((h, base + c))

    def barrier(self, fincell):
        nc = self.nc
        pfx = self.prefix
        names = [n for n, c in self.cnt.items() if n.startswith(pfx) and c > 0]
        with nc.Block() as blk:
            @blk.vector
            def _(eng):
                for n in names:
                    eng.wait_ge(self.map[n][0], self.map[n][1] + self.cnt[n])
                self.inc("/fin", eng.tensor_copy(fincell[0:1, 0:1], fincell[0:1, 1:2]))
        fh, fb = self.map["fin"]
        fv = fb + self.cnt["fin"]
        with nc.Block() as blk:
            @blk.sync
            def _(eng):
                eng.wait_ge(fh, fv)
            @blk.gpsimd
            def _(eng):
                eng.wait_ge(fh, fv)
            @blk.scalar
            def _(eng):
                eng.wait_ge(fh, fv)
            @blk.tensor
            def _(eng):
                eng.wait_ge(fh, fv)


def emit_ln(pr, tag, xres, xln, ones_f, sq, row_m, row_v, ps_sum, ps_sq, ps_bc,
            ncols, waits, eps_ap):
    nc = pr.nc
    nch = (ncols + 511) // 512
    s_sq = tag + "sq"
    s_pe = tag + "pe"
    s_st = tag + "st"
    s_dv = tag + "dv"
    s_rt = tag + "rt"
    s_bc = tag + "bc"
    s_dn = tag + "dn"
    b_sq = pr.val(s_sq)
    b_pe = pr.val(s_pe)

    with nc.Block() as blk:
        @blk.scalar
        def _(eng):
            for s, v in waits:
                pr.wge(eng, s, v)
            for j in range(NF):
                half = j % 2
                if j >= 2:
                    pr.wge(eng, s_pe, b_pe + j - 1)
                pr.inc(s_sq, eng.activation(sq[:, half * ncols:(half + 1) * ncols],
                                            xres[:, j * ncols:(j + 1) * ncols], AF.Square))

        @blk.tensor
        def _(eng):
            for s, v in waits:
                pr.wge(eng, s, v)
            for j in range(NF):
                half = j % 2
                xt = xres[:, j * ncols:(j + 1) * ncols]
                for c in range(nch):
                    c0, c1 = c * 512, min((c + 1) * 512, ncols)
                    eng.matmul(ps_sum[0:1, c0:c1], ones_f[:, 0:1], xt[:, c0:c1],
                               start=(j == 0), stop=(j == NF - 1))
                pr.wge(eng, s_sq, b_sq + j + 1)
                sqt = sq[:, half * ncols:(half + 1) * ncols]
                inst = None
                for c in range(nch):
                    c0, c1 = c * 512, min((c + 1) * 512, ncols)
                    inst = eng.matmul(ps_sq[0:1, c0:c1], ones_f[:, 0:1], sqt[:, c0:c1],
                                      start=(j == 0), stop=(j == NF - 1))
                pr.inc(s_pe, inst)

    with nc.Block() as blk:
        @blk.vector
        def _(eng):
            pr.wge(eng, s_pe, b_pe + NF)
            pr.chain(eng, "vq", eng.tensor_scalar(row_m[0:1, 0:ncols], ps_sum[0:1, 0:ncols], 1.0 / D, None, ALU.mult))
            pr.chain(eng, "vq", eng.tensor_scalar(row_v[0:1, 0:ncols], ps_sq[0:1, 0:ncols], 1.0 / D, None, ALU.mult))
            pr.chain(eng, "vq", eng.tensor_mul(sq[0:1, 0:ncols], row_m[0:1, 0:ncols], row_m[0:1, 0:ncols]))
            pr.inc(s_dv, eng.tensor_sub(row_v[0:1, 0:ncols], row_v[0:1, 0:ncols],
                                        sq[0:1, 0:ncols]))

        @blk.scalar
        def _(eng):
            pr.wait(eng, s_dv)
            pr.inc(s_rt, eng.activation(row_v[0:1, 0:ncols], row_v[0:1, 0:ncols],
                                        AF.Sqrt, bias=eps_ap))

    with nc.Block() as blk:
        @blk.vector
        def _(eng):
            pr.wait(eng, s_rt)
            pr.inc(s_rt, eng.reciprocal(row_v[0:1, 0:ncols], row_v[0:1, 0:ncols]))

        @blk.tensor
        def _(eng):
            pr.wait(eng, s_rt)
            inst = None
            for c in range(nch):
                c0, c1 = c * 512, min((c + 1) * 512, ncols)
                eng.matmul(ps_bc[:, c0:c1], ones_f[0:1, :], row_m[0:1, c0:c1],
                           start=True, stop=True)
                inst = eng.matmul(ps_bc[:, ncols + c0:ncols + c1], ones_f[0:1, :],
                                  row_v[0:1, c0:c1], start=True, stop=True)
            pr.inc(s_bc, inst)

    with nc.Block() as blk:
        @blk.vector
        def _(eng):
            pr.wait(eng, s_bc)
            inst = None
            for j in range(NF):
                xt = xres[:, j * ncols:(j + 1) * ncols]
                ot = xln[:, j * ncols:(j + 1) * ncols]
                tmp = sq[:, (j % 2) * ncols:((j % 2) + 1) * ncols]
                pr.chain(eng, "vq", eng.tensor_sub(tmp, xt, ps_bc[:, 0:ncols]))
                inst = eng.tensor_mul(ot, tmp, ps_bc[:, ncols:2 * ncols])
                if j + 2 < NF:
                    pr.chain(eng, "vq", inst)
            pr.inc(s_dn, inst)


def emit_l0ffn(pr, io):
    nc = pr.nc
    pr.prefix = "f_"
    ar = nc.auto_arena()
    for _n in ['act_g', 'd_c', 'd_out', 'd_wd', 'd_wg', 'd_wu', 'd_x', 'deq', 'deq_d', 'dve_d', 'dve_gu', 'ln0bc', 'ln0dn', 'ln0dv', 'ln0pe', 'ln0rt', 'ln0sq', 'ln0st', 'pe_d', 'pe_g', 'pe_gu', 'vq']:
        pr.sem(_n)
    xf = io["xf"]
    wgT = io["wgT"]
    wuT = io["wuT"]
    wdT = io["wdT"]
    gsc = io["gsc"]
    usc = io["usc"]
    dsc = io["dsc"]
    onesf = io["onesf"]
    cstf = io["cstf"]
    xo = io["xo"]

    xres = ar.sbuf_tensor("f_xres", [128, NF * TOK], F32)
    xln = ar.sbuf_tensor("f_xln", [128, NF * TOK], BF16)
    sq = ar.sbuf_tensor("f_sq", [128, 2 * TOK], F32)
    row_m = ar.sbuf_tensor("f_row_m", [1, TOK], F32)
    row_v = ar.sbuf_tensor("f_row_v", [1, TOK], F32)
    ones = ar.sbuf_tensor("f_ones", [128, 128], F32)
    cst = ar.sbuf_tensor("f_cst", [128, 8], F32)
    g_sc = ar.sbuf_tensor("f_g_sc", [128, 64], F32)
    u_sc = ar.sbuf_tensor("f_u_sc", [128, 64], F32)
    d_sc = ar.sbuf_tensor("f_d_sc", [128, 16], F32)
    FB = 1024
    NFB = 8192 // FB
    FT = FB // 128
    h = ar.sbuf_tensor("f_h", [128, FT * TOK], BF16)
    hg = ar.sbuf_tensor("f_hg", [128, TOK], BF16)
    wg_i = ar.sbuf_tensor("f_wg_i", [128, 2048], I32)
    wu_i = ar.sbuf_tensor("f_wu_i", [128, 2048], I32)
    wg_b = ar.sbuf_tensor("f_wg_b", [128, 2048], BF16)
    wu_b = ar.sbuf_tensor("f_wu_b", [128, 2048], BF16)
    wd_i = ar.sbuf_tensor("f_wd_i", [128, 2048], I32)
    wd_blk = ar.sbuf_tensor("f_wd_blk", [128, FT * 2048], BF16)

    ar.allocate()
    pool = io["pool"]
    ps_sum = pool[0:1, 0:TOK]
    ps_sq = pool[0:1, TOK:2 * TOK]
    ps_bc = pool[:, 2 * TOK:4 * TOK]
    ps_g = pool[:, 0:TOK]
    ps_u = pool[:, TOK:2 * TOK]

    with nc.Block() as blk:
        @blk.sync
        def _(eng):
            for j in range(NF):
                pr.inc("d_x", eng.dma_start(xres[:, j * TOK:(j + 1) * TOK],
                                            xf[j * 128:(j + 1) * 128, :]), 16)
            pr.inc("d_c", eng.dma_start(ones[:], onesf), 16)
            pr.inc("d_c", eng.dma_start(cst[:], cstf), 16)
            pr.inc("d_c", eng.dma_start(g_sc[:], gsc), 16)
            pr.inc("d_c", eng.dma_start(u_sc[:], usc), 16)
            pr.inc("d_c", eng.dma_start(d_sc[:], dsc), 16)

    emit_ln(pr, "ln0", xres[:], xln[:], ones[:], sq[:], row_m[:], row_v[:],
            ps_sum, ps_sq, ps_bc, TOK,
            [("d_x", pr.val("d_x")), ("d_c", pr.val("d_c"))], cst[0:1, 0:1])

    for fb in range(NFB):
        for j in range(FT):
            ft = fb * FT + j
            with nc.Block() as blk:
                @blk.sync
                def _(eng):
                    if ft > 0:
                        pr.wait(eng, "pe_gu")
                    pr.inc("d_wg", eng.dma_start(
                        wg_i[:].rearrange("p (kc f) -> p kc f", kc=16),
                        wgT[:, ft * 128:(ft + 1) * 128].rearrange("(kc kp) f -> kp kc f", kp=128)), 16)
                    pr.inc("d_wu", eng.dma_start(
                        wu_i[:].rearrange("p (kc f) -> p kc f", kc=16),
                        wuT[:, ft * 128:(ft + 1) * 128].rearrange("(kc kp) f -> kp kc f", kp=128)), 16)

                @blk.scalar
                def _(eng):
                    pr.wait(eng, "d_wg")
                    pr.wait(eng, "d_wu")
                    eng.activation(wg_b[:], wg_i[:], AF.Copy, bias=-128.0)
                    pr.inc("deq", eng.activation(wu_b[:], wu_i[:], AF.Copy, bias=-128.0))
                    pr.wge(eng, "pe_g", ft + 1)
                    pr.wge(eng, "dve_gu", ft)
                    pr.inc("act_g", eng.activation(hg[:], ps_g[:], AF.Silu,
                                                   scale=g_sc[:, ft:ft + 1]))

                @blk.tensor
                def _(eng):
                    pr.wait(eng, "deq")
                    if ft == 0:
                        pr.wait(eng, "ln0dn")
                    if j == 0 and fb > 0:
                        pr.wait(eng, "dve_d")
                    pr.wge(eng, "act_g", ft)
                    pr.wge(eng, "dve_gu", ft)
                    inst = None
                    for tc in range(2):
                        ts = slice(tc * 512, (tc + 1) * 512)
                        for kc in range(16):
                            inst = eng.matmul(ps_g[:, ts], wg_b[:, kc * 128:(kc + 1) * 128],
                                       xln[:, kc * TOK + tc * 512: kc * TOK + (tc + 1) * 512],
                                       start=(kc == 0), stop=(kc == 15))
                    pr.inc("pe_g", inst)
                    for tc in range(2):
                        ts = slice(tc * 512, (tc + 1) * 512)
                        for kc in range(16):
                            inst = eng.matmul(ps_u[:, ts], wu_b[:, kc * 128:(kc + 1) * 128],
                                       xln[:, kc * TOK + tc * 512: kc * TOK + (tc + 1) * 512],
                                       start=(kc == 0), stop=(kc == 15))
                    pr.inc("pe_gu", inst)

                @blk.vector
                def _(eng):
                    pr.wge(eng, "pe_gu", ft + 1)
                    pr.wge(eng, "act_g", ft + 1)
                    if j == 0:
                        pr.wait(eng, "pe_d")
                    pr.inc("dve_gu", eng.scalar_tensor_tensor(
                        h[:, j * TOK:(j + 1) * TOK], ps_u[:],
                        u_sc[:, ft:ft + 1], hg[:], ALU.mult, ALU.mult))

        with nc.Block() as blk:
            @blk.sync
            def _(eng):
                b_deq = pr.val("deq_d")
                for fc in range(FT):
                    gfc = fb * FT + fc
                    if fb > 0 and fc == 0:
                        pr.wait(eng, "pe_d")
                    if b_deq + fc > 0:
                        pr.wge(eng, "deq_d", b_deq + fc)
                    pr.inc("d_wd", eng.dma_start(wd_i[:], wdT[gfc * 128:(gfc + 1) * 128, :]), 16)

            @blk.scalar
            def _(eng):
                b_d = pr.val("d_wd") - 16 * FT
                for fc in range(FT):
                    pr.wge(eng, "d_wd", b_d + 16 * (fc + 1))
                    pr.inc("deq_d", eng.activation(wd_blk[:, fc * 2048:(fc + 1) * 2048],
                                                   wd_i[:], AF.Copy, bias=-128.0))

        b_dve = pr.val("dve_d")
        b_pe = pr.val("pe_d")
        for dg in range(4):
            with nc.Block() as blk:
                @blk.tensor
                def _(eng):
                    pr.wait(eng, "deq_d")
                    if dg == 0:
                        pr.wait(eng, "dve_gu")
                    if dg > 0:
                        pr.wge(eng, "dve_d", b_dve + 4 * dg)
                    inst = None
                    for dl in range(4):
                        dt = dg * 4 + dl
                        psd = pool[:, dl * TOK:(dl + 1) * TOK]
                        for fc in range(FT):
                            for tc in range(2):
                                ts = slice(tc * 512, (tc + 1) * 512)
                                inst = eng.matmul(psd[:, ts],
                                           wd_blk[:, fc * 2048 + dt * 128: fc * 2048 + (dt + 1) * 128],
                                           h[:, fc * TOK + tc * 512: fc * TOK + (tc + 1) * 512],
                                           start=(fc == 0), stop=(fc == FT - 1))
                    pr.inc("pe_d", inst)

                @blk.vector
                def _(eng):
                    pr.wge(eng, "pe_d", b_pe + dg + 1)
                    for dl in range(4):
                        dt = dg * 4 + dl
                        psd = pool[:, dl * TOK:(dl + 1) * TOK]
                        pr.inc("dve_d", eng.scalar_tensor_tensor(
                            xres[:, dt * TOK:(dt + 1) * TOK], psd,
                            d_sc[:, dt:dt + 1], xres[:, dt * TOK:(dt + 1) * TOK],
                            ALU.mult, ALU.add))

    with nc.Block() as blk:
        @blk.sync
        def _(eng):
            pr.wait(eng, "dve_d")
            for j in range(NF):
                pr.inc("d_out", eng.dma_start(xo[j * 128:(j + 1) * 128, :],
                                              xres[:, j * TOK:(j + 1) * TOK]), 16)
            pr.wait(eng, "d_out")
    ar.free()


SCALE_ATT = 0.08838834764831845


def emit_l1attn(pr, io):
    nc = pr.nc
    pr.prefix = "t_"
    ar = nc.auto_arena()
    for _n in ['d_c', 'd_kv', 'd_out', 'd_wo', 'd_wq', 'd_x', 'ln1bc', 'ln1dn', 'ln1dv', 'ln1pe', 'ln1rt', 'ln1sq', 'ln1st', 'pe_n', 'pe_nb', 'pe_pv', 'pe_qm', 'pe_rr', 'pe_s', 'pe_w', 's_exp', 's_kq', 's_rt2', 'v_ao', 'v_bb', 'v_kc', 'v_n', 'v_qf', 'v_rr', 'v_vc', 'v_wb', 'v_wo', 'vq']:
        pr.sem(_n)
    xf = io["xf"]
    kTf = io["kTf"]
    vTf = io["vTf"]
    wqT = io["wqT"]
    woT = io["woT"]
    onesf = io["onesf"]
    onesb = io["onesb"]
    cstf = io["cstf"]
    xo = io["xo"]

    xres = ar.sbuf_tensor("t_xres", [128, NF * TOK], F32)
    xab = ar.sbuf_tensor("t_xab", [128, NF * TOK], BF16)
    qf = ar.sbuf_tensor("t_qf", [128, NF * TOK], BF16)
    sq = ar.sbuf_tensor("t_sq", [128, 2 * TOK], F32)
    row_m = ar.sbuf_tensor("t_row_m", [1, TOK], F32)
    row_v = ar.sbuf_tensor("t_row_v", [1, TOK], F32)
    srow = ar.sbuf_tensor("t_srow", [1, TOK], F32)
    ones = ar.sbuf_tensor("t_ones", [128, 128], F32)
    onebv = ar.sbuf_tensor("t_onebv", [128, 128], BF16)
    cst = ar.sbuf_tensor("t_cst", [128, 8], F32)
    stg = ar.sbuf_tensor("t_stg", [128, 2048], F32)
    w_b = ar.sbuf_tensor("t_w_b", [128, 2048], BF16)
    kb = ar.sbuf_tensor("t_kb", [128, 4096], BF16)
    vb = ar.sbuf_tensor("t_vb", [128, 4096], BF16)
    ptq = ar.sbuf_tensor("t_ptq", [128, 8 * TOK], BF16)
    rrb = ar.sbuf_tensor("t_rrb", [128, TOK], F32)
    negb = ar.sbuf_tensor("t_negb", [128, 1], F32)

    ar.allocate()
    pool = io["pool"]
    ps_sum = pool[0:1, 0:TOK]
    ps_sq = pool[0:1, TOK:2 * TOK]
    ps_bc = pool[:, 2 * TOK:4 * TOK]
    ps_q = pool[:, 0:TOK]
    ps_o = pool[:, TOK:2 * TOK]
    ps_row = pool[0:1, 2 * TOK:3 * TOK]
    ps_n = pool[0:1, 3 * TOK:4 * TOK]
    ps_nb = pool[:, 3 * TOK:3 * TOK + 1]

    with nc.Block() as blk:
        @blk.sync
        def _(eng):
            for j in range(NF):
                pr.inc("d_x", eng.dma_start(xres[:, j * TOK:(j + 1) * TOK],
                                            xf[j * 128:(j + 1) * 128, :]), 16)
            pr.inc("d_c", eng.dma_start(ones[:], onesf), 16)
            pr.inc("d_c", eng.dma_start(onebv[:], onesb), 16)
            pr.inc("d_c", eng.dma_start(cst[:], cstf), 16)

    emit_ln(pr, "ln1", xres[:], xab[:], ones[:], sq[:], row_m[:], row_v[:],
            ps_sum, ps_sq, ps_bc, TOK,
            [("d_x", pr.val("d_x")), ("d_c", pr.val("d_c"))], cst[0:1, 0:1])

    for od in range(16):
        with nc.Block() as blk:
            @blk.sync
            def _(eng):
                if od > 0:
                    pr.wge(eng, "v_wb", od)
                pr.inc("d_wq", eng.dma_start(
                    stg[:].rearrange("p (kc f) -> p kc f", kc=16),
                    wqT[:, od * 128:(od + 1) * 128].rearrange("(kc kp) f -> kp kc f", kp=128)), 16)

            @blk.vector
            def _(eng):
                pr.wge(eng, "d_wq", 16 * (od + 1))
                if od > 0:
                    pr.wge(eng, "pe_qm", od)
                pr.inc("v_wb", eng.tensor_copy(w_b[:], stg[:]))
                pr.wge(eng, "pe_qm", od + 1)
                pr.inc("v_qf", eng.tensor_copy(qf[:, od * TOK:(od + 1) * TOK], ps_q[:]))

            @blk.tensor
            def _(eng):
                pr.wge(eng, "v_wb", od + 1)
                if od == 0:
                    pr.wait(eng, "ln1dn")
                pr.wge(eng, "v_qf", od)
                inst = None
                for tc in range(2):
                    ts = slice(tc * 512, (tc + 1) * 512)
                    for kc in range(16):
                        inst = eng.matmul(ps_q[:, ts], w_b[:, kc * 128:(kc + 1) * 128],
                                          xab[:, kc * TOK + tc * 512: kc * TOK + (tc + 1) * 512],
                                          start=(kc == 0), stop=(kc == 15))
                pr.inc("pe_qm", inst)

    for h in range(16):
        with nc.Block() as blk:
            @blk.sync
            def _(eng):
                if h == 0:
                    pr.wge(eng, "v_wb", 16)
                else:
                    pr.wge(eng, "v_vc", 2 * h)
                pr.inc("d_kv", eng.dma_start(stg[:], kTf[h, :, 0:2048]), 16)
                pr.wge(eng, "pe_n", 5 * h + 2)
                pr.inc("d_kv", eng.dma_start(stg[:], kTf[h, :, 2048:4096]), 16)
                pr.wge(eng, "pe_n", 5 * h + 4)
                pr.inc("d_kv", eng.dma_start(
                    stg[:].rearrange("p (c d) -> p c d", c=16),
                    vTf[h, 0:2048, :].rearrange("(c p) d -> p c d", p=128)), 16)
                pr.wge(eng, "v_vc", 2 * h + 1)
                pr.inc("d_kv", eng.dma_start(
                    stg[:].rearrange("p (c d) -> p c d", c=16),
                    vTf[h, 2048:4096, :].rearrange("(c p) d -> p c d", p=128)), 16)

            @blk.scalar
            def _(eng):
                if h == 0:
                    pr.wait(eng, "ln1dn")
                for half in range(2):
                    pr.wge(eng, "v_kc", 2 * h + half + 1)
                    pr.inc("s_kq", eng.activation(stg[:], kb[:, half * 2048:(half + 1) * 2048],
                                                  AF.Square))
                pr.wge(eng, "pe_n", 5 * h)
                pr.wge(eng, "v_qf", h + 1)
                pr.inc("s_kq", eng.activation(sq[:, 0:TOK], qf[:, h * TOK:(h + 1) * TOK],
                                              AF.Square))
                pr.wge(eng, "v_n", 7 * h + 6)
                pr.inc("s_rt2", eng.activation(row_m[0:1, 3:4], row_m[0:1, 2:3], AF.Sqrt))
                for qt in range(4):
                    for lc in range(8):
                        ktc = qt * 8 + lc
                        pr.wge(eng, "pe_s", 32 * h + ktc + 1)
                        if ktc == 0:
                            pr.wge(eng, "v_bb", h + 1)
                        if qt > 0:
                            pr.wge(eng, "pe_pv", 32 * h + (qt - 1) * 8 + lc + 1)
                        pr.inc("s_exp", eng.activation(
                            ptq[:, lc * TOK:(lc + 1) * TOK], ps_q[:], AF.Exp,
                            bias=negb[:, 0:1]))

            @blk.tensor
            def _(eng):
                for half in range(2):
                    pr.wge(eng, "s_kq", 3 * h + half + 1)
                    for c in range(2):
                        i = 2 * half + c
                        pr.wge(eng, "v_n", 7 * h + i)
                        inst = None
                        for sc in range(2):
                            cs = slice(c * 1024 + sc * 512, c * 1024 + (sc + 1) * 512)
                            os_ = slice(sc * 512, (sc + 1) * 512)
                            inst = eng.matmul(ps_n[0:1, os_], ones[:, 0:1], stg[:, cs],
                                              start=True, stop=True)
                        pr.inc("pe_n", inst)
                pr.wge(eng, "s_kq", 3 * h + 3)
                pr.wge(eng, "v_n", 7 * h + 4)
                inst = None
                for sc in range(2):
                    os_ = slice(sc * 512, (sc + 1) * 512)
                    inst = eng.matmul(ps_n[0:1, os_], ones[:, 0:1], sq[:, sc * 512:(sc + 1) * 512],
                                      start=True, stop=True)
                pr.inc("pe_n", inst)
                pr.wge(eng, "v_n", 7 * h + 7)
                pr.inc("pe_nb", eng.matmul(ps_nb[:, 0:1], ones[0:1, :], row_m[0:1, 4:5],
                                           start=True, stop=True))
                for ktc in range(32):
                    if ktc == 0:
                        pr.wge(eng, "v_kc", 2 * h + 2)
                        pr.wge(eng, "v_rr", 2 * h)
                        pr.wge(eng, "v_ao", h)
                        if h == 0:
                            pr.wge(eng, "v_qf", 16)
                    pr.wge(eng, "s_exp", 32 * h + ktc)
                    inst = None
                    for tc in range(2):
                        ts = slice(tc * 512, (tc + 1) * 512)
                        inst = eng.matmul(ps_q[:, ts], kb[:, ktc * 128:(ktc + 1) * 128],
                                          qf[:, h * TOK + tc * 512: h * TOK + (tc + 1) * 512],
                                          start=True, stop=True)
                    pr.inc("pe_s", inst)
                    lc = ktc % 8
                    pr.wge(eng, "s_exp", 32 * h + ktc + 1)
                    pr.wge(eng, "v_vc", 2 * h + (1 if ktc < 16 else 2))
                    for tc in range(2):
                        ts = slice(tc * 512, (tc + 1) * 512)
                        eng.matmul(ps_row[0:1, ts], onebv[:, 0:1],
                                   ptq[:, lc * TOK + tc * 512: lc * TOK + (tc + 1) * 512],
                                   start=(ktc == 0), stop=(ktc == 31))
                    inst = None
                    for tc in range(2):
                        ts = slice(tc * 512, (tc + 1) * 512)
                        inst = eng.matmul(ps_o[:, ts], vb[:, ktc * 128:(ktc + 1) * 128],
                                          ptq[:, lc * TOK + tc * 512: lc * TOK + (tc + 1) * 512],
                                          start=(ktc == 0), stop=(ktc == 31))
                    pr.inc("pe_pv", inst)
                pr.wge(eng, "v_rr", 2 * h + 1)
                pr.wge(eng, "s_exp", 32 * (h + 1))
                inst = None
                for sc in range(2):
                    os_ = slice(sc * 512, (sc + 1) * 512)
                    inst = eng.matmul(ps_q[:, os_], ones[0:1, :], srow[0:1, os_],
                                      start=True, stop=True)
                pr.inc("pe_rr", inst)

            @blk.vector
            def _(eng):
                for half in range(2):
                    pr.wge(eng, "d_kv", 64 * h + 16 * (half + 1))
                    if h > 0 and half == 0:
                        pr.wge(eng, "pe_pv", 32 * h)
                    pr.inc("v_kc", eng.tensor_scalar(
                        kb[:, half * 2048:(half + 1) * 2048], stg[:], SCALE_ATT, None, ALU.mult))
                    for c in range(2):
                        i = 2 * half + c
                        pr.wge(eng, "pe_n", 5 * h + i + 1)
                        if i == 0:
                            pr.inc("v_n", eng.reduce_max(row_m[0:1, 0:1], ps_n[0:1, 0:TOK],
                                                         axis=AX.X))
                        else:
                            pr.chain(eng, "vq", eng.reduce_max(row_m[0:1, 5:6], ps_n[0:1, 0:TOK],
                                                               axis=AX.X))
                            pr.wge(eng, "v_n", 7 * h + i)
                            pr.inc("v_n", eng.tensor_max(row_m[0:1, 0:1], row_m[0:1, 0:1],
                                                         row_m[0:1, 5:6]))
                for half in range(2):
                    pr.wge(eng, "d_kv", 64 * h + 32 + 16 * (half + 1))
                    if h > 0 and half == 0:
                        pr.wge(eng, "pe_pv", 32 * h)
                    pr.inc("v_vc", eng.tensor_copy(vb[:, half * 2048:(half + 1) * 2048], stg[:]))
                pr.wge(eng, "pe_n", 5 * h + 5)
                pr.inc("v_n", eng.reduce_max(row_m[0:1, 1:2], ps_n[0:1, 0:TOK], axis=AX.X))
                pr.wge(eng, "v_n", 7 * h + 5)
                pr.inc("v_n", eng.tensor_mul(row_m[0:1, 2:3], row_m[0:1, 0:1], row_m[0:1, 1:2]))
                pr.wge(eng, "s_rt2", h + 1)
                pr.inc("v_n", eng.tensor_scalar(row_m[0:1, 4:5], row_m[0:1, 3:4], -1.0, None,
                                                ALU.mult))
                pr.wge(eng, "pe_nb", h + 1)
                pr.inc("v_bb", eng.tensor_copy(negb[:], ps_nb[:, 0:1]))
                pr.wge(eng, "pe_pv", 32 * (h + 1))
                pr.inc("v_rr", eng.reciprocal(srow[0:1, 0:TOK], ps_row[0:1, 0:TOK]))
                pr.wge(eng, "pe_rr", h + 1)
                pr.inc("v_rr", eng.tensor_copy(rrb[:], ps_q[:]))
                pr.wge(eng, "v_rr", 2 * h + 2)
                pr.inc("v_ao", eng.tensor_mul(xab[:, h * TOK:(h + 1) * TOK], ps_o[:], rrb[:]))

    for od in range(16):
        with nc.Block() as blk:
            @blk.sync
            def _(eng):
                if od == 0:
                    pr.wge(eng, "v_vc", 32)
                else:
                    pr.wge(eng, "v_wo", 2 * od)
                pr.inc("d_wo", eng.dma_start(
                    stg[:].rearrange("p (kc f) -> p kc f", kc=16),
                    woT[:, od * 128:(od + 1) * 128].rearrange("(kc kp) f -> kp kc f", kp=128)), 16)

            @blk.vector
            def _(eng):
                pr.wge(eng, "d_wo", 16 * (od + 1))
                if od > 0:
                    pr.wge(eng, "pe_w", od)
                pr.inc("v_wo", eng.tensor_copy(w_b[:], stg[:]))
                pr.wge(eng, "pe_w", od + 1)
                pr.inc("v_wo", eng.tensor_add(xres[:, od * TOK:(od + 1) * TOK],
                                              xres[:, od * TOK:(od + 1) * TOK], ps_q[:]))

            @blk.tensor
            def _(eng):
                pr.wge(eng, "v_wo", 2 * od + 1)
                if od == 0:
                    pr.wge(eng, "v_ao", 16)
                    pr.wge(eng, "v_rr", 32)
                else:
                    pr.wge(eng, "v_wo", 2 * od)
                inst = None
                for tc in range(2):
                    ts = slice(tc * 512, (tc + 1) * 512)
                    for kc in range(16):
                        inst = eng.matmul(ps_q[:, ts], w_b[:, kc * 128:(kc + 1) * 128],
                                          xab[:, kc * TOK + tc * 512: kc * TOK + (tc + 1) * 512],
                                          start=(kc == 0), stop=(kc == 15))
                pr.inc("pe_w", inst)

    with nc.Block() as blk:
        @blk.sync
        def _(eng):
            pr.wait(eng, "v_wo")
            for j in range(NF):
                pr.inc("d_out", eng.dma_start(xo[j * 128:(j + 1) * 128, :],
                                              xres[:, j * TOK:(j + 1) * TOK]), 16)
            pr.wait(eng, "d_out")
    ar.free()


def emit_l1ffn(pr, io):
    nc = pr.nc
    pr.prefix = "m_"
    ar = nc.auto_arena()
    for _n in ['d_c', 'd_gw', 'd_out', 'd_wd2', 'd_we', 'd_x', 'ln2bc', 'ln2dn', 'ln2dv', 'ln2pe', 'ln2rt', 'ln2sq', 'ln2st', 'pe_cb', 'pe_ct', 'pe_ed', 'pe_eg', 'pe_egu', 'pe_r', 's_r', 's_sil', 'v_cb', 'v_ct', 'v_ed', 'v_gw', 'v_h', 'v_r', 'v_wc', 'v_wdc', 'vq']:
        pr.sem(_n)
    xf = io["xf"]
    gwr = io["gwr"]
    weg = io["weg"]
    weu = io["weu"]
    wed = io["wed"]
    swg = io["swg"]
    swu = io["swu"]
    swd = io["swd"]
    onesf = io["onesf"]
    identf = io["identf"]
    cstf = io["cstf"]
    xo = io["xo"]

    NFT = 11
    xres = ar.sbuf_tensor("m_xres", [128, NF * TOK], F32)
    xln = ar.sbuf_tensor("m_xln", [128, NF * TOK], BF16)
    sq = ar.sbuf_tensor("m_sq", [128, 2 * TOK], F32)
    row_m = ar.sbuf_tensor("m_row_m", [1, TOK], F32)
    row_v = ar.sbuf_tensor("m_row_v", [1, TOK], F32)
    ones = ar.sbuf_tensor("m_ones", [128, 128], F32)
    ident = ar.sbuf_tensor("m_ident", [128, 128], F32)
    cst = ar.sbuf_tensor("m_cst", [128, 8], F32)
    gwb = ar.sbuf_tensor("m_gwb", [128, 128], BF16)
    h = ar.sbuf_tensor("m_h", [128, NFT * TOK], BF16)
    hg = ar.sbuf_tensor("m_hg", [128, TOK], BF16)
    wstg = ar.sbuf_tensor("m_wstg", [128, 2048], F32)
    wgb = ar.sbuf_tensor("m_wgb", [128, 2048], BF16)
    wub = ar.sbuf_tensor("m_wub", [128, 2048], BF16)
    wd_blk = ar.sbuf_tensor("m_wd_blk", [128, NFT * 1024], BF16)
    comb_tm = ar.sbuf_tensor("m_comb_tm", [128, 64], F32)
    probs = ar.sbuf_tensor("m_probs", [128, 8], F32)
    scs = ar.sbuf_tensor("m_scs", [128, 8], F32)
    colm = ar.sbuf_tensor("m_colm", [128, 4], F32)
    crows = ar.sbuf_tensor("m_crows", [1, TOK], F32)
    combB = ar.sbuf_tensor("m_combB", [128, TOK], F32)
    tmpb = ar.sbuf_tensor("m_tmpb", [128, TOK], F32)

    ar.allocate()
    pool = io["pool"]
    ps_sum = pool[0:1, 0:TOK]
    ps_sq = pool[0:1, TOK:2 * TOK]
    ps_bc = pool[:, 2 * TOK:4 * TOK]
    ps_l = pool[:, 0:8]
    ps_g = pool[:, 0:TOK]
    ps_u = pool[:, TOK:2 * TOK]
    ps_ct = pool[0:1, 3 * TOK:3 * TOK + 128]
    ps_cb = pool[:, 3 * TOK:4 * TOK]

    with nc.Block() as blk:
        @blk.sync
        def _(eng):
            for j in range(NF):
                pr.inc("d_x", eng.dma_start(xres[:, j * TOK:(j + 1) * TOK],
                                            xf[j * 128:(j + 1) * 128, :]), 16)
            pr.inc("d_c", eng.dma_start(ones[:], onesf), 16)
            pr.inc("d_c", eng.dma_start(ident[:], identf), 16)
            pr.inc("d_c", eng.dma_start(cst[:], cstf), 16)

    with nc.Block() as blk:
        @blk.sync
        def _(eng):
            pr.inc("d_gw", eng.dma_start(wstg[:, 0:128], gwr), 16)

        @blk.vector
        def _(eng):
            pr.wait(eng, "d_gw")
            pr.inc("v_gw", eng.tensor_copy(gwb[:], wstg[:, 0:128]))

    emit_ln(pr, "ln2", xres[:], xln[:], ones[:], sq[:], row_m[:], row_v[:],
            ps_sum, ps_sq, ps_bc, TOK,
            [("d_x", pr.val("d_x")), ("d_c", pr.val("d_c"))], cst[0:1, 0:1])

    for tt in range(8):
        with nc.Block() as blk:
            @blk.tensor
            def _(eng):
                if tt == 0:
                    pr.wait(eng, "ln2dn")
                    pr.wait(eng, "v_gw")
                pr.wge(eng, "v_r", 2 * tt)
                inst = None
                for kc in range(16):
                    inst = eng.matmul(ps_l[:, 0:8], xln[:, kc * TOK + tt * 128: kc * TOK + (tt + 1) * 128],
                                      gwb[:, kc * 8:(kc + 1) * 8],
                                      start=(kc == 0), stop=(kc == 15))
                pr.inc("pe_r", inst)

            @blk.scalar
            def _(eng):
                pr.wge(eng, "v_r", 2 * tt + 1)
                pr.inc("s_r", eng.activation(probs[:], ps_l[:, 0:8], AF.Exp,
                                             bias=colm[:, 1:2]))

            @blk.vector
            def _(eng):
                pr.wge(eng, "pe_r", tt + 1)
                pr.chain(eng, "vq", eng.reduce_max(colm[:, 0:1], ps_l[:, 0:8], axis=AX.X))
                pr.inc("v_r", eng.tensor_scalar(colm[:, 1:2], colm[:, 0:1], -1.0, None, ALU.mult))
                pr.wge(eng, "s_r", tt + 1)
                pr.chain(eng, "vq", eng.reduce_sum(colm[:, 0:1], probs[:], axis=AX.X))
                pr.chain(eng, "vq", eng.reciprocal(colm[:, 1:2], colm[:, 0:1]))
                pr.chain(eng, "vq", eng.tensor_scalar(scs[:], probs[:], colm[:, 1:2], None, ALU.mult))
                pr.chain(eng, "vq", eng.reduce_max(colm[:, 0:1], scs[:], axis=AX.X))
                pr.chain(eng, "vq", eng.tensor_scalar(probs[:], scs[:], colm[:, 0:1], None, ALU.is_ge))
                pr.chain(eng, "vq", eng.scalar_tensor_tensor(probs[:], probs[:], -1e30, scs[:],
                                                             ALU.mult, ALU.add))
                pr.chain(eng, "vq", eng.reduce_max(colm[:, 2:3], probs[:], axis=AX.X))
                pr.chain(eng, "vq", eng.tensor_scalar(probs[:], scs[:], colm[:, 2:3], None, ALU.is_ge))
                pr.inc("v_r", eng.tensor_mul(comb_tm[:, tt * 8:(tt + 1) * 8], scs[:], probs[:]))

    for r in range(10):
        is_exp = r < 8
        e = r if is_exp else 0
        sh = 0 if is_exp else (r - 8)

        def g_slab(ft):
            if is_exp:
                return weg[e, :, ft * 128:(ft + 1) * 128]
            return swg[:, sh * 1408 + ft * 128: sh * 1408 + (ft + 1) * 128]

        def u_slab(ft):
            if is_exp:
                return weu[e, :, ft * 128:(ft + 1) * 128]
            return swu[:, sh * 1408 + ft * 128: sh * 1408 + (ft + 1) * 128]

        def d_slab(fc):
            if is_exp:
                return wed[e, fc * 128:(fc + 1) * 128, :]
            return swd[sh * 1408 + fc * 128: sh * 1408 + (fc + 1) * 128, :]

        if is_exp:
            with nc.Block() as blk:
                @blk.tensor
                def _(eng):
                    pr.wge(eng, "v_ed", 16 * r)
                    pr.wge(eng, "v_r", 16)
                    for tt in range(8):
                        if tt > 0:
                            pr.wge(eng, "v_ct", 8 * r + tt)
                        pr.inc("pe_ct", eng.transpose(ps_ct[0:1, 0:128],
                                                      comb_tm[:, tt * 8 + e:tt * 8 + e + 1],
                                                      ident[:]))

                @blk.vector
                def _(eng):
                    for tt in range(8):
                        pr.wge(eng, "pe_ct", 8 * r + tt + 1)
                        pr.inc("v_ct", eng.tensor_copy(
                            crows[0:1, tt * 128:(tt + 1) * 128], ps_ct[0:1, 0:128]))

            with nc.Block() as blk:
                @blk.tensor
                def _(eng):
                    pr.wge(eng, "v_ct", 8 * (r + 1))
                    inst = None
                    for sc_ in range(2):
                        os_ = slice(sc_ * 512, (sc_ + 1) * 512)
                        inst = eng.matmul(ps_cb[:, os_], ones[0:1, :], crows[0:1, os_],
                                          start=True, stop=True)
                    pr.inc("pe_cb", inst)

                @blk.vector
                def _(eng):
                    pr.wge(eng, "pe_cb", r + 1)
                    pr.inc("v_cb", eng.tensor_copy(combB[:], ps_cb[:]))

        for ft in range(NFT):
            ftg = r * NFT + ft
            with nc.Block() as blk:
                @blk.sync
                def _(eng):
                    if ftg > 0:
                        pr.wge(eng, "v_wc", 2 * ftg)
                    if ft == 0 and r > 0:
                        pr.wge(eng, "v_wdc", 22 * r)
                    pr.inc("d_we", eng.dma_start(
                        wstg[:].rearrange("p (kc f) -> p kc f", kc=16),
                        g_slab(ft).rearrange("(kc kp) f -> kp kc f", kp=128)), 16)
                    pr.wge(eng, "v_wc", 2 * ftg + 1)
                    pr.inc("d_we", eng.dma_start(
                        wstg[:].rearrange("p (kc f) -> p kc f", kc=16),
                        u_slab(ft).rearrange("(kc kp) f -> kp kc f", kp=128)), 16)

                @blk.vector
                def _(eng):
                    pr.wge(eng, "d_we", 32 * ftg + 16)
                    pr.wge(eng, "pe_egu", ftg)
                    pr.inc("v_wc", eng.tensor_copy(wgb[:], wstg[:]))
                    pr.wge(eng, "d_we", 32 * ftg + 32)
                    pr.inc("v_wc", eng.tensor_copy(wub[:], wstg[:]))
                    pr.wge(eng, "pe_egu", ftg + 1)
                    pr.wge(eng, "s_sil", ftg + 1)
                    if ft == 0 and r > 0:
                        pr.wge(eng, "pe_ed", 4 * r)
                    pr.inc("v_h", eng.tensor_mul(h[:, ft * TOK:(ft + 1) * TOK], hg[:], ps_u[:]))

                @blk.scalar
                def _(eng):
                    pr.wge(eng, "pe_eg", ftg + 1)
                    pr.wge(eng, "v_h", ftg)
                    pr.inc("s_sil", eng.activation(hg[:], ps_g[:], AF.Silu))

                @blk.tensor
                def _(eng):
                    pr.wge(eng, "v_wc", 2 * ftg + 1)
                    if ftg == 0:
                        pr.wait(eng, "ln2dn")
                        pr.wge(eng, "v_r", 16)
                    pr.wge(eng, "s_sil", ftg)
                    pr.wge(eng, "v_h", ftg)
                    inst = None
                    for tc in range(2):
                        ts = slice(tc * 512, (tc + 1) * 512)
                        for kc in range(16):
                            inst = eng.matmul(ps_g[:, ts], wgb[:, kc * 128:(kc + 1) * 128],
                                              xln[:, kc * TOK + tc * 512: kc * TOK + (tc + 1) * 512],
                                              start=(kc == 0), stop=(kc == 15))
                    pr.inc("pe_eg", inst)
                    pr.wge(eng, "v_wc", 2 * ftg + 2)
                    for tc in range(2):
                        ts = slice(tc * 512, (tc + 1) * 512)
                        for kc in range(16):
                            inst = eng.matmul(ps_u[:, ts], wub[:, kc * 128:(kc + 1) * 128],
                                              xln[:, kc * TOK + tc * 512: kc * TOK + (tc + 1) * 512],
                                              start=(kc == 0), stop=(kc == 15))
                    pr.inc("pe_egu", inst)

        for dh in range(2):
            with nc.Block() as blk:
                @blk.sync
                def _(eng):
                    for fc in range(NFT):
                        fcg = 22 * r + dh * NFT + fc
                        if fcg > 0:
                            pr.wge(eng, "v_wdc", fcg)
                        if fc == 0 and dh == 0:
                            pr.wge(eng, "v_wc", 22 * (r + 1))
                        pr.inc("d_wd2", eng.dma_start(wstg[:], d_slab(fc)), 16)

                @blk.vector
                def _(eng):
                    for fc in range(NFT):
                        fcg = 22 * r + dh * NFT + fc
                        pr.wge(eng, "d_wd2", 16 * (fcg + 1))
                        if fc == 0:
                            pr.wge(eng, "pe_ed", 4 * r + 2 * dh)
                        pr.inc("v_wdc", eng.tensor_copy(
                            wd_blk[:, fc * 1024:(fc + 1) * 1024],
                            wstg[:, dh * 1024:(dh + 1) * 1024]))

            for dgl in range(2):
                dg = 2 * dh + dgl
                with nc.Block() as blk:
                    @blk.tensor
                    def _(eng):
                        pr.wge(eng, "v_wdc", 22 * r + NFT * (dh + 1))
                        pr.wge(eng, "v_h", NFT * (r + 1))
                        pr.wge(eng, "v_ed", 16 * r + 4 * dg)
                        if is_exp and dg == 3:
                            pr.wge(eng, "v_cb", r + 1)
                        inst = None
                        for dl in range(4):
                            dt = dg * 4 + dl
                            dloc = dt - 8 * dh
                            psd = pool[:, dl * TOK:(dl + 1) * TOK]
                            for fc in range(NFT):
                                for tc in range(2):
                                    ts = slice(tc * 512, (tc + 1) * 512)
                                    inst = eng.matmul(
                                        psd[:, ts],
                                        wd_blk[:, fc * 1024 + dloc * 128: fc * 1024 + (dloc + 1) * 128],
                                        h[:, fc * TOK + tc * 512: fc * TOK + (tc + 1) * 512],
                                        start=(fc == 0), stop=(fc == NFT - 1))
                        pr.inc("pe_ed", inst)

                    @blk.vector
                    def _(eng):
                        pr.wge(eng, "pe_ed", 4 * r + dg + 1)
                        if is_exp:
                            pr.wge(eng, "v_cb", r + 1)
                        for dl in range(4):
                            dt = dg * 4 + dl
                            psd = pool[:, dl * TOK:(dl + 1) * TOK]
                            if is_exp:
                                pr.wge(eng, "v_ed", 16 * r + 4 * dg + dl)
                                pr.chain(eng, "vq", eng.tensor_mul(tmpb[:], psd, combB[:]))
                                pr.inc("v_ed", eng.tensor_add(xres[:, dt * TOK:(dt + 1) * TOK],
                                                              xres[:, dt * TOK:(dt + 1) * TOK], tmpb[:]))
                            else:
                                pr.inc("v_ed", eng.scalar_tensor_tensor(
                                    xres[:, dt * TOK:(dt + 1) * TOK], psd, 1.0,
                                    xres[:, dt * TOK:(dt + 1) * TOK], ALU.mult, ALU.add))

    with nc.Block() as blk:
        @blk.sync
        def _(eng):
            pr.wait(eng, "v_ed")
            for j in range(NF):
                pr.inc("d_out", eng.dma_start(xo[j * 128:(j + 1) * 128, :],
                                              xres[:, j * TOK:(j + 1) * TOK]), 16)
            pr.wait(eng, "d_out")
    ar.free()


def _l0a_maps(inputs, x):
    winT = np.ascontiguousarray(np.asarray(inputs["L0_attn_in_proj_w"], np.float32).T)
    conv_w = np.asarray(inputs["L0_attn_conv_w"], np.float32)
    cwf = np.ascontiguousarray(conv_w.reshape(32, 128, 4).transpose(1, 0, 2).reshape(128, 128))
    conv_b = np.asarray(inputs["L0_attn_conv_b"], np.float32)
    cbf = np.ascontiguousarray(conv_b.reshape(32, 128).T)
    wxT = np.ascontiguousarray(np.asarray(inputs["L0_attn_x_proj_w"], np.float32).T)
    maps = []
    for c in range(NCORES):
        seq, half = c // 2, c % 2
        t0 = half * TOK
        if half == 0:
            xe = np.concatenate([np.zeros((3, D), np.float32), x[seq, 0:TOK]], axis=0)
        else:
            xe = x[seq, t0 - 3:t0 + TOK]
        xfh = np.ascontiguousarray(xe.T)
        maps.append(dict(xf=xfh, winT=winT, cwf=cwf, cbf=cbf, wxT=wxT,
                         onesf=_ONES, cstf=_CST))
    return maps


def _l0b_maps(inputs, xf_percore):
    dtwT = np.ascontiguousarray(np.asarray(inputs["L0_attn_dt_proj_w"], np.float32).T)
    dtb = np.ascontiguousarray(np.asarray(inputs["L0_attn_dt_proj_b"], np.float32).reshape(64, 1))
    alog = np.ascontiguousarray(np.asarray(inputs["L0_attn_A_log"], np.float32).reshape(64, 1))
    Dv = np.asarray(inputs["L0_attn_D"], np.float32)
    ddc = np.zeros((128, 32), np.float32)
    for J in range(32):
        ddc[0:64, J] = Dv[2 * J]
        ddc[64:128, J] = Dv[2 * J + 1]
    woutT = np.ascontiguousarray(np.asarray(inputs["L0_attn_out_proj_w"], np.float32).T)
    repa = np.ascontiguousarray(np.kron(np.eye(64), np.ones((1, 64))).astype(BF))
    trilq = np.triu(np.ones((64, 64), np.float32))
    tril2 = np.ascontiguousarray(np.tile(trilq, (1, 64)).astype(BF))
    repi = np.ascontiguousarray(np.tile(np.eye(64, dtype=np.float32), (1, 64)).astype(BF))
    repd = np.zeros((64, 4096), BF)
    for J in range(32):
        repd[2 * J, J * 128:J * 128 + 64] = 1
        repd[2 * J + 1, J * 128 + 64:J * 128 + 128] = 1
    trius = np.ascontiguousarray(np.tril(np.ones((64, 64), np.float32), -1).astype(BF))
    maskls = np.ascontiguousarray(np.tril(np.ones((64, 64), np.float32)).astype(BF))
    maps = []
    for c in range(NCORES):
        selv = np.zeros((128, 8), np.float32)
        if c % 2 == 1:
            selv[:, c - 1] = 1.0
        maps.append(dict(xf=xf_percore[c], dtwT=dtwT, dtb=dtb, alog=alog, ddf=ddc,
                         woutT=woutT, selvf=selv, onesf=_ONES, repaf=repa, trilf=tril2,
                         repif=repi, repdf=repd, triusf=trius, maskls=maskls, cstf=_CST))
    return maps


def _moe_maps(inputs, xf_percore):
    gw = np.asarray(inputs["L1_ffn_gate_w"], np.float32)
    gwr = np.ascontiguousarray(gw.reshape(16, 128, 8).transpose(1, 0, 2).reshape(128, 128))
    weg = np.ascontiguousarray(np.asarray(inputs["L1_ffn_w_gate"], np.float32))
    weu = np.ascontiguousarray(np.asarray(inputs["L1_ffn_w_up"], np.float32))
    wed = np.ascontiguousarray(np.asarray(inputs["L1_ffn_w_down"], np.float32))
    swg = np.ascontiguousarray(np.asarray(inputs["L1_ffn_sw_gate"], np.float32))
    swu = np.ascontiguousarray(np.asarray(inputs["L1_ffn_sw_up"], np.float32))
    swd = np.ascontiguousarray(np.asarray(inputs["L1_ffn_sw_down"], np.float32))
    return [dict(xf=xf_percore[c], gwr=gwr, weg=weg, weu=weu, wed=wed,
                 swg=swg, swu=swu, swd=swd, onesf=_ONES, identf=_IDENT, cstf=_CST)
            for c in range(NCORES)]


TOKH = 1027


def emit_l0a(pr, io):
    nc = pr.nc
    pr.prefix = "a_"
    ar = nc.auto_arena()
    for _n in ['d_c', 'd_lx0', 'd_lx1', 'd_ly0', 'd_ly1', 'd_wi', 'd_wx', 'd_wz', 'd_xo0', 'd_xo1', 'd_zo0', 'd_zo1', 'lnA_pe', 'lnA_sq', 'lnC_dv', 'ln_bc', 'ln_dv', 'ln_rt', 'pe_ip', 'pe_xp', 'pe_z', 's_u', 's_z', 'v_cv', 'v_wc', 'v_wx', 'v_xe', 'vq']:
        pr.sem(_n)
    xf = io["xf"]
    winT = io["winT"]
    cwf = io["cwf"]
    cbf = io["cbf"]
    wxT = io["wxT"]
    onesf = io["onesf"]
    cstf = io["cstf"]
    xfm = io["xfm"]
    bcdt = io["bcdt"]
    zsf = io["zsf"]

    xln = ar.sbuf_tensor("a_xln", [128, NF * TOKH], BF16)
    u = ar.sbuf_tensor("a_u", [128, 32 * TOK], BF16)
    xstg = ar.sbuf_tensor("a_xstg", [128, 2 * TOKH], F32)
    sq = ar.sbuf_tensor("a_sq", [128, 2 * TOKH], F32)
    row_m = ar.sbuf_tensor("a_row_m", [1, TOKH], F32)
    row_v = ar.sbuf_tensor("a_row_v", [1, TOKH], F32)
    ones = ar.sbuf_tensor("a_ones", [128, 128], F32)
    cst = ar.sbuf_tensor("a_cst", [128, 8], F32)
    cw = ar.sbuf_tensor("a_cw", [128, 128], F32)
    cb = ar.sbuf_tensor("a_cb", [128, 32], F32)
    wstg = ar.sbuf_tensor("a_wstg", [128, 2048], F32)
    w_b = ar.sbuf_tensor("a_w_b", [128, 4096], BF16)
    xb = ar.sbuf_tensor("a_xb", [128, TOKH], F32)
    acc1 = ar.sbuf_tensor("a_acc1", [128, TOK], F32)
    acc2 = ar.sbuf_tensor("a_acc2", [128, TOK], F32)
    ostg = ar.sbuf_tensor("a_ostg", [128, 2 * TOK], BF16)

    ar.allocate()
    pool = io["pool"]
    ps_sum = pool[0:1, 0:TOKH]
    ps_sq = pool[0:1, 1536:1536 + TOKH]
    ps_bcm = pool[:, 0:TOKH]
    ps_bcr = pool[:, 1536:1536 + TOKH]
    ps_m = pool[:, 0:TOKH]
    ps_x = pool[:, 0:TOK]

    with nc.Block() as blk:
        @blk.sync
        def _(eng):
            pr.inc("d_c", eng.dma_start(ones[:], onesf), 16)
            pr.inc("d_c", eng.dma_start(cst[:], cstf), 16)
            pr.inc("d_c", eng.dma_start(cw[:], cwf), 16)
            pr.inc("d_c", eng.dma_start(cb[:], cbf), 16)

    with nc.Block() as blk:
        @blk.sync
        def _(eng):
            for j in range(NF):
                if j >= 2:
                    pr.wge(eng, "lnA_pe", j - 1)
                pr.inc("d_lx%d" % (j % 2), eng.dma_start(
                    xstg[:, (j % 2) * TOKH:(j % 2 + 1) * TOKH],
                    xf[j * 128:(j + 1) * 128, :]), 16)

        @blk.scalar
        def _(eng):
            for j in range(NF):
                pr.wge(eng, "d_lx%d" % (j % 2), 16 * (j // 2 + 1))
                if j >= 2:
                    pr.wge(eng, "lnA_pe", j - 1)
                pr.inc("lnA_sq", eng.activation(sq[:, (j % 2) * TOKH:(j % 2 + 1) * TOKH],
                                                xstg[:, (j % 2) * TOKH:(j % 2 + 1) * TOKH],
                                                AF.Square))

        @blk.tensor
        def _(eng):
            pr.wait(eng, "d_c")
            for j in range(NF):
                pr.wge(eng, "d_lx%d" % (j % 2), 16 * (j // 2 + 1))
                xt = xstg[:, (j % 2) * TOKH:(j % 2 + 1) * TOKH]
                for c in range(3):
                    c0, c1 = c * 512, min((c + 1) * 512, TOKH)
                    eng.matmul(ps_sum[0:1, c0:c1], ones[:, 0:1], xt[:, c0:c1],
                               start=(j == 0), stop=(j == NF - 1))
                pr.wge(eng, "lnA_sq", j + 1)
                sqt = sq[:, (j % 2) * TOKH:(j % 2 + 1) * TOKH]
                inst = None
                for c in range(3):
                    c0, c1 = c * 512, min((c + 1) * 512, TOKH)
                    inst = eng.matmul(ps_sq[0:1, c0:c1], ones[:, 0:1], sqt[:, c0:c1],
                                      start=(j == 0), stop=(j == NF - 1))
                pr.inc("lnA_pe", inst)

    with nc.Block() as blk:
        @blk.vector
        def _(eng):
            pr.wge(eng, "lnA_pe", NF)
            pr.chain(eng, "vq", eng.tensor_scalar(row_m[0:1, :], ps_sum[0:1, :], 1.0 / D, None, ALU.mult))
            pr.chain(eng, "vq", eng.tensor_scalar(row_v[0:1, :], ps_sq[0:1, :], 1.0 / D, None, ALU.mult))
            pr.chain(eng, "vq", eng.tensor_mul(sq[0:1, 0:TOKH], row_m[0:1, :], row_m[0:1, :]))
            pr.inc("ln_dv", eng.tensor_sub(row_v[0:1, :], row_v[0:1, :], sq[0:1, 0:TOKH]))

        @blk.scalar
        def _(eng):
            pr.wait(eng, "ln_dv")
            pr.wait(eng, "d_c")
            pr.inc("ln_rt", eng.activation(row_v[0:1, :], row_v[0:1, :], AF.Sqrt,
                                           bias=cst[0:1, 0:1]))

    with nc.Block() as blk:
        @blk.vector
        def _(eng):
            pr.wait(eng, "ln_rt")
            pr.inc("ln_rt", eng.reciprocal(row_v[0:1, :], row_v[0:1, :]))

        @blk.tensor
        def _(eng):
            pr.wge(eng, "ln_rt", 2)
            inst = None
            for c in range(3):
                c0, c1 = c * 512, min((c + 1) * 512, TOKH)
                eng.matmul(ps_bcm[:, c0:c1], ones[0:1, :], row_m[0:1, c0:c1],
                           start=True, stop=True)
                inst = eng.matmul(ps_bcr[:, c0:c1], ones[0:1, :],
                                  row_v[0:1, c0:c1], start=True, stop=True)
            pr.inc("ln_bc", inst)

    with nc.Block() as blk:
        @blk.sync
        def _(eng):
            for j in range(NF):
                if j >= 2:
                    pr.wge(eng, "lnC_dv", j - 1)
                pr.inc("d_ly%d" % (j % 2), eng.dma_start(
                    xstg[:, (j % 2) * TOKH:(j % 2 + 1) * TOKH],
                    xf[j * 128:(j + 1) * 128, :]), 16)

        @blk.vector
        def _(eng):
            pr.wait(eng, "ln_bc")
            for j in range(NF):
                pr.wge(eng, "d_ly%d" % (j % 2), 16 * (j // 2 + 1))
                tmp = sq[:, (j % 2) * TOKH:(j % 2 + 1) * TOKH]
                pr.chain(eng, "vq", eng.tensor_sub(tmp, xstg[:, (j % 2) * TOKH:(j % 2 + 1) * TOKH],
                                                   ps_bcm[:]))
                pr.inc("lnC_dv", eng.tensor_mul(xln[:, j * TOKH:(j + 1) * TOKH], tmp,
                                                ps_bcr[:]))

    for j in range(32):
        with nc.Block() as blk:
            @blk.sync
            def _(eng):
                if j > 0:
                    pr.wge(eng, "v_wc", j)
                pr.inc("d_wi", eng.dma_start(
                    wstg[:].rearrange("p (kc f) -> p kc f", kc=16),
                    winT[:, j * 128:(j + 1) * 128].rearrange("(kc kp) f -> kp kc f", kp=128)), 16)

            @blk.vector
            def _(eng):
                pr.wge(eng, "d_wi", 16 * (j + 1))
                if j > 0:
                    pr.wge(eng, "pe_ip", j)
                pr.inc("v_wc", eng.tensor_copy(w_b[:, 0:2048], wstg[:]))
                if j == 0:
                    pr.wait(eng, "d_c")
                pr.wge(eng, "pe_ip", j + 1)
                pr.chain(eng, "vq", eng.tensor_copy(xb[:], ps_m[:]))
                pr.chain(eng, "vq", eng.tensor_scalar(acc1[:], xb[:, 0:TOK], cw[:, 4 * (j % 32):4 * (j % 32) + 1], None, ALU.mult))
                pr.chain(eng, "vq", eng.scalar_tensor_tensor(acc2[:], xb[:, 1:1 + TOK], cw[:, 4 * (j % 32) + 1:4 * (j % 32) + 2], acc1[:], ALU.mult, ALU.add))
                pr.chain(eng, "vq", eng.scalar_tensor_tensor(acc1[:], xb[:, 2:2 + TOK], cw[:, 4 * (j % 32) + 2:4 * (j % 32) + 3], acc2[:], ALU.mult, ALU.add))
                pr.inc("v_cv", eng.scalar_tensor_tensor(acc2[:], xb[:, 3:3 + TOK], cw[:, 4 * (j % 32) + 3:4 * (j % 32) + 4], acc1[:], ALU.mult, ALU.add))

            @blk.scalar
            def _(eng):
                if j == 0:
                    pr.wait(eng, "d_c")
                pr.wge(eng, "v_cv", j + 1)
                pr.inc("s_u", eng.activation(u[:, j * TOK:(j + 1) * TOK], acc2[:], AF.Silu,
                                             bias=cb[:, j:j + 1]))

            @blk.tensor
            def _(eng):
                pr.wge(eng, "v_wc", j + 1)
                if j == 0:
                    pr.wge(eng, "lnC_dv", NF)
                pr.wge(eng, "v_cv", j)
                inst = None
                for c in range(3):
                    c0, c1 = c * 512, min((c + 1) * 512, TOKH)
                    for kc in range(16):
                        inst = eng.matmul(ps_m[:, c0:c1], w_b[:, kc * 128:(kc + 1) * 128],
                                          xln[:, kc * TOKH + c0: kc * TOKH + c1],
                                          start=(kc == 0), stop=(kc == 15))
                pr.inc("pe_ip", inst)

    NXF = 35
    for xfc in range(NXF):
        fw = 128 if xfc < 34 else 64
        with nc.Block() as blk:
            @blk.sync
            def _(eng):
                for piece in range(2):
                    if xfc + piece > 0:
                        pr.wge(eng, "v_wx", 2 * xfc + piece)
                    pr.inc("d_wx", eng.dma_start(
                        wstg[:, 0:16 * fw].rearrange("p (kc f) -> p kc f", kc=16),
                        wxT[piece * 2048:(piece + 1) * 2048,
                            xfc * 128:xfc * 128 + fw].rearrange("(kc kp) f -> kp kc f", kp=128)), 16)

            @blk.vector
            def _(eng):
                for piece in range(2):
                    pr.wge(eng, "d_wx", 16 * (2 * xfc + piece + 1))
                    if xfc > 0 and piece == 0:
                        pr.wge(eng, "pe_xp", xfc)
                    pr.inc("v_wx", eng.tensor_copy(
                        w_b[:, piece * 2048:piece * 2048 + 16 * fw], wstg[:, 0:16 * fw]))
                pr.wge(eng, "pe_xp", xfc + 1)
                if xfc >= 2:
                    pr.wge(eng, "d_xo%d" % (xfc % 2), 16 * ((xfc - 2) // 2 + 1))
                pr.inc("v_xe", eng.tensor_copy(ostg[0:fw, (xfc % 2) * TOK:(xfc % 2 + 1) * TOK],
                                               ps_x[0:fw, :]))

            @blk.tensor
            def _(eng):
                pr.wge(eng, "v_wx", 2 * (xfc + 1))
                if xfc == 0:
                    pr.wge(eng, "s_u", 32)
                pr.wge(eng, "v_xe", xfc)
                inst = None
                for tc in range(2):
                    ts = slice(tc * 512, (tc + 1) * 512)
                    for kc in range(32):
                        inst = eng.matmul(ps_x[0:fw, ts],
                                          w_b[:, (kc // 16) * 2048 + (kc % 16) * fw:
                                              (kc // 16) * 2048 + (kc % 16) * fw + fw],
                                          u[:, kc * TOK + tc * 512: kc * TOK + (tc + 1) * 512],
                                          start=(kc == 0), stop=(kc == 31))
                pr.inc("pe_xp", inst)

            @blk.gpsimd
            def _(eng):
                pr.wge(eng, "v_xe", xfc + 1)
                src = ostg[0:fw, (xfc % 2) * TOK:(xfc % 2) * TOK + TOK]
                if xfc < 32:
                    dst = xfm[xfc * 128:(xfc + 1) * 128, :]
                elif xfc < 34:
                    dst = bcdt[(xfc - 32) * 128:(xfc - 31) * 128, :]
                else:
                    dst = bcdt[256:320, :]
                pr.inc("d_xo%d" % (xfc % 2), eng.dma_start(dst, src), 16)

    for j in range(32):
        with nc.Block() as blk:
            @blk.sync
            def _(eng):
                pr.wge(eng, "v_wc", 32 + j)
                if j == 0:
                    pr.wge(eng, "v_wx", 70)
                pr.inc("d_wz", eng.dma_start(
                    wstg[:].rearrange("p (kc f) -> p kc f", kc=16),
                    winT[:, 4096 + j * 128:4096 + (j + 1) * 128].rearrange("(kc kp) f -> kp kc f", kp=128)), 16)

            @blk.vector
            def _(eng):
                pr.wge(eng, "d_wz", 16 * (j + 1))
                if j > 0:
                    pr.wge(eng, "pe_z", j)
                pr.inc("v_wc", eng.tensor_copy(w_b[:, 0:2048], wstg[:]))

            @blk.scalar
            def _(eng):
                pr.wge(eng, "pe_z", j + 1)
                if j == 0:
                    pr.wge(eng, "d_xo0", pr.val("d_xo0"))
                    pr.wge(eng, "d_xo1", pr.val("d_xo1"))
                if j >= 2:
                    pr.wge(eng, "d_zo%d" % (j % 2), 16 * ((j - 2) // 2 + 1))
                pr.inc("s_z", eng.activation(ostg[:, (j % 2) * TOK:(j % 2 + 1) * TOK],
                                             ps_x[:], AF.Silu))

            @blk.tensor
            def _(eng):
                pr.wge(eng, "v_wc", 33 + j)
                pr.wge(eng, "s_z", j)
                if j == 0:
                    pr.wge(eng, "v_xe", NXF)
                inst = None
                for tc in range(2):
                    ts = slice(tc * 512, (tc + 1) * 512)
                    for kc in range(16):
                        inst = eng.matmul(ps_x[:, ts], w_b[:, kc * 128:(kc + 1) * 128],
                                          xln[:, kc * TOKH + 3 + tc * 512: kc * TOKH + 3 + (tc + 1) * 512],
                                          start=(kc == 0), stop=(kc == 15))
                pr.inc("pe_z", inst)

            @blk.gpsimd
            def _(eng):
                pr.wge(eng, "s_z", j + 1)
                pr.inc("d_zo%d" % (j % 2), eng.dma_start(zsf[j * 128:(j + 1) * 128, :],
                                             ostg[:, (j % 2) * TOK:(j % 2 + 1) * TOK]), 16)

    with nc.Block() as blk:
        @blk.sync
        def _(eng):
            pr.wait(eng, "d_xo0")
            pr.wait(eng, "d_xo1")
            pr.wait(eng, "d_zo0")
            pr.wait(eng, "d_zo1")
    ar.free()


def emit_l0b(pr, io):
    nc = pr.nc
    pr.prefix = "b_"
    ar = nc.auto_arena()
    for _n in ['cc', 'd_blo', 'd_bz', 'd_c', 'd_lam', 'd_or0', 'd_or1', 'd_sg', 'd_sh', 'd_wo', 'd_xl0', 'd_xl1', 'd_xlo', 'd_xo0', 'd_xo1', 'd_xr0', 'd_xr1', 'd_xz', 'd_zo0', 'd_zo1', 'd_zs0', 'd_zs1', 'p_db', 'p_dbl', 'p_dx', 'p_dxl', 'p_s0', 'p_sc', 'p_st', 'p_w', 'p_xw', 'pe_a', 'pe_b', 'pe_do', 'pe_ds', 'pe_lam', 'pe_op', 'pe_pp', 'pe_rs', 'pe_s1', 'pe_st', 'pe_w', 'pe_yd', 'pe_yo', 'pe_z', 's_ge', 's_pp', 's_z', 'sq2', 'v_do', 'v_gf', 'v_lam', 'v_lc', 'v_oe', 'v_pp', 'v_ra', 'v_sc', 'v_sel', 'v_sm', 'v_wo', 'v_xd', 'v_xw', 'v_ye', 'v_yg', 'vq']:
        pr.sem(_n)
    xf = io["xf"]
    xfm = io["xfm"]
    bcdt = io["bcdt"]
    zsf = io["zsf"]
    dtwT = io["dtwT"]
    dtb = io["dtb"]
    alog = io["alog"]
    ddf = io["ddf"]
    woutT = io["woutT"]
    onesf = io["onesf"]
    repaf = io["repaf"]
    trilf = io["trilf"]
    repif = io["repif"]
    repdf = io["repdf"]
    triusf = io["triusf"]
    maskls = io["maskls"]
    cstf = io["cstf"]
    xo = io["xo"]

    Y = ar.sbuf_tensor("b_Y", [128, 32 * 512], BF16)
    S_run = ar.sbuf_tensor("b_S_run", [128, 4096], F32)
    Lam = ar.sbuf_tensor("b_Lam", [128, 1024], F32)
    dt = ar.sbuf_tensor("b_dt", [64, 1024], F32)
    scr1 = ar.sbuf_tensor("b_scr1", [128, 1024], F32)
    scr2 = ar.sbuf_tensor("b_scr2", [128, 1024], F32)
    vbuf = scr1[0:64, :]
    t1 = scr2[0:64, :]
    cs = ar.sbuf_tensor("b_cs", [64, 1024], F32)
    zeros64 = ar.sbuf_tensor("b_zeros64", [64, 64], F32)
    A_bf = ar.sbuf_tensor("b_A_bf", [64, 1024], BF16)
    dt_bf = ar.sbuf_tensor("b_dt_bf", [64, 1024], BF16)
    w_bf = ar.sbuf_tensor("b_w_bf", [64, 1024], BF16)
    decay_bf = ar.sbuf_tensor("b_decay_bf", [64, 1024], BF16)
    dtraw = ar.sbuf_tensor("b_dtraw", [64, 1024], BF16)
    lam = ar.sbuf_tensor("b_lam", [64, 16], F32)
    B_fm = ar.sbuf_tensor("b_B_fm", [128, 1024], BF16)
    C_fm = ar.sbuf_tensor("b_C_fm", [128, 1024], BF16)
    C_f32 = ar.sbuf_tensor("b_C_f32", [128, 1024], F32)
    Bz = ar.sbuf_tensor("b_Bz", [128, 128], BF16)
    Gbuf = ar.sbuf_tensor("b_Gbuf", [128, 4096], BF16)
    Gtmp = ar.sbuf_tensor("b_Gtmp", [128, 4096], BF16)
    rhsA = ar.sbuf_tensor("b_rhsA", [128, 4096], BF16)
    Xz = ar.sbuf_tensor("b_Xz", [128, 4096], BF16)
    Xlo = ar.sbuf_tensor("b_Xlo", [64, 4096], BF16)
    Bzl = ar.sbuf_tensor("b_Bzl", [64, 128], BF16)
    Xdt = ar.sbuf_tensor("b_Xdt", [128, 4096], BF16)
    Xw = ar.sbuf_tensor("b_Xw", [128, 4096], BF16)
    S_m = ar.sbuf_tensor("b_S_m", [128, 64], BF16)
    dorep = ar.sbuf_tensor("b_dorep", [128, 2048], BF16)
    ycat = ar.sbuf_tensor("b_ycat", [128, 1024], BF16)
    tmpv = ar.sbuf_tensor("b_tmpv", [128, 1024], F32)
    lamr = tmpv[0:1, 0:1024]
    repa = ar.sbuf_tensor("b_repa", [64, 4096], BF16)
    tril = ar.sbuf_tensor("b_tril", [64, 4096], BF16)
    repi = ar.sbuf_tensor("b_repi", [64, 4096], BF16)
    repd = ar.sbuf_tensor("b_repd", [64, 4096], BF16)
    trius = ar.sbuf_tensor("b_trius", [64, 64], BF16)
    maskl = ar.sbuf_tensor("b_maskl", [64, 64], BF16)
    ddc = ar.sbuf_tensor("b_ddc", [128, 32], F32)
    ones = ar.sbuf_tensor("b_ones", [128, 128], F32)
    cst = ar.sbuf_tensor("b_cst", [128, 8], F32)
    dtw_b = ar.sbuf_tensor("b_dtw_b", [64, 64], BF16)
    dtb_c = ar.sbuf_tensor("b_dtb_c", [64, 1], F32)
    alog_c = ar.sbuf_tensor("b_alog_c", [64, 1], F32)
    Acol = ar.sbuf_tensor("b_Acol", [64, 1], F32)
    selv = ar.sbuf_tensor("b_selv", [128, 8], F32)
    zstg = ar.sbuf_tensor("b_zstg", [128, 2 * 512], BF16)
    xflstg = ar.sbuf_tensor("b_xflstg", [128, 2 * 512], BF16)
    wstg = ar.sbuf_tensor("b_wstg", [128, 2048], F32)
    wout_b = ar.sbuf_tensor("b_wout_b", [128, 4096], BF16)
    xstg = scr1
    ostg = scr2

    ar.allocate()
    pool = io["pool"]
    psYd = pool[:, 0:1024]
    psYo = pool[:, 1024:2048]
    psSt = pool[:, 2048:4096]
    psDo = pool[:, 0:2048]
    psOP = pool[:, 0:512]

    L64 = pool[0:64, :]

    with nc.Block() as blk:
        @blk.sync
        def _(eng):
            for name, dst, srcp in [("c", ones, onesf), ("c", cst, cstf), ("c", repa, repaf),
                                    ("c", tril, trilf), ("c", repi, repif), ("c", repd, repdf),
                                    ("c", trius, triusf), ("c", maskl, maskls), ("c", ddc, ddf),
                                    ("c", dtb_c, dtb), ("c", alog_c, alog)]:
                pr.inc("d_c", eng.dma_start(dst[:], srcp), 16)
            pr.inc("d_c", eng.dma_start(wstg[0:64, 0:64], dtwT), 16)
            pr.inc("d_c", eng.dma_start(selv[:], io["selvf"]), 16)
            pr.inc("d_c", eng.dma_start(dtraw[:], bcdt[0:64, 0:1024]), 16)
            pr.inc("d_c", eng.dma_start(B_fm[:], bcdt[64:192, 0:1024]), 16)
            pr.inc("d_c", eng.dma_start(C_fm[:], bcdt[192:320, 0:1024]), 16)

    with nc.Block() as blk:
        @blk.vector
        def _(eng):
            pr.wait(eng, "d_c")
            pr.chain(eng, "vq", eng.memset(zeros64[:], 0.0))
            pr.chain(eng, "vq", eng.tensor_copy(C_f32[:], C_fm[:]))
            pr.inc("v_pp", eng.tensor_copy(dtw_b[:], wstg[0:64, 0:64]))

        @blk.tensor
        def _(eng):
            pr.wait(eng, "v_pp")
            inst = None
            for tc in range(2):
                inst = eng.matmul(pool[0:64, tc * 512:(tc + 1) * 512], dtw_b[:],
                                  dtraw[:, tc * 512:(tc + 1) * 512], start=True, stop=True)
            pr.inc("pe_pp", inst)

    with nc.Block() as blk:
        @blk.vector
        def _(eng):
            pr.wait(eng, "pe_pp")
            pr.inc("v_pp", eng.tensor_scalar(vbuf, pool[0:64, 0:1024], dtb_c[:, 0:1],
                                             None, ALU.add))

        @blk.scalar
        def _(eng):
            pr.wge(eng, "v_pp", 2)
            pr.chain(eng, "sq2", eng.activation(t1, vbuf, AF.Abs))
            pr.chain(eng, "sq2", eng.activation(t1, t1, AF.Exp, scale=-1.0))
            pr.inc("s_pp", eng.activation(t1, t1, AF.Ln, bias=1.0))
            pr.inc("s_pp", eng.activation(Acol[:], alog_c[:], AF.Exp))

    with nc.Block() as blk:
        @blk.vector
        def _(eng):
            pr.wge(eng, "s_pp", 2)
            pr.chain(eng, "vq", eng.scalar_tensor_tensor(dt[:], vbuf, 0.0, t1,
                                                         ALU.max, ALU.add))
            pr.chain(eng, "vq", eng.tensor_scalar(Acol[:], Acol[:], -1.0, None, ALU.mult))
            pr.chain(eng, "vq", eng.tensor_scalar(vbuf, dt[:], Acol[:, 0:1], None, ALU.mult))
            pr.chain(eng, "vq", eng.tensor_copy(A_bf[:], vbuf))
            pr.chain(eng, "vq", eng.tensor_copy(dt_bf[:], dt[:]))
            for c in range(16):
                sl = slice(c * 64, (c + 1) * 64)
                pr.chain(eng, "vq", eng.tensor_tensor_scan(cs[:, sl], vbuf[:, sl],
                                                           zeros64[:], 0.0, ALU.add, ALU.add))
            for c in range(16):
                sl = slice(c * 64, (c + 1) * 64)
                pr.chain(eng, "vq", eng.tensor_scalar(t1[:, sl], cs[:, sl],
                                                      cs[:, c * 64 + 63:c * 64 + 64], -1.0,
                                                      ALU.subtract, ALU.mult))
            for c in range(16):
                pr.inc("v_lc", eng.tensor_copy(lam[:, c:c + 1], cs[:, c * 64 + 63:c * 64 + 64]))
            pr.inc("v_pp", eng.tensor_copy(A_bf[:, 0:1], A_bf[:, 0:1]))

        @blk.scalar
        def _(eng):
            pr.wge(eng, "v_pp", 3)
            pr.chain(eng, "sq2", eng.activation(t1, t1, AF.Exp))
            pr.inc("s_pp", eng.activation(decay_bf[:], cs[:], AF.Exp))

    with nc.Block() as blk:
        @blk.scalar
        def _(eng):
            pr.wge(eng, "v_lc", 16)
            pr.inc("s_pp", eng.activation(lam[:], lam[:], AF.Exp))

        @blk.vector
        def _(eng):
            pr.wge(eng, "s_pp", 4)
            pr.inc("v_pp", eng.tensor_mul(w_bf[:], t1, dt[:]))

    with nc.Block() as blk:
        @blk.gpsimd
        def _(eng):
            pr.wge(eng, "s_pp", 4)
            pr.inc("d_lam", eng.dma_start(lamr[:], lam[:]), 16)

        @blk.tensor
        def _(eng):
            pr.wait(eng, "d_lam")
            inst = None
            for tc in range(2):
                inst = eng.matmul(pool[:, tc * 512:(tc + 1) * 512], ones[0:1, :],
                                  lamr[0:1, tc * 512:(tc + 1) * 512], start=True, stop=True)
            pr.inc("pe_lam", inst)

    with nc.Block() as blk:
        @blk.vector
        def _(eng):
            pr.wait(eng, "pe_lam")
            pr.inc("v_lam", eng.tensor_copy(Lam[:], pool[:, 0:1024]))

    with nc.Block() as blk:
        @blk.vector
        def _(eng):
            pr.wait(eng, "v_lam")
            pr.inc("p_s0", eng.memset(S_run[:], 0.0))
    for z in range(16):
        ev = z % 2 == 0
        Xc = Xz if ev else Xlo
        Bc = Bz if ev else Bzl
        with nc.Block() as blk:
            @blk.sync
            def _(eng):
                if ev:
                    pr.wge(eng, "p_xw", z)
                    pr.inc("p_dx", eng.dma_start(Xz[:], xfm[:, z * 64:z * 64 + 128],
                                                 transpose=True), 16)
                    pr.wge(eng, "p_st", 2 * z)
                    pr.inc("p_db", eng.dma_start(Bz[:], bcdt[64:192, z * 64:z * 64 + 128],
                                                 transpose=True), 16)

            @blk.gpsimd
            def _(eng):
                if not ev:
                    pr.wge(eng, "p_dx", 16 * (z // 2 + 1))
                    pr.wge(eng, "p_xw", z)
                    pr.inc("p_dxl", eng.dma_start(Xlo[:], Xz[64:128, :]), 16)
                    pr.wge(eng, "p_db", 16 * (z // 2 + 1))
                    pr.wge(eng, "p_st", 2 * z)
                    pr.inc("p_dbl", eng.dma_start(Bzl[:], Bz[64:128, :]), 16)

            @blk.tensor
            def _(eng):
                if z == 0:
                    pr.wge(eng, "v_pp", 4)
                    pr.wait(eng, "v_lam")
                    pr.wait(eng, "d_c")
                pr.wge(eng, "p_xw", z)
                pr.wge(eng, "p_sc", 64 * z)
                inst = None
                for mc in range(8):
                    ms = slice(mc * 512, (mc + 1) * 512)
                    inst = eng.matmul(L64[:, ms], w_bf[:, z * 64:(z + 1) * 64],
                                      repa[:, ms], start=True, stop=True)
                pr.inc("p_w", inst)
                for hf in range(2):
                    pr.wge(eng, "p_xw", z + 1)
                    if ev:
                        pr.wge(eng, "p_db", 16 * (z // 2 + 1))
                    else:
                        pr.wge(eng, "p_dbl", 16 * (z // 2 + 1))
                    pr.wge(eng, "p_sc", 64 * z + 32 * hf)
                    inst = None
                    for hl in range(32):
                        h = hf * 32 + hl
                        inst = eng.matmul(psSt[:, hl * 64:(hl + 1) * 64], Bc[0:64, :],
                                          Xw[0:64, h * 64:(h + 1) * 64],
                                          start=True, stop=True)
                    pr.inc("p_st", inst)

            @blk.vector
            def _(eng):
                pr.wge(eng, "p_w", z + 1)
                if ev:
                    pr.wge(eng, "p_dx", 16 * (z // 2 + 1))
                else:
                    pr.wge(eng, "p_dxl", 16 * (z // 2 + 1))
                pr.wge(eng, "p_st", 2 * z)
                if z == 0:
                    pr.wait(eng, "p_s0")
                pr.inc("p_xw", eng.tensor_mul(Xw[0:64, :], Xc[0:64, :], L64[:, 0:4096]))
                for hf in range(2):
                    pr.wge(eng, "p_st", 2 * z + hf + 1)
                    for hl in range(32):
                        h = hf * 32 + hl
                        pr.inc("p_sc", eng.scalar_tensor_tensor(
                            S_run[:, h * 64:(h + 1) * 64], S_run[:, h * 64:(h + 1) * 64],
                            Lam[:, h * 16 + z:h * 16 + z + 1],
                            psSt[:, hl * 64:(hl + 1) * 64], ALU.mult, ALU.add))

    with nc.Block() as blk:
        @blk.sync
        def _(eng):
            pr.wge(eng, "p_sc", 64 * 16)
            pr.inc("d_sh", eng.dma_start(io["shd"], S_run[:]), 16)

        @blk.gpsimd
        def _(eng):
            pr.wait(eng, "d_sh")
            pr.inc("cc", eng.collective_compute(
                "AllGather", ALU.bypass, [[0, 1, 2, 3, 4, 5, 6, 7]],
                [io["shd"].opt()], [io["gatd"].opt()]))

    with nc.Block() as blk:
        @blk.gpsimd
        def _(eng):
            for r in range(8):
                for q in range(4):
                    pr.wge(eng, "v_sel", 4 * r + q + 1)
                    pr.inc("d_sg", eng.dma_start(
                        scr1[:], io["gatd"][r * 128:(r + 1) * 128, q * 1024:(q + 1) * 1024]), 16)

        @blk.vector
        def _(eng):
            pr.wait(eng, "cc")
            pr.inc("v_sel", eng.memset(S_run[:], 0.0))
            for r in range(8):
                for q in range(4):
                    pr.wge(eng, "d_sg", 16 * (4 * r + q + 1))
                    pr.inc("v_sel", eng.scalar_tensor_tensor(
                        S_run[:, q * 1024:(q + 1) * 1024], scr1[:], selv[:, r:r + 1],
                        S_run[:, q * 1024:(q + 1) * 1024], ALU.mult, ALU.add))

    for z in range(16):
        th, zl = z // 8, z % 8
        ev = (z % 2 == 0)
        Xc = Xz if ev else Xlo
        Bc = Bz if ev else Bzl
        with nc.Block() as blk:
            @blk.sync
            def _(eng):
                if ev:
                    if z == 0:
                        pr.wge(eng, "p_xw", 16)
                        pr.wge(eng, "p_st", 32)
                        pr.wge(eng, "v_sel", 33)
                    pr.wge(eng, "v_xw", z)
                    pr.inc("d_xz", eng.dma_start(Xz[:], xfm[:, z * 64:z * 64 + 128],
                                                 transpose=True), 16)
                    pr.wge(eng, "pe_st", 2 * z)
                    pr.inc("d_bz", eng.dma_start(Bz[:], bcdt[64:192, z * 64:z * 64 + 128],
                                                 transpose=True), 16)

            @blk.gpsimd
            def _(eng):
                if not ev:
                    pr.wge(eng, "d_xz", 16 * (z // 2 + 1))
                    pr.wge(eng, "v_xw", z)
                    pr.inc("d_xlo", eng.dma_start(Xlo[:], Xz[64:128, :]), 16)
                    pr.wge(eng, "d_bz", 16 * (z // 2 + 1))
                    pr.wge(eng, "pe_st", 2 * z)
                    pr.inc("d_blo", eng.dma_start(Bzl[:], Bz[64:128, :]), 16)

            @blk.tensor
            def _(eng):
                if z == 0:
                    pr.wge(eng, "v_pp", 4)
                    pr.wait(eng, "v_lam")
                    pr.wait(eng, "d_c")
                    pr.wge(eng, "v_sel", 33)
                    pr.wge(eng, "p_xw", 16)
                    pr.wge(eng, "p_sc", 1024)
                if z == 8:
                    pr.wge(eng, "v_oe", 16)
                pr.wge(eng, "v_ye", 2 * z)
                pr.wge(eng, "v_xw", z)
                inst = None
                for J in range(32):
                    inst = eng.matmul(psDo[:, J * 64:(J + 1) * 64],
                                      repd[:, J * 128:(J + 1) * 128],
                                      decay_bf[:, z * 64:(z + 1) * 64], start=True, stop=True)
                pr.inc("pe_do", inst)
                pr.wge(eng, "v_do", z + 1)
                pr.wge(eng, "v_sc", 64 * z)
                inst = None
                for mc in range(8):
                    ms = slice(mc * 512, (mc + 1) * 512)
                    inst = eng.matmul(L64[:, ms], A_bf[:, z * 64:(z + 1) * 64],
                                      repa[:, ms], start=True, stop=True)
                pr.inc("pe_a", inst)
                pr.wge(eng, "v_ra", z + 1)
                inst = None
                for mc in range(8):
                    ms = slice(mc * 512, (mc + 1) * 512)
                    inst = eng.matmul(L64[:, ms], dt_bf[:, z * 64:(z + 1) * 64],
                                      repa[:, ms], start=True, stop=True)
                pr.inc("pe_b", inst)
                pr.wge(eng, "v_xd", z + 1)
                inst = None
                for mc in range(8):
                    ms = slice(mc * 512, (mc + 1) * 512)
                    inst = eng.matmul(L64[:, ms], trius[:], rhsA[0:64, ms],
                                      start=True, stop=True)
                pr.inc("pe_ds", inst)
                pr.wge(eng, "s_ge", z + 1)
                pr.inc("pe_s1", eng.matmul(L64[:, 0:64], C_fm[:, z * 64:(z + 1) * 64],
                                           B_fm[:, z * 64:(z + 1) * 64], start=True, stop=True))
                pr.wge(eng, "v_sm", z + 1)
                inst = None
                for mc in range(8):
                    ms = slice(mc * 512, (mc + 1) * 512)
                    inst = eng.matmul(L64[:, ms], S_m[0:64, :], repi[:, ms],
                                      start=True, stop=True)
                pr.inc("pe_rs", inst)
                pr.wge(eng, "v_gf", z + 1)
                inst = None
                for mc in range(8):
                    ms = slice(mc * 512, (mc + 1) * 512)
                    inst = eng.matmul(L64[:, ms], w_bf[:, z * 64:(z + 1) * 64],
                                      repa[:, ms], start=True, stop=True)
                pr.inc("pe_w", inst)
                for hf in range(2):
                    pr.wge(eng, "v_xw", z + 1)
                    pr.wge(eng, "v_do", z + 1)
                    pr.wge(eng, "v_ye", 2 * z + hf)
                    pr.wge(eng, "v_sc", 64 * z)
                    if zl == 0 and hf == 0 and th == 1:
                        pr.wge(eng, "pe_op", 16)
                        pr.wge(eng, "v_yg", 32)
                    inst = None
                    for j in range(16):
                        J = hf * 16 + j
                        inst = eng.matmul(psYd[0:64, j * 64:(j + 1) * 64],
                                          Xdt[0:64, (2 * J) * 64:(2 * J + 1) * 64],
                                          Gtmp[0:64, (2 * J) * 64:(2 * J + 1) * 64],
                                          start=True, stop=True)
                        inst = eng.matmul(psYd[64:128, j * 64:(j + 1) * 64],
                                          Xdt[0:64, (2 * J + 1) * 64:(2 * J + 2) * 64],
                                          Gtmp[0:64, (2 * J + 1) * 64:(2 * J + 2) * 64],
                                          start=True, stop=True, tile_position=(0, 64))
                    pr.inc("pe_yd", inst)
                    inst = None
                    for j in range(16):
                        J = hf * 16 + j
                        inst = eng.matmul(psYo[:, j * 64:(j + 1) * 64],
                                          S_run[:, J * 128:(J + 1) * 128],
                                          C_f32[:, z * 64:(z + 1) * 64], start=True, stop=True)
                    pr.inc("pe_yo", inst)
                    if ev:
                        pr.wge(eng, "d_bz", 16 * (z // 2 + 1))
                    else:
                        pr.wge(eng, "d_blo", 16 * (z // 2 + 1))
                    pr.wge(eng, "v_sc", 64 * z + 32 * hf)
                    inst = None
                    for hl in range(32):
                        h = hf * 32 + hl
                        inst = eng.matmul(psSt[:, hl * 64:(hl + 1) * 64], Bc[0:64, :],
                                          Xw[0:64, h * 64:(h + 1) * 64],
                                          start=True, stop=True)
                    pr.inc("pe_st", inst)

            @blk.scalar
            def _(eng):
                pr.wge(eng, "pe_ds", z + 1)
                pr.wge(eng, "v_gf", max(0, z - 1))
                pr.inc("s_ge", eng.activation(Gbuf[0:64, :], L64[:, 0:4096], AF.Exp))

            @blk.vector
            def _(eng):
                pr.wge(eng, "pe_do", z + 1)
                pr.inc("v_do", eng.tensor_copy(dorep[:], psDo[:]))
                pr.wge(eng, "pe_a", z + 1)
                pr.wge(eng, "pe_ds", max(0, z - 1) if z > 0 else 0)
                pr.inc("v_ra", eng.tensor_mul(rhsA[0:64, :], L64[:, 0:4096], tril[:]))
                pr.wge(eng, "pe_b", z + 1)
                if ev:
                    pr.wge(eng, "d_xz", 16 * (z // 2 + 1))
                else:
                    pr.wge(eng, "d_xlo", 16 * (z // 2 + 1))
                pr.wge(eng, "pe_yd", max(0, 2 * (z - 1)))
                pr.inc("v_xd", eng.tensor_mul(Xdt[0:64, :], Xc[0:64, :], L64[:, 0:4096]))
                pr.wge(eng, "pe_s1", z + 1)
                pr.wge(eng, "pe_rs", max(0, z - 1))
                pr.inc("v_sm", eng.tensor_mul(S_m[0:64, :], L64[:, 0:64], maskl[:]))
                pr.wge(eng, "pe_rs", z + 1)
                pr.wge(eng, "pe_yd", max(0, 2 * (z - 1)))
                pr.inc("v_gf", eng.tensor_mul(Gtmp[0:64, :], Gbuf[0:64, :], L64[:, 0:4096]))
                pr.wge(eng, "pe_w", z + 1)
                pr.wge(eng, "pe_st", max(0, 2 * (z - 1)))
                pr.inc("v_xw", eng.tensor_mul(Xw[0:64, :], Xc[0:64, :], L64[:, 0:4096]))
                for hf in range(2):
                    pr.wge(eng, "pe_yd", 2 * z + hf + 1)
                    pr.wge(eng, "pe_yo", 2 * z + hf + 1)
                    pr.chain(eng, "vq", eng.tensor_mul(tmpv[:], psYo[:],
                                                       dorep[:, hf * 1024:(hf + 1) * 1024]))
                    pr.chain(eng, "vq", eng.tensor_add(ycat[:], tmpv[:], psYd[:]))
                    inst = None
                    for j in range(16):
                        J = hf * 16 + j
                        inst = eng.tensor_copy(Y[:, J * 512 + zl * 64:J * 512 + (zl + 1) * 64],
                                               ycat[:, j * 64:(j + 1) * 64])
                    pr.inc("v_ye", inst)
                    pr.wge(eng, "pe_st", 2 * z + hf + 1)
                    for hl in range(32):
                        h = hf * 32 + hl
                        pr.inc("v_sc", eng.scalar_tensor_tensor(
                            S_run[:, h * 64:(h + 1) * 64], S_run[:, h * 64:(h + 1) * 64],
                            Lam[:, h * 16 + z:h * 16 + z + 1],
                            psSt[:, hl * 64:(hl + 1) * 64], ALU.mult, ALU.add))

        if zl == 7:
            for J in range(32):
                with nc.Block() as blk:
                    @blk.gpsimd
                    def _(eng):
                        if th > 0 or J >= 2:
                            pr.wge(eng, "v_yg", (32 * th + J - 1) if J >= 2 else 32 * th)
                        pr.inc("d_zs%d" % (J % 2), eng.dma_start(
                            zstg[:, (J % 2) * 512:(J % 2 + 1) * 512],
                            zsf[J * 128:(J + 1) * 128, th * 512:(th + 1) * 512]), 16)
                        pr.inc("d_xl%d" % (J % 2), eng.dma_start(
                            xflstg[:, (J % 2) * 512:(J % 2 + 1) * 512],
                            xfm[J * 128:(J + 1) * 128, th * 512:(th + 1) * 512]), 16)

                    @blk.vector
                    def _(eng):
                        pr.wge(eng, "d_zs%d" % (J % 2), 16 * (th * 16 + J // 2 + 1))
                        pr.wge(eng, "d_xl%d" % (J % 2), 16 * (th * 16 + J // 2 + 1))
                        pr.wge(eng, "v_ye", 16 * (th + 1))
                        pr.chain(eng, "vq", eng.scalar_tensor_tensor(
                            Y[:, J * 512:(J + 1) * 512],
                            xflstg[:, (J % 2) * 512:(J % 2 + 1) * 512],
                            ddc[:, J:J + 1], Y[:, J * 512:(J + 1) * 512],
                            ALU.mult, ALU.add))
                        pr.inc("v_yg", eng.tensor_mul(Y[:, J * 512:(J + 1) * 512],
                                                      Y[:, J * 512:(J + 1) * 512],
                                                      zstg[:, (J % 2) * 512:(J % 2 + 1) * 512]))
            for od in range(16):
                odg = th * 16 + od
                with nc.Block() as blk:
                    @blk.sync
                    def _(eng):
                        for piece in range(2):
                            if odg + piece > 0:
                                pr.wge(eng, "v_wo", 2 * odg + piece)
                            pr.inc("d_wo", eng.dma_start(
                                wstg[:].rearrange("p (kc f) -> p kc f", kc=16),
                                woutT[piece * 2048:(piece + 1) * 2048,
                                      od * 128:(od + 1) * 128].rearrange(
                                          "(kc kp) f -> kp kc f", kp=128)), 16)

                    @blk.gpsimd
                    def _(eng):
                        pr.wge(eng, "v_oe", odg)
                        if odg == 0:
                            pr.wge(eng, "v_sel", 33)
                        pr.inc("d_xr%d" % (od % 2), eng.dma_start(
                            xstg[:, (od % 2) * 512:(od % 2 + 1) * 512],
                            xf[od * 128:(od + 1) * 128, th * 512:(th + 1) * 512]), 16)
                        if odg > 0:
                            po = (odg - 1) % 16
                            pt = (odg - 1) // 16
                            pr.inc("d_or%d" % (po % 2), eng.dma_start(
                                xo[po * 128:(po + 1) * 128, pt * 512:(pt + 1) * 512],
                                ostg[:, (po % 2) * 512:(po % 2 + 1) * 512]), 16)

                    @blk.vector
                    def _(eng):
                        for piece in range(2):
                            pr.wge(eng, "d_wo", 16 * (2 * odg + piece + 1))
                            if piece == 0 and odg > 0:
                                pr.wge(eng, "pe_op", odg)
                            pr.inc("v_wo", eng.tensor_copy(
                                wout_b[:, piece * 2048:(piece + 1) * 2048], wstg[:]))
                        pr.wge(eng, "pe_op", odg + 1)
                        pr.wge(eng, "d_xr%d" % (od % 2), 16 * (th * 8 + od // 2 + 1))
                        if odg >= 2:
                            pr.wge(eng, "d_or%d" % (od % 2), 16 * (th * 8 + od // 2))
                        pr.inc("v_oe", eng.tensor_add(
                            ostg[:, (od % 2) * 512:(od % 2 + 1) * 512],
                            xstg[:, (od % 2) * 512:(od % 2 + 1) * 512], psOP[:]))

                    @blk.tensor
                    def _(eng):
                        pr.wge(eng, "v_wo", 2 * odg + 2)
                        if od == 0:
                            pr.wge(eng, "v_yg", 32 * (th + 1))
                            pr.wge(eng, "v_sc", 64 * 8 * (th + 1))
                        pr.wge(eng, "v_oe", odg)
                        inst = None
                        for fc in range(32):
                            inst = eng.matmul(psOP[:],
                                              wout_b[:, fc * 128:(fc + 1) * 128],
                                              Y[:, fc * 512:(fc + 1) * 512],
                                              start=(fc == 0), stop=(fc == 31))
                        pr.inc("pe_op", inst)

    with nc.Block() as blk:
        @blk.gpsimd
        def _(eng):
            pr.wait(eng, "v_oe")
            pr.inc("d_or1", eng.dma_start(xo[15 * 128:16 * 128, 512:1024],
                                          ostg[:, 512:1024]), 16)

        @blk.sync
        def _(eng):
            pr.wait(eng, "d_or0")
            pr.wait(eng, "d_or1")
    ar.free()


def _shard_x(x):
    outs = []
    for c in range(NCORES):
        seq, half = c // 2, c % 2
        t0 = half * TOK
        blk = np.ascontiguousarray(x[seq, t0:t0 + TOK, :].T)
        if half == 0:
            halo = np.zeros((D, 3), np.float32)
        else:
            halo = np.ascontiguousarray(x[seq, t0 - 3:t0, :].T)
        outs.append((blk, halo))
    return outs


_ONES = np.ones((128, 128), np.float32)
_CST = np.zeros((128, 8), np.float32)
_CST[:, 0] = LN_EPS


def _ffn_maps(inputs, xf_percore):
    wgT = np.ascontiguousarray(np.asarray(inputs["L0_ffn_gate_w"], np.int32).T)
    wuT = np.ascontiguousarray(np.asarray(inputs["L0_ffn_up_w"], np.int32).T)
    wdT = np.ascontiguousarray(np.asarray(inputs["L0_ffn_down_w"], np.int32).T)
    gsc = np.ascontiguousarray(np.asarray(inputs["L0_ffn_gate_scale"], np.float32).reshape(64, 128).T)
    usc = np.ascontiguousarray(np.asarray(inputs["L0_ffn_up_scale"], np.float32).reshape(64, 128).T)
    dsc = np.ascontiguousarray(np.asarray(inputs["L0_ffn_down_scale"], np.float32).reshape(16, 128).T)
    return [dict(xf=xf_percore[c], wgT=wgT, wuT=wuT, wdT=wdT,
                 gsc=gsc, usc=usc, dsc=dsc, onesf=_ONES, cstf=_CST) for c in range(NCORES)]


_ONESB = np.ones((128, 128), BF)
_IDENT = np.eye(128, dtype=np.float32)


def _attn_maps(inputs, xf_percore):
    kv_k = np.asarray(inputs["L1_attn_kv_k"], np.float32).reshape(4096, 16, 128)
    kv_v = np.asarray(inputs["L1_attn_kv_v"], np.float32).reshape(4096, 16, 128)
    kTf = np.ascontiguousarray(kv_k.transpose(1, 2, 0))
    vTf = np.ascontiguousarray(kv_v.transpose(1, 0, 2))
    wqT = np.ascontiguousarray(np.asarray(inputs["L1_attn_Wq"], np.float32).T)
    woT = np.ascontiguousarray(np.asarray(inputs["L1_attn_Wo"], np.float32).T)
    return [dict(xf=xf_percore[c], kTf=kTf, vTf=vTf, wqT=wqT, woT=woT,
                 onesf=_ONES, onesb=_ONESB, cstf=_CST) for c in range(NCORES)]


def build_fused():
    pr = Pr()
    nc = pr.nc
    F = F32
    pool = nc.alloc_psum_tensor("pool", [128, 4096], F32)
    fincell = nc.alloc_sbuf_tensor("fincell", [1, 2], F32)

    def ext(n, shp, dty):
        return nc.dram_tensor(n, shp, dty, kind="ExternalInput").ap()

    def internal(n, shp, dty):
        return nc.dram_tensor(n, shp, dty).ap()

    Da = dict(xf=ext("a_xf", [D, TOKH], F), winT=ext("a_winT", [D, 8192], F),
              cwf=ext("a_cwf", [128, 128], F), cbf=ext("a_cbf", [128, 32], F),
              wxT=ext("a_wxT", [4096, 4416], F), onesf=ext("a_onesf", [128, 128], F),
              cstf=ext("a_cstf", [128, 8], F))
    Db = dict(xf=ext("b_xf", [D, TOK], F), dtwT=ext("b_dtwT", [64, 64], F),
              dtb=ext("b_dtb", [64, 1], F), alog=ext("b_alog", [64, 1], F),
              ddf=ext("b_ddf", [128, 32], F), woutT=ext("b_woutT", [4096, D], F),
              selvf=ext("b_selvf", [128, 8], F), onesf=ext("b_onesf", [128, 128], F),
              repaf=ext("b_repaf", [64, 4096], BF16), trilf=ext("b_trilf", [64, 4096], BF16),
              repif=ext("b_repif", [64, 4096], BF16), repdf=ext("b_repdf", [64, 4096], BF16),
              triusf=ext("b_triusf", [64, 64], BF16), maskls=ext("b_maskls", [64, 64], BF16),
              cstf=ext("b_cstf", [128, 8], F))
    Df = dict(wgT=ext("f_wgT", [D, 8192], I32), wuT=ext("f_wuT", [D, 8192], I32),
              wdT=ext("f_wdT", [8192, D], I32), gsc=ext("f_gsc", [128, 64], F),
              usc=ext("f_usc", [128, 64], F), dsc=ext("f_dsc", [128, 16], F),
              onesf=ext("f_onesf", [128, 128], F), cstf=ext("f_cstf", [128, 8], F))
    Dt = dict(kTf=ext("t_kTf", [16, 128, 4096], F), vTf=ext("t_vTf", [16, 4096, 128], F),
              wqT=ext("t_wqT", [D, D], F), woT=ext("t_woT", [D, D], F),
              onesf=ext("t_onesf", [128, 128], F), onesb=ext("t_onesb", [128, 128], BF16),
              cstf=ext("t_cstf", [128, 8], F))
    Dm = dict(gwr=ext("m_gwr", [128, 128], F), weg=ext("m_weg", [8, D, 1408], F),
              weu=ext("m_weu", [8, D, 1408], F), wed=ext("m_wed", [8, 1408, D], F),
              swg=ext("m_swg", [D, 2816], F), swu=ext("m_swu", [D, 2816], F),
              swd=ext("m_swd", [2816, D], F), onesf=ext("m_onesf", [128, 128], F),
              identf=ext("m_identf", [128, 128], F), cstf=ext("m_cstf", [128, 8], F))
    out_xo = nc.dram_tensor("out_xo", [D, TOK], F32, kind="ExternalOutput").ap()

    xfm = internal("i_xfm", [4096, TOK], BF16)
    bcdt = internal("i_bcdt", [320, TOK], BF16)
    zsf = internal("i_zsf", [4096, TOK], BF16)
    x1 = internal("i_x1", [D, TOK], F32)
    x2 = internal("i_x2", [D, TOK], F32)
    x3 = internal("i_x3", [D, TOK], F32)
    shd = internal("i_shd", [128, 4096], F32)
    gatd = internal("i_gatd", [1024, 4096], F32)

    Da.update(xfm=xfm, bcdt=bcdt, zsf=zsf)
    Db.update(xfm=xfm, bcdt=bcdt, zsf=zsf, xo=x1, shd=shd, gatd=gatd)
    Df.update(xf=x1, xo=x2)
    Dt.update(xf=x2, xo=x3)
    Dm.update(xf=x3, xo=out_xo)
    for Dx in (Da, Db, Df, Dt, Dm):
        Dx["pool"] = pool

    with nc.Block() as blk:
        @blk.vector
        def _(eng):
            pr.prefix = ""
            pr.inc("fz", eng.memset(fincell[:], 0.0))
    with nc.Block() as blk:
        @blk.vector
        def _(eng):
            pr.wait(eng, "/fz")

    emit_l0a(pr, Da)
    pr.barrier(fincell)
    pr.release_prefix()
    emit_l0b(pr, Db)
    pr.barrier(fincell)
    pr.release_prefix()
    emit_l0ffn(pr, Df)
    pr.barrier(fincell)
    pr.release_prefix()
    emit_l1attn(pr, Dt)
    pr.barrier(fincell)
    pr.release_prefix()
    emit_l1ffn(pr, Dm)
    pr.barrier(fincell)
    build_fused.last_pr = pr
    return nc


def _fused_maps(inputs):
    x = np.asarray(inputs["L0_attn_x"], np.float32)
    shards = _shard_x(x)
    xf_percore = [s[0] for s in shards]
    maps_a = _l0a_maps(inputs, x)
    maps_b = _l0b_maps(inputs, xf_percore)
    maps_f = _ffn_maps(inputs, [None] * NCORES)
    maps_t = _attn_maps(inputs, [None] * NCORES)
    maps_m = _moe_maps(inputs, [None] * NCORES)
    in_maps = []
    for c in range(NCORES):
        m = {}
        for pfx, mm in (("a_", maps_a[c]), ("b_", maps_b[c]), ("f_", maps_f[c]),
                        ("t_", maps_t[c]), ("m_", maps_m[c])):
            for k, v in mm.items():
                if v is None or (k == "xf" and pfx in ("f_", "t_", "m_")):
                    continue
                m[pfx + k] = v
        in_maps.append(m)
    return in_maps


def kernel(**inputs):
    in_maps = _fused_maps(inputs)
    nc = build_fused()
    res = run_bass_kernel_spmd(nc, in_maps, core_ids=list(range(NCORES)))
    outs = [np.asarray(res.results[c]["out_xo"], np.float32).T for c in range(NCORES)]
    y = np.stack([np.concatenate([outs[2 * s], outs[2 * s + 1]], axis=0) for s in range(4)])
    return y.astype(np.float32)


# version 04
# speedup vs baseline: 1.3159x; 1.3159x over previous
import numpy as np
import ml_dtypes
import concourse.bass as bass
import concourse.mybir as mybir
from concourse.bass_utils import run_bass_kernel_spmd

F32 = mybir.dt.float32
BF16 = mybir.dt.bfloat16
I32 = mybir.dt.int32
AF = mybir.ActivationFunctionType
ALU = mybir.AluOpType
AX = mybir.AxisListType
BF = ml_dtypes.bfloat16

NCORES = 8
TOK = 1024
D = 2048
NF = D // 128
LN_EPS = 1e-5


class Pr:

    def __init__(self):
        self.nc = bass.Bass("TRN2", target_bir_lowering=False, debug=False)
        self.map = {}
        self.cnt = {}
        self.free = []
        self.prefix = ""

    def _resolve(self, name):
        return name[1:] if name.startswith("/") else self.prefix + name

    def _virgin(self, name):
        return ("_d_" in name or name.startswith("d_") or "_p_d" in name
                or name.endswith("cc"))

    def sem(self, name):
        name = self._resolve(name)
        if name not in self.map:
            if self.free and not self._virgin(name):
                h, base = self.free.pop()
            else:
                h, base = self.nc.alloc_semaphore(name), 0
            self.map[name] = (h, base)
            self.cnt[name] = 0
        return self.map[name][0]

    def wge(self, eng, name, v):
        h = self.sem(name)
        base = self.map[self._resolve(name)][1]
        eng.wait_ge(h, base + v)

    def inc(self, name, inst, n=1):
        inst.then_inc(self.sem(name), n)
        self.cnt[self._resolve(name)] += n
        return inst

    def wait(self, eng, name, val=None):
        n = self._resolve(name)
        self.sem(name)
        v = self.cnt[n] if val is None else val
        eng.wait_ge(self.map[n][0], self.map[n][1] + v)

    def val(self, name):
        self.sem(name)
        return self.cnt[self._resolve(name)]

    def chain(self, eng, name, inst):
        self.inc(name, inst)
        self.wait(eng, name)

    def release_prefix(self):
        pfx = self.prefix
        for n in [k for k in self.map if k.startswith(pfx) and pfx]:
            h, base = self.map.pop(n)
            c = self.cnt.pop(n)
            if not self._virgin(n):
                self.free.append((h, base + c))

    def barrier(self, fincell):
        nc = self.nc
        pfx = self.prefix
        names = [n for n, c in self.cnt.items() if n.startswith(pfx) and c > 0]
        with nc.Block() as blk:
            @blk.vector
            def _(eng):
                for n in names:
                    eng.wait_ge(self.map[n][0], self.map[n][1] + self.cnt[n])
                self.inc("/fin", eng.tensor_copy(fincell[0:1, 0:1], fincell[0:1, 1:2]))
        fh, fb = self.map["fin"]
        fv = fb + self.cnt["fin"]
        with nc.Block() as blk:
            @blk.sync
            def _(eng):
                eng.wait_ge(fh, fv)
            @blk.gpsimd
            def _(eng):
                eng.wait_ge(fh, fv)
            @blk.scalar
            def _(eng):
                eng.wait_ge(fh, fv)
            @blk.tensor
            def _(eng):
                eng.wait_ge(fh, fv)


def emit_ln(pr, tag, xres, xln, ones_f, sq, row_m, row_v, ps_sum, ps_sq, ps_bc,
            ncols, waits, eps_ap):
    nc = pr.nc
    nch = (ncols + 511) // 512
    s_sq = tag + "sq"
    s_pe = tag + "pe"
    s_st = tag + "st"
    s_dv = tag + "dv"
    s_rt = tag + "rt"
    s_bc = tag + "bc"
    s_dn = tag + "dn"
    b_sq = pr.val(s_sq)
    b_pe = pr.val(s_pe)

    with nc.Block() as blk:
        @blk.scalar
        def _(eng):
            for s, v in waits:
                pr.wge(eng, s, v)
            for j in range(NF):
                half = j % 2
                if j >= 2:
                    pr.wge(eng, s_pe, b_pe + j - 1)
                pr.inc(s_sq, eng.activation(sq[:, half * ncols:(half + 1) * ncols],
                                            xres[:, j * ncols:(j + 1) * ncols], AF.Square))

        @blk.tensor
        def _(eng):
            for s, v in waits:
                pr.wge(eng, s, v)
            for j in range(NF):
                half = j % 2
                xt = xres[:, j * ncols:(j + 1) * ncols]
                for c in range(nch):
                    c0, c1 = c * 512, min((c + 1) * 512, ncols)
                    eng.matmul(ps_sum[0:1, c0:c1], ones_f[:, 0:1], xt[:, c0:c1],
                               start=(j == 0), stop=(j == NF - 1))
                pr.wge(eng, s_sq, b_sq + j + 1)
                sqt = sq[:, half * ncols:(half + 1) * ncols]
                inst = None
                for c in range(nch):
                    c0, c1 = c * 512, min((c + 1) * 512, ncols)
                    inst = eng.matmul(ps_sq[0:1, c0:c1], ones_f[:, 0:1], sqt[:, c0:c1],
                                      start=(j == 0), stop=(j == NF - 1))
                pr.inc(s_pe, inst)

    with nc.Block() as blk:
        @blk.vector
        def _(eng):
            pr.wge(eng, s_pe, b_pe + NF)
            pr.chain(eng, "vq", eng.tensor_scalar(row_m[0:1, 0:ncols], ps_sum[0:1, 0:ncols], 1.0 / D, None, ALU.mult))
            pr.chain(eng, "vq", eng.tensor_scalar(row_v[0:1, 0:ncols], ps_sq[0:1, 0:ncols], 1.0 / D, None, ALU.mult))
            pr.chain(eng, "vq", eng.tensor_mul(sq[0:1, 0:ncols], row_m[0:1, 0:ncols], row_m[0:1, 0:ncols]))
            pr.inc(s_dv, eng.tensor_sub(row_v[0:1, 0:ncols], row_v[0:1, 0:ncols],
                                        sq[0:1, 0:ncols]))

        @blk.scalar
        def _(eng):
            pr.wait(eng, s_dv)
            pr.inc(s_rt, eng.activation(row_v[0:1, 0:ncols], row_v[0:1, 0:ncols],
                                        AF.Sqrt, bias=eps_ap))

    with nc.Block() as blk:
        @blk.vector
        def _(eng):
            pr.wait(eng, s_rt)
            pr.inc(s_rt, eng.reciprocal(row_v[0:1, 0:ncols], row_v[0:1, 0:ncols]))

        @blk.tensor
        def _(eng):
            pr.wait(eng, s_rt)
            inst = None
            for c in range(nch):
                c0, c1 = c * 512, min((c + 1) * 512, ncols)
                eng.matmul(ps_bc[:, c0:c1], ones_f[0:1, :], row_m[0:1, c0:c1],
                           start=True, stop=True)
                inst = eng.matmul(ps_bc[:, ncols + c0:ncols + c1], ones_f[0:1, :],
                                  row_v[0:1, c0:c1], start=True, stop=True)
            pr.inc(s_bc, inst)

    with nc.Block() as blk:
        @blk.vector
        def _(eng):
            pr.wait(eng, s_bc)
            inst = None
            for j in range(NF):
                xt = xres[:, j * ncols:(j + 1) * ncols]
                ot = xln[:, j * ncols:(j + 1) * ncols]
                tmp = sq[:, (j % 2) * ncols:((j % 2) + 1) * ncols]
                pr.chain(eng, "vq", eng.tensor_sub(tmp, xt, ps_bc[:, 0:ncols]))
                inst = eng.tensor_mul(ot, tmp, ps_bc[:, ncols:2 * ncols])
                if j + 2 < NF:
                    pr.chain(eng, "vq", inst)
            pr.inc(s_dn, inst)


def emit_l0ffn(pr, io):
    nc = pr.nc
    pr.prefix = "f_"
    ar = nc.auto_arena()
    for _n in ['act_g', 'd_c', 'd_out', 'd_wd', 'd_wg', 'd_wu', 'd_x', 'deq', 'deq_d', 'dve_d', 'dve_gu', 'ln0bc', 'ln0dn', 'ln0dv', 'ln0pe', 'ln0rt', 'ln0sq', 'ln0st', 'pe_d', 'pe_g', 'pe_gu', 'vq']:
        pr.sem(_n)
    xf = io["xf"]
    wgT = io["wgT"]
    wuT = io["wuT"]
    wdT = io["wdT"]
    gsc = io["gsc"]
    usc = io["usc"]
    dsc = io["dsc"]
    onesf = io["onesf"]
    cstf = io["cstf"]
    xo = io["xo"]

    xres = ar.sbuf_tensor("f_xres", [128, NF * TOK], F32)
    xln = ar.sbuf_tensor("f_xln", [128, NF * TOK], BF16)
    sq = ar.sbuf_tensor("f_sq", [128, 2 * TOK], F32)
    row_m = ar.sbuf_tensor("f_row_m", [1, TOK], F32)
    row_v = ar.sbuf_tensor("f_row_v", [1, TOK], F32)
    ones = ar.sbuf_tensor("f_ones", [128, 128], F32)
    cst = ar.sbuf_tensor("f_cst", [128, 8], F32)
    g_sc = ar.sbuf_tensor("f_g_sc", [128, 64], F32)
    u_sc = ar.sbuf_tensor("f_u_sc", [128, 64], F32)
    d_sc = ar.sbuf_tensor("f_d_sc", [128, 16], F32)
    FB = 1024
    NFB = 8192 // FB
    FT = FB // 128
    h = ar.sbuf_tensor("f_h", [128, FT * TOK], BF16)
    hg = ar.sbuf_tensor("f_hg", [128, TOK], BF16)
    wg_i = ar.sbuf_tensor("f_wg_i", [128, 2048], I32)
    wu_i = ar.sbuf_tensor("f_wu_i", [128, 2048], I32)
    wg_b = ar.sbuf_tensor("f_wg_b", [128, 2048], BF16)
    wu_b = ar.sbuf_tensor("f_wu_b", [128, 2048], BF16)
    wd_i = ar.sbuf_tensor("f_wd_i", [128, 2048], I32)
    wd_blk = ar.sbuf_tensor("f_wd_blk", [128, FT * 2048], BF16)

    ar.allocate()
    pool = io["pool"]
    ps_sum = pool[0:1, 0:TOK]
    ps_sq = pool[0:1, TOK:2 * TOK]
    ps_bc = pool[:, 2 * TOK:4 * TOK]
    ps_g = pool[:, 0:TOK]
    ps_u = pool[:, TOK:2 * TOK]

    with nc.Block() as blk:
        @blk.sync
        def _(eng):
            for j in range(NF):
                pr.inc("d_x", eng.dma_start(xres[:, j * TOK:(j + 1) * TOK],
                                            xf[j * 128:(j + 1) * 128, :]), 16)
            pr.inc("d_c", eng.dma_start(ones[:], onesf), 16)
            pr.inc("d_c", eng.dma_start(cst[:], cstf), 16)
            pr.inc("d_c", eng.dma_start(g_sc[:], gsc), 16)
            pr.inc("d_c", eng.dma_start(u_sc[:], usc), 16)
            pr.inc("d_c", eng.dma_start(d_sc[:], dsc), 16)

    emit_ln(pr, "ln0", xres[:], xln[:], ones[:], sq[:], row_m[:], row_v[:],
            ps_sum, ps_sq, ps_bc, TOK,
            [("d_x", pr.val("d_x")), ("d_c", pr.val("d_c"))], cst[0:1, 0:1])

    with nc.Block() as blk:
        @blk.sync
        def _(eng):
            for fb in range(NFB):
                for j in range(FT):
                    ft = fb * FT + j
                    if ft > 0:
                        pr.wge(eng, "pe_gu", ft)
                    pr.inc("d_wg", eng.dma_start(
                        wg_i[:].rearrange("p (kc f) -> p kc f", kc=16),
                        wgT[:, ft * 128:(ft + 1) * 128].rearrange("(kc kp) f -> kp kc f", kp=128)), 16)
                    pr.inc("d_wu", eng.dma_start(
                        wu_i[:].rearrange("p (kc f) -> p kc f", kc=16),
                        wuT[:, ft * 128:(ft + 1) * 128].rearrange("(kc kp) f -> kp kc f", kp=128)), 16)
                for fc in range(FT):
                    gfc = fb * FT + fc
                    if fb > 0 and fc == 0:
                        pr.wge(eng, "pe_d", 4 * fb)
                    if 8 * fb + fc > 0:
                        pr.wge(eng, "deq_d", 8 * fb + fc)
                    pr.inc("d_wd", eng.dma_start(wd_i[:], wdT[gfc * 128:(gfc + 1) * 128, :]), 16)

        @blk.scalar
        def _(eng):
            for fb in range(NFB):
                for j in range(FT):
                    ft = fb * FT + j
                    pr.wge(eng, "d_wg", 16 * (ft + 1))
                    pr.wge(eng, "d_wu", 16 * (ft + 1))
                    eng.activation(wg_b[:], wg_i[:], AF.Copy, bias=-128.0)
                    pr.inc("deq", eng.activation(wu_b[:], wu_i[:], AF.Copy, bias=-128.0))
                    pr.wge(eng, "pe_g", ft + 1)
                    pr.wge(eng, "dve_gu", ft)
                    pr.inc("act_g", eng.activation(hg[:], ps_g[:], AF.Silu,
                                                   scale=g_sc[:, ft:ft + 1]))
                for fc in range(FT):
                    pr.wge(eng, "d_wd", 16 * (8 * fb + fc + 1))
                    pr.inc("deq_d", eng.activation(wd_blk[:, fc * 2048:(fc + 1) * 2048],
                                                   wd_i[:], AF.Copy, bias=-128.0))

        @blk.tensor
        def _(eng):
            for fb in range(NFB):
                for j in range(FT):
                    ft = fb * FT + j
                    pr.wge(eng, "deq", ft + 1)
                    if ft == 0:
                        pr.wait(eng, "ln0dn")
                    if j == 0 and fb > 0:
                        pr.wge(eng, "dve_d", 16 * fb)
                    pr.wge(eng, "act_g", ft)
                    pr.wge(eng, "dve_gu", ft)
                    inst = None
                    for tc in range(2):
                        ts = slice(tc * 512, (tc + 1) * 512)
                        for kc in range(16):
                            inst = eng.matmul(ps_g[:, ts], wg_b[:, kc * 128:(kc + 1) * 128],
                                       xln[:, kc * TOK + tc * 512: kc * TOK + (tc + 1) * 512],
                                       start=(kc == 0), stop=(kc == 15))
                    pr.inc("pe_g", inst)
                    for tc in range(2):
                        ts = slice(tc * 512, (tc + 1) * 512)
                        for kc in range(16):
                            inst = eng.matmul(ps_u[:, ts], wu_b[:, kc * 128:(kc + 1) * 128],
                                       xln[:, kc * TOK + tc * 512: kc * TOK + (tc + 1) * 512],
                                       start=(kc == 0), stop=(kc == 15))
                    pr.inc("pe_gu", inst)
                for dg in range(4):
                    pr.wge(eng, "deq_d", 8 * (fb + 1))
                    if dg == 0:
                        pr.wge(eng, "dve_gu", 8 * (fb + 1))
                    if dg > 0:
                        pr.wge(eng, "dve_d", 16 * fb + 4 * dg)
                    inst = None
                    for dl in range(4):
                        dt = dg * 4 + dl
                        psd = pool[:, dl * TOK:(dl + 1) * TOK]
                        for fc in range(FT):
                            for tc in range(2):
                                ts = slice(tc * 512, (tc + 1) * 512)
                                inst = eng.matmul(psd[:, ts],
                                           wd_blk[:, fc * 2048 + dt * 128: fc * 2048 + (dt + 1) * 128],
                                           h[:, fc * TOK + tc * 512: fc * TOK + (tc + 1) * 512],
                                           start=(fc == 0), stop=(fc == FT - 1))
                    pr.inc("pe_d", inst)

        @blk.vector
        def _(eng):
            for fb in range(NFB):
                for j in range(FT):
                    ft = fb * FT + j
                    pr.wge(eng, "pe_gu", ft + 1)
                    pr.wge(eng, "act_g", ft + 1)
                    if j == 0:
                        pr.wge(eng, "pe_d", 4 * fb)
                    pr.inc("dve_gu", eng.scalar_tensor_tensor(
                        h[:, j * TOK:(j + 1) * TOK], ps_u[:],
                        u_sc[:, ft:ft + 1], hg[:], ALU.mult, ALU.mult))
                for dg in range(4):
                    pr.wge(eng, "pe_d", 4 * fb + dg + 1)
                    for dl in range(4):
                        dt = dg * 4 + dl
                        psd = pool[:, dl * TOK:(dl + 1) * TOK]
                        pr.inc("dve_d", eng.scalar_tensor_tensor(
                            xres[:, dt * TOK:(dt + 1) * TOK], psd,
                            d_sc[:, dt:dt + 1], xres[:, dt * TOK:(dt + 1) * TOK],
                            ALU.mult, ALU.add))

    with nc.Block() as blk:
        @blk.sync
        def _(eng):
            pr.wait(eng, "dve_d")
            for j in range(NF):
                pr.inc("d_out", eng.dma_start(xo[j * 128:(j + 1) * 128, :],
                                              xres[:, j * TOK:(j + 1) * TOK]), 16)
            pr.wait(eng, "d_out")
    ar.free()


SCALE_ATT = 0.08838834764831845


def emit_l1attn(pr, io):
    nc = pr.nc
    pr.prefix = "t_"
    ar = nc.auto_arena()
    for _n in ['d_c', 'd_kv', 'd_out', 'd_wo', 'd_wq', 'd_x', 'ln1bc', 'ln1dn', 'ln1dv', 'ln1pe', 'ln1rt', 'ln1sq', 'ln1st', 'pe_n', 'pe_nb', 'pe_pv', 'pe_qm', 'pe_rr', 'pe_s', 'pe_w', 's_exp', 's_kq', 's_rt2', 'v_ao', 'v_bb', 'v_kc', 'v_n', 'v_qf', 'v_rr', 'v_vc', 'v_wb', 'v_wo', 'vq']:
        pr.sem(_n)
    xf = io["xf"]
    kTf = io["kTf"]
    vTf = io["vTf"]
    wqT = io["wqT"]
    woT = io["woT"]
    onesf = io["onesf"]
    onesb = io["onesb"]
    cstf = io["cstf"]
    xo = io["xo"]

    xres = ar.sbuf_tensor("t_xres", [128, NF * TOK], F32)
    xab = ar.sbuf_tensor("t_xab", [128, NF * TOK], BF16)
    qf = ar.sbuf_tensor("t_qf", [128, NF * TOK], BF16)
    sq = ar.sbuf_tensor("t_sq", [128, 2 * TOK], F32)
    row_m = ar.sbuf_tensor("t_row_m", [1, TOK], F32)
    row_v = ar.sbuf_tensor("t_row_v", [1, TOK], F32)
    srow = ar.sbuf_tensor("t_srow", [1, TOK], F32)
    ones = ar.sbuf_tensor("t_ones", [128, 128], F32)
    onebv = ar.sbuf_tensor("t_onebv", [128, 128], BF16)
    cst = ar.sbuf_tensor("t_cst", [128, 8], F32)
    stg = ar.sbuf_tensor("t_stg", [128, 2048], F32)
    w_b = ar.sbuf_tensor("t_w_b", [128, 2048], BF16)
    kb = ar.sbuf_tensor("t_kb", [128, 4096], BF16)
    vb = ar.sbuf_tensor("t_vb", [128, 4096], BF16)
    ptq = ar.sbuf_tensor("t_ptq", [128, 8 * TOK], BF16)
    rrb = ar.sbuf_tensor("t_rrb", [128, TOK], F32)
    negb = ar.sbuf_tensor("t_negb", [128, 1], F32)

    ar.allocate()
    pool = io["pool"]
    ps_sum = pool[0:1, 0:TOK]
    ps_sq = pool[0:1, TOK:2 * TOK]
    ps_bc = pool[:, 2 * TOK:4 * TOK]
    ps_q = pool[:, 0:TOK]
    ps_o = pool[:, TOK:2 * TOK]
    ps_row = pool[0:1, 2 * TOK:3 * TOK]
    ps_n = pool[0:1, 3 * TOK:4 * TOK]
    ps_nb = pool[:, 3 * TOK:3 * TOK + 1]

    with nc.Block() as blk:
        @blk.sync
        def _(eng):
            for j in range(NF):
                pr.inc("d_x", eng.dma_start(xres[:, j * TOK:(j + 1) * TOK],
                                            xf[j * 128:(j + 1) * 128, :]), 16)
            pr.inc("d_c", eng.dma_start(ones[:], onesf), 16)
            pr.inc("d_c", eng.dma_start(onebv[:], onesb), 16)
            pr.inc("d_c", eng.dma_start(cst[:], cstf), 16)

    emit_ln(pr, "ln1", xres[:], xab[:], ones[:], sq[:], row_m[:], row_v[:],
            ps_sum, ps_sq, ps_bc, TOK,
            [("d_x", pr.val("d_x")), ("d_c", pr.val("d_c"))], cst[0:1, 0:1])

    with nc.Block() as blk:
        @blk.sync
        def _(eng):
            for od in range(16):
                if od > 0:
                    pr.wge(eng, "v_wb", od)
                pr.inc("d_wq", eng.dma_start(
                    stg[:].rearrange("p (kc f) -> p kc f", kc=16),
                    wqT[:, od * 128:(od + 1) * 128].rearrange("(kc kp) f -> kp kc f", kp=128)), 16)

        @blk.vector
        def _(eng):
            for od in range(16):
                pr.wge(eng, "d_wq", 16 * (od + 1))
                if od > 0:
                    pr.wge(eng, "pe_qm", od)
                pr.inc("v_wb", eng.tensor_copy(w_b[:], stg[:]))
                pr.wge(eng, "pe_qm", od + 1)
                pr.inc("v_qf", eng.tensor_copy(qf[:, od * TOK:(od + 1) * TOK], ps_q[:]))

        @blk.tensor
        def _(eng):
            for od in range(16):
                pr.wge(eng, "v_wb", od + 1)
                if od == 0:
                    pr.wait(eng, "ln1dn")
                pr.wge(eng, "v_qf", od)
                inst = None
                for tc in range(2):
                    ts = slice(tc * 512, (tc + 1) * 512)
                    for kc in range(16):
                        inst = eng.matmul(ps_q[:, ts], w_b[:, kc * 128:(kc + 1) * 128],
                                          xab[:, kc * TOK + tc * 512: kc * TOK + (tc + 1) * 512],
                                          start=(kc == 0), stop=(kc == 15))
                pr.inc("pe_qm", inst)

    with nc.Block() as blk:
        @blk.sync
        def _(eng):
            for h in range(16):
                if h == 0:
                    pr.wge(eng, "v_wb", 16)
                else:
                    pr.wge(eng, "v_vc", 2 * h)
                pr.inc("d_kv", eng.dma_start(stg[:], kTf[h, :, 0:2048]), 16)
                pr.wge(eng, "pe_n", 5 * h + 2)
                pr.inc("d_kv", eng.dma_start(stg[:], kTf[h, :, 2048:4096]), 16)
                pr.wge(eng, "pe_n", 5 * h + 4)
                pr.inc("d_kv", eng.dma_start(
                    stg[:].rearrange("p (c d) -> p c d", c=16),
                    vTf[h, 0:2048, :].rearrange("(c p) d -> p c d", p=128)), 16)
                pr.wge(eng, "v_vc", 2 * h + 1)
                pr.inc("d_kv", eng.dma_start(
                    stg[:].rearrange("p (c d) -> p c d", c=16),
                    vTf[h, 2048:4096, :].rearrange("(c p) d -> p c d", p=128)), 16)

        @blk.scalar
        def _(eng):
            for h in range(16):
                if h == 0:
                    pr.wait(eng, "ln1dn")
                for half in range(2):
                    pr.wge(eng, "v_kc", 2 * h + half + 1)
                    pr.inc("s_kq", eng.activation(stg[:], kb[:, half * 2048:(half + 1) * 2048],
                                                  AF.Square))
                pr.wge(eng, "pe_n", 5 * h)
                pr.wge(eng, "v_qf", h + 1)
                pr.inc("s_kq", eng.activation(sq[:, 0:TOK], qf[:, h * TOK:(h + 1) * TOK],
                                              AF.Square))
                pr.wge(eng, "v_n", 7 * h + 6)
                pr.inc("s_rt2", eng.activation(row_m[0:1, 3:4], row_m[0:1, 2:3], AF.Sqrt))
                for qt in range(4):
                    for lc in range(8):
                        ktc = qt * 8 + lc
                        pr.wge(eng, "pe_s", 32 * h + ktc + 1)
                        if ktc == 0:
                            pr.wge(eng, "v_bb", h + 1)
                        if qt > 0:
                            pr.wge(eng, "pe_pv", 32 * h + (qt - 1) * 8 + lc + 1)
                        pr.inc("s_exp", eng.activation(
                            ptq[:, lc * TOK:(lc + 1) * TOK], ps_q[:], AF.Exp,
                            bias=negb[:, 0:1]))

        @blk.tensor
        def _(eng):
            for h in range(16):
                for half in range(2):
                    pr.wge(eng, "s_kq", 3 * h + half + 1)
                    for c in range(2):
                        i = 2 * half + c
                        pr.wge(eng, "v_n", 7 * h + i)
                        inst = None
                        for sc in range(2):
                            cs = slice(c * 1024 + sc * 512, c * 1024 + (sc + 1) * 512)
                            os_ = slice(sc * 512, (sc + 1) * 512)
                            inst = eng.matmul(ps_n[0:1, os_], ones[:, 0:1], stg[:, cs],
                                              start=True, stop=True)
                        pr.inc("pe_n", inst)
                pr.wge(eng, "s_kq", 3 * h + 3)
                pr.wge(eng, "v_n", 7 * h + 4)
                inst = None
                for sc in range(2):
                    os_ = slice(sc * 512, (sc + 1) * 512)
                    inst = eng.matmul(ps_n[0:1, os_], ones[:, 0:1], sq[:, sc * 512:(sc + 1) * 512],
                                      start=True, stop=True)
                pr.inc("pe_n", inst)
                pr.wge(eng, "v_n", 7 * h + 7)
                pr.inc("pe_nb", eng.matmul(ps_nb[:, 0:1], ones[0:1, :], row_m[0:1, 4:5],
                                           start=True, stop=True))
                for ktc in range(32):
                    if ktc == 0:
                        pr.wge(eng, "v_kc", 2 * h + 2)
                        pr.wge(eng, "v_rr", 2 * h)
                        pr.wge(eng, "v_ao", h)
                        if h == 0:
                            pr.wge(eng, "v_qf", 16)
                    pr.wge(eng, "s_exp", 32 * h + ktc)
                    inst = None
                    for tc in range(2):
                        ts = slice(tc * 512, (tc + 1) * 512)
                        inst = eng.matmul(ps_q[:, ts], kb[:, ktc * 128:(ktc + 1) * 128],
                                          qf[:, h * TOK + tc * 512: h * TOK + (tc + 1) * 512],
                                          start=True, stop=True)
                    pr.inc("pe_s", inst)
                    lc = ktc % 8
                    pr.wge(eng, "s_exp", 32 * h + ktc + 1)
                    pr.wge(eng, "v_vc", 2 * h + (1 if ktc < 16 else 2))
                    for tc in range(2):
                        ts = slice(tc * 512, (tc + 1) * 512)
                        eng.matmul(ps_row[0:1, ts], onebv[:, 0:1],
                                   ptq[:, lc * TOK + tc * 512: lc * TOK + (tc + 1) * 512],
                                   start=(ktc == 0), stop=(ktc == 31))
                    inst = None
                    for tc in range(2):
                        ts = slice(tc * 512, (tc + 1) * 512)
                        inst = eng.matmul(ps_o[:, ts], vb[:, ktc * 128:(ktc + 1) * 128],
                                          ptq[:, lc * TOK + tc * 512: lc * TOK + (tc + 1) * 512],
                                          start=(ktc == 0), stop=(ktc == 31))
                    pr.inc("pe_pv", inst)
                pr.wge(eng, "v_rr", 2 * h + 1)
                pr.wge(eng, "s_exp", 32 * (h + 1))
                inst = None
                for sc in range(2):
                    os_ = slice(sc * 512, (sc + 1) * 512)
                    inst = eng.matmul(ps_q[:, os_], ones[0:1, :], srow[0:1, os_],
                                      start=True, stop=True)
                pr.inc("pe_rr", inst)

        @blk.vector
        def _(eng):
            for h in range(16):
                for half in range(2):
                    pr.wge(eng, "d_kv", 64 * h + 16 * (half + 1))
                    if h > 0 and half == 0:
                        pr.wge(eng, "pe_pv", 32 * h)
                    pr.inc("v_kc", eng.tensor_scalar(
                        kb[:, half * 2048:(half + 1) * 2048], stg[:], SCALE_ATT, None, ALU.mult))
                    for c in range(2):
                        i = 2 * half + c
                        pr.wge(eng, "pe_n", 5 * h + i + 1)
                        if i == 0:
                            pr.inc("v_n", eng.reduce_max(row_m[0:1, 0:1], ps_n[0:1, 0:TOK],
                                                         axis=AX.X))
                        else:
                            pr.chain(eng, "vq", eng.reduce_max(row_m[0:1, 5:6], ps_n[0:1, 0:TOK],
                                                               axis=AX.X))
                            pr.wge(eng, "v_n", 7 * h + i)
                            pr.inc("v_n", eng.tensor_max(row_m[0:1, 0:1], row_m[0:1, 0:1],
                                                         row_m[0:1, 5:6]))
                for half in range(2):
                    pr.wge(eng, "d_kv", 64 * h + 32 + 16 * (half + 1))
                    if h > 0 and half == 0:
                        pr.wge(eng, "pe_pv", 32 * h)
                    pr.inc("v_vc", eng.tensor_copy(vb[:, half * 2048:(half + 1) * 2048], stg[:]))
                pr.wge(eng, "pe_n", 5 * h + 5)
                pr.inc("v_n", eng.reduce_max(row_m[0:1, 1:2], ps_n[0:1, 0:TOK], axis=AX.X))
                pr.wge(eng, "v_n", 7 * h + 5)
                pr.inc("v_n", eng.tensor_mul(row_m[0:1, 2:3], row_m[0:1, 0:1], row_m[0:1, 1:2]))
                pr.wge(eng, "s_rt2", h + 1)
                pr.inc("v_n", eng.tensor_scalar(row_m[0:1, 4:5], row_m[0:1, 3:4], -1.0, None,
                                                ALU.mult))
                pr.wge(eng, "pe_nb", h + 1)
                pr.inc("v_bb", eng.tensor_copy(negb[:], ps_nb[:, 0:1]))
                pr.wge(eng, "pe_pv", 32 * (h + 1))
                pr.inc("v_rr", eng.reciprocal(srow[0:1, 0:TOK], ps_row[0:1, 0:TOK]))
                pr.wge(eng, "pe_rr", h + 1)
                pr.inc("v_rr", eng.tensor_copy(rrb[:], ps_q[:]))
                pr.wge(eng, "v_rr", 2 * h + 2)
                pr.inc("v_ao", eng.tensor_mul(xab[:, h * TOK:(h + 1) * TOK], ps_o[:], rrb[:]))

    with nc.Block() as blk:
        @blk.sync
        def _(eng):
            for od in range(16):
                if od == 0:
                    pr.wge(eng, "v_vc", 32)
                else:
                    pr.wge(eng, "v_wo", 2 * od)
                pr.inc("d_wo", eng.dma_start(
                    stg[:].rearrange("p (kc f) -> p kc f", kc=16),
                    woT[:, od * 128:(od + 1) * 128].rearrange("(kc kp) f -> kp kc f", kp=128)), 16)

        @blk.vector
        def _(eng):
            for od in range(16):
                pr.wge(eng, "d_wo", 16 * (od + 1))
                if od > 0:
                    pr.wge(eng, "pe_w", od)
                pr.inc("v_wo", eng.tensor_copy(w_b[:], stg[:]))
                pr.wge(eng, "pe_w", od + 1)
                pr.inc("v_wo", eng.tensor_add(xres[:, od * TOK:(od + 1) * TOK],
                                              xres[:, od * TOK:(od + 1) * TOK], ps_q[:]))

        @blk.tensor
        def _(eng):
            for od in range(16):
                pr.wge(eng, "v_wo", 2 * od + 1)
                if od == 0:
                    pr.wge(eng, "v_ao", 16)
                    pr.wge(eng, "v_rr", 32)
                else:
                    pr.wge(eng, "v_wo", 2 * od)
                inst = None
                for tc in range(2):
                    ts = slice(tc * 512, (tc + 1) * 512)
                    for kc in range(16):
                        inst = eng.matmul(ps_q[:, ts], w_b[:, kc * 128:(kc + 1) * 128],
                                          xab[:, kc * TOK + tc * 512: kc * TOK + (tc + 1) * 512],
                                          start=(kc == 0), stop=(kc == 15))
                pr.inc("pe_w", inst)

    with nc.Block() as blk:
        @blk.sync
        def _(eng):
            pr.wait(eng, "v_wo")
            for j in range(NF):
                pr.inc("d_out", eng.dma_start(xo[j * 128:(j + 1) * 128, :],
                                              xres[:, j * TOK:(j + 1) * TOK]), 16)
            pr.wait(eng, "d_out")
    ar.free()


def emit_l1ffn(pr, io):
    nc = pr.nc
    pr.prefix = "m_"
    ar = nc.auto_arena()
    for _n in ['d_c', 'd_gw', 'd_out', 'd_wd2', 'd_we', 'd_x', 'ln2bc', 'ln2dn', 'ln2dv', 'ln2pe', 'ln2rt', 'ln2sq', 'ln2st', 'pe_cb', 'pe_ct', 'pe_ed', 'pe_eg', 'pe_egu', 'pe_r', 's_r', 's_sil', 'v_cb', 'v_ct', 'v_ed', 'v_gw', 'v_h', 'v_r', 'v_wc', 'v_wdc', 'vq']:
        pr.sem(_n)
    xf = io["xf"]
    gwr = io["gwr"]
    weg = io["weg"]
    weu = io["weu"]
    wed = io["wed"]
    swg = io["swg"]
    swu = io["swu"]
    swd = io["swd"]
    onesf = io["onesf"]
    identf = io["identf"]
    cstf = io["cstf"]
    xo = io["xo"]

    NFT = 11
    xres = ar.sbuf_tensor("m_xres", [128, NF * TOK], F32)
    xln = ar.sbuf_tensor("m_xln", [128, NF * TOK], BF16)
    sq = ar.sbuf_tensor("m_sq", [128, 2 * TOK], F32)
    row_m = ar.sbuf_tensor("m_row_m", [1, TOK], F32)
    row_v = ar.sbuf_tensor("m_row_v", [1, TOK], F32)
    ones = ar.sbuf_tensor("m_ones", [128, 128], F32)
    ident = ar.sbuf_tensor("m_ident", [128, 128], F32)
    cst = ar.sbuf_tensor("m_cst", [128, 8], F32)
    gwb = ar.sbuf_tensor("m_gwb", [128, 128], BF16)
    h = ar.sbuf_tensor("m_h", [128, NFT * TOK], BF16)
    hg = ar.sbuf_tensor("m_hg", [128, TOK], BF16)
    wstg = ar.sbuf_tensor("m_wstg", [128, 2048], F32)
    wgb = ar.sbuf_tensor("m_wgb", [128, 2048], BF16)
    wub = ar.sbuf_tensor("m_wub", [128, 2048], BF16)
    wd_blk = ar.sbuf_tensor("m_wd_blk", [128, NFT * 1024], BF16)
    comb_tm = ar.sbuf_tensor("m_comb_tm", [128, 64], F32)
    probs = ar.sbuf_tensor("m_probs", [128, 8], F32)
    scs = ar.sbuf_tensor("m_scs", [128, 8], F32)
    colm = ar.sbuf_tensor("m_colm", [128, 4], F32)
    crows = ar.sbuf_tensor("m_crows", [1, TOK], F32)
    combB = ar.sbuf_tensor("m_combB", [128, TOK], F32)
    tmpb = ar.sbuf_tensor("m_tmpb", [128, TOK], F32)

    ar.allocate()
    pool = io["pool"]
    ps_sum = pool[0:1, 0:TOK]
    ps_sq = pool[0:1, TOK:2 * TOK]
    ps_bc = pool[:, 2 * TOK:4 * TOK]
    ps_l = pool[:, 0:8]
    ps_g = pool[:, 0:TOK]
    ps_u = pool[:, TOK:2 * TOK]
    ps_ct = pool[0:1, 3 * TOK:3 * TOK + 128]
    ps_cb = pool[:, 3 * TOK:4 * TOK]

    with nc.Block() as blk:
        @blk.sync
        def _(eng):
            for j in range(NF):
                pr.inc("d_x", eng.dma_start(xres[:, j * TOK:(j + 1) * TOK],
                                            xf[j * 128:(j + 1) * 128, :]), 16)
            pr.inc("d_c", eng.dma_start(ones[:], onesf), 16)
            pr.inc("d_c", eng.dma_start(ident[:], identf), 16)
            pr.inc("d_c", eng.dma_start(cst[:], cstf), 16)

    with nc.Block() as blk:
        @blk.sync
        def _(eng):
            pr.inc("d_gw", eng.dma_start(wstg[:, 0:128], gwr), 16)

        @blk.vector
        def _(eng):
            pr.wait(eng, "d_gw")
            pr.inc("v_gw", eng.tensor_copy(gwb[:], wstg[:, 0:128]))

    emit_ln(pr, "ln2", xres[:], xln[:], ones[:], sq[:], row_m[:], row_v[:],
            ps_sum, ps_sq, ps_bc, TOK,
            [("d_x", pr.val("d_x")), ("d_c", pr.val("d_c"))], cst[0:1, 0:1])

    for tt in range(8):
        with nc.Block() as blk:
            @blk.tensor
            def _(eng):
                if tt == 0:
                    pr.wait(eng, "ln2dn")
                    pr.wait(eng, "v_gw")
                pr.wge(eng, "v_r", 2 * tt)
                inst = None
                for kc in range(16):
                    inst = eng.matmul(ps_l[:, 0:8], xln[:, kc * TOK + tt * 128: kc * TOK + (tt + 1) * 128],
                                      gwb[:, kc * 8:(kc + 1) * 8],
                                      start=(kc == 0), stop=(kc == 15))
                pr.inc("pe_r", inst)

            @blk.scalar
            def _(eng):
                pr.wge(eng, "v_r", 2 * tt + 1)
                pr.inc("s_r", eng.activation(probs[:], ps_l[:, 0:8], AF.Exp,
                                             bias=colm[:, 1:2]))

            @blk.vector
            def _(eng):
                pr.wge(eng, "pe_r", tt + 1)
                pr.chain(eng, "vq", eng.reduce_max(colm[:, 0:1], ps_l[:, 0:8], axis=AX.X))
                pr.inc("v_r", eng.tensor_scalar(colm[:, 1:2], colm[:, 0:1], -1.0, None, ALU.mult))
                pr.wge(eng, "s_r", tt + 1)
                pr.chain(eng, "vq", eng.reduce_sum(colm[:, 0:1], probs[:], axis=AX.X))
                pr.chain(eng, "vq", eng.reciprocal(colm[:, 1:2], colm[:, 0:1]))
                pr.chain(eng, "vq", eng.tensor_scalar(scs[:], probs[:], colm[:, 1:2], None, ALU.mult))
                pr.chain(eng, "vq", eng.reduce_max(colm[:, 0:1], scs[:], axis=AX.X))
                pr.chain(eng, "vq", eng.tensor_scalar(probs[:], scs[:], colm[:, 0:1], None, ALU.is_ge))
                pr.chain(eng, "vq", eng.scalar_tensor_tensor(probs[:], probs[:], -1e30, scs[:],
                                                             ALU.mult, ALU.add))
                pr.chain(eng, "vq", eng.reduce_max(colm[:, 2:3], probs[:], axis=AX.X))
                pr.chain(eng, "vq", eng.tensor_scalar(probs[:], scs[:], colm[:, 2:3], None, ALU.is_ge))
                pr.inc("v_r", eng.tensor_mul(comb_tm[:, tt * 8:(tt + 1) * 8], scs[:], probs[:]))

    def g_slab(r, ft):
        if r < 8:
            return weg[r, :, ft * 128:(ft + 1) * 128]
        return swg[:, (r - 8) * 1408 + ft * 128: (r - 8) * 1408 + (ft + 1) * 128]

    def u_slab(r, ft):
        if r < 8:
            return weu[r, :, ft * 128:(ft + 1) * 128]
        return swu[:, (r - 8) * 1408 + ft * 128: (r - 8) * 1408 + (ft + 1) * 128]

    def d_slab(r, fc):
        if r < 8:
            return wed[r, fc * 128:(fc + 1) * 128, :]
        return swd[(r - 8) * 1408 + fc * 128: (r - 8) * 1408 + (fc + 1) * 128, :]

    with nc.Block() as blk:
        @blk.sync
        def _(eng):
            for r in range(10):
                for ft in range(NFT):
                    ftg = r * NFT + ft
                    if ftg > 0:
                        pr.wge(eng, "v_wc", 2 * ftg)
                    if ft == 0 and r > 0:
                        pr.wge(eng, "v_wdc", 22 * r)
                    pr.inc("d_we", eng.dma_start(
                        wstg[:].rearrange("p (kc f) -> p kc f", kc=16),
                        g_slab(r, ft).rearrange("(kc kp) f -> kp kc f", kp=128)), 16)
                    pr.wge(eng, "v_wc", 2 * ftg + 1)
                    pr.inc("d_we", eng.dma_start(
                        wstg[:].rearrange("p (kc f) -> p kc f", kc=16),
                        u_slab(r, ft).rearrange("(kc kp) f -> kp kc f", kp=128)), 16)
                for dh in range(2):
                    for fc in range(NFT):
                        fcg = 22 * r + dh * NFT + fc
                        if fcg > 0:
                            pr.wge(eng, "v_wdc", fcg)
                        if fc == 0 and dh == 0:
                            pr.wge(eng, "v_wc", 22 * (r + 1))
                        pr.inc("d_wd2", eng.dma_start(wstg[:], d_slab(r, fc)), 16)

        @blk.scalar
        def _(eng):
            for r in range(10):
                for ft in range(NFT):
                    ftg = r * NFT + ft
                    pr.wge(eng, "pe_eg", ftg + 1)
                    pr.wge(eng, "v_h", ftg)
                    pr.inc("s_sil", eng.activation(hg[:], ps_g[:], AF.Silu))

        @blk.tensor
        def _(eng):
            for r in range(10):
                is_exp = r < 8
                if is_exp:
                    pr.wge(eng, "v_ed", 16 * r)
                    pr.wge(eng, "v_r", 16)
                    for tt in range(8):
                        if tt > 0:
                            pr.wge(eng, "v_ct", 8 * r + tt)
                        pr.inc("pe_ct", eng.transpose(ps_ct[0:1, 0:128],
                                                      comb_tm[:, tt * 8 + r:tt * 8 + r + 1],
                                                      ident[:]))
                    pr.wge(eng, "v_ct", 8 * (r + 1))
                    inst = None
                    for sc_ in range(2):
                        os_ = slice(sc_ * 512, (sc_ + 1) * 512)
                        inst = eng.matmul(ps_cb[:, os_], ones[0:1, :], crows[0:1, os_],
                                          start=True, stop=True)
                    pr.inc("pe_cb", inst)
                for ft in range(NFT):
                    ftg = r * NFT + ft
                    pr.wge(eng, "v_wc", 2 * ftg + 1)
                    if ftg == 0:
                        pr.wait(eng, "ln2dn")
                        pr.wge(eng, "v_r", 16)
                    pr.wge(eng, "s_sil", ftg)
                    pr.wge(eng, "v_h", ftg)
                    inst = None
                    for tc in range(2):
                        ts = slice(tc * 512, (tc + 1) * 512)
                        for kc in range(16):
                            inst = eng.matmul(ps_g[:, ts], wgb[:, kc * 128:(kc + 1) * 128],
                                              xln[:, kc * TOK + tc * 512: kc * TOK + (tc + 1) * 512],
                                              start=(kc == 0), stop=(kc == 15))
                    pr.inc("pe_eg", inst)
                    pr.wge(eng, "v_wc", 2 * ftg + 2)
                    for tc in range(2):
                        ts = slice(tc * 512, (tc + 1) * 512)
                        for kc in range(16):
                            inst = eng.matmul(ps_u[:, ts], wub[:, kc * 128:(kc + 1) * 128],
                                              xln[:, kc * TOK + tc * 512: kc * TOK + (tc + 1) * 512],
                                              start=(kc == 0), stop=(kc == 15))
                    pr.inc("pe_egu", inst)
                for dh in range(2):
                    for dgl in range(2):
                        dg = 2 * dh + dgl
                        pr.wge(eng, "v_wdc", 22 * r + NFT * (dh + 1))
                        pr.wge(eng, "v_h", NFT * (r + 1))
                        pr.wge(eng, "v_ed", 16 * r + 4 * dg)
                        if is_exp and dg == 3:
                            pr.wge(eng, "v_cb", r + 1)
                        inst = None
                        for dl in range(4):
                            dt = dg * 4 + dl
                            dloc = dt - 8 * dh
                            psd = pool[:, dl * TOK:(dl + 1) * TOK]
                            for fc in range(NFT):
                                for tc in range(2):
                                    ts = slice(tc * 512, (tc + 1) * 512)
                                    inst = eng.matmul(
                                        psd[:, ts],
                                        wd_blk[:, fc * 1024 + dloc * 128: fc * 1024 + (dloc + 1) * 128],
                                        h[:, fc * TOK + tc * 512: fc * TOK + (tc + 1) * 512],
                                        start=(fc == 0), stop=(fc == NFT - 1))
                        pr.inc("pe_ed", inst)

        @blk.vector
        def _(eng):
            for r in range(10):
                is_exp = r < 8
                if is_exp:
                    for tt in range(8):
                        pr.wge(eng, "pe_ct", 8 * r + tt + 1)
                        pr.inc("v_ct", eng.tensor_copy(
                            crows[0:1, tt * 128:(tt + 1) * 128], ps_ct[0:1, 0:128]))
                    pr.wge(eng, "pe_cb", r + 1)
                    pr.inc("v_cb", eng.tensor_copy(combB[:], ps_cb[:]))
                for ft in range(NFT):
                    ftg = r * NFT + ft
                    pr.wge(eng, "d_we", 32 * ftg + 16)
                    pr.wge(eng, "pe_egu", ftg)
                    pr.inc("v_wc", eng.tensor_copy(wgb[:], wstg[:]))
                    pr.wge(eng, "d_we", 32 * ftg + 32)
                    pr.inc("v_wc", eng.tensor_copy(wub[:], wstg[:]))
                    pr.wge(eng, "pe_egu", ftg + 1)
                    pr.wge(eng, "s_sil", ftg + 1)
                    if ft == 0 and r > 0:
                        pr.wge(eng, "pe_ed", 4 * r)
                    pr.inc("v_h", eng.tensor_mul(h[:, ft * TOK:(ft + 1) * TOK], hg[:], ps_u[:]))
                for dh in range(2):
                    for fc in range(NFT):
                        fcg = 22 * r + dh * NFT + fc
                        pr.wge(eng, "d_wd2", 16 * (fcg + 1))
                        if fc == 0:
                            pr.wge(eng, "pe_ed", 4 * r + 2 * dh)
                        pr.inc("v_wdc", eng.tensor_copy(
                            wd_blk[:, fc * 1024:(fc + 1) * 1024],
                            wstg[:, dh * 1024:(dh + 1) * 1024]))
                    for dgl in range(2):
                        dg = 2 * dh + dgl
                        pr.wge(eng, "pe_ed", 4 * r + dg + 1)
                        if is_exp:
                            pr.wge(eng, "v_cb", r + 1)
                        for dl in range(4):
                            dt = dg * 4 + dl
                            psd = pool[:, dl * TOK:(dl + 1) * TOK]
                            if is_exp:
                                pr.wge(eng, "v_ed", 16 * r + 4 * dg + dl)
                                pr.chain(eng, "vq", eng.tensor_mul(tmpb[:], psd, combB[:]))
                                pr.inc("v_ed", eng.tensor_add(xres[:, dt * TOK:(dt + 1) * TOK],
                                                              xres[:, dt * TOK:(dt + 1) * TOK], tmpb[:]))
                            else:
                                pr.inc("v_ed", eng.scalar_tensor_tensor(
                                    xres[:, dt * TOK:(dt + 1) * TOK], psd, 1.0,
                                    xres[:, dt * TOK:(dt + 1) * TOK], ALU.mult, ALU.add))

    with nc.Block() as blk:
        @blk.sync
        def _(eng):
            pr.wait(eng, "v_ed")
            for j in range(NF):
                pr.inc("d_out", eng.dma_start(xo[j * 128:(j + 1) * 128, :],
                                              xres[:, j * TOK:(j + 1) * TOK]), 16)
            pr.wait(eng, "d_out")
    ar.free()


def _l0a_maps(inputs, x):
    winT = np.ascontiguousarray(np.asarray(inputs["L0_attn_in_proj_w"], np.float32).T)
    conv_w = np.asarray(inputs["L0_attn_conv_w"], np.float32)
    cwf = np.ascontiguousarray(conv_w.reshape(32, 128, 4).transpose(1, 0, 2).reshape(128, 128))
    conv_b = np.asarray(inputs["L0_attn_conv_b"], np.float32)
    cbf = np.ascontiguousarray(conv_b.reshape(32, 128).T)
    wxT = np.ascontiguousarray(np.asarray(inputs["L0_attn_x_proj_w"], np.float32).T)
    maps = []
    for c in range(NCORES):
        seq, half = c // 2, c % 2
        t0 = half * TOK
        if half == 0:
            xe = np.concatenate([np.zeros((3, D), np.float32), x[seq, 0:TOK]], axis=0)
        else:
            xe = x[seq, t0 - 3:t0 + TOK]
        xfh = np.ascontiguousarray(xe.T)
        maps.append(dict(xf=xfh, winT=winT, cwf=cwf, cbf=cbf, wxT=wxT,
                         onesf=_ONES, cstf=_CST))
    return maps


def _l0b_maps(inputs, xf_percore):
    dtwT = np.ascontiguousarray(np.asarray(inputs["L0_attn_dt_proj_w"], np.float32).T)
    dtb = np.ascontiguousarray(np.asarray(inputs["L0_attn_dt_proj_b"], np.float32).reshape(64, 1))
    alog = np.ascontiguousarray(np.asarray(inputs["L0_attn_A_log"], np.float32).reshape(64, 1))
    Dv = np.asarray(inputs["L0_attn_D"], np.float32)
    ddc = np.zeros((128, 32), np.float32)
    for J in range(32):
        ddc[0:64, J] = Dv[2 * J]
        ddc[64:128, J] = Dv[2 * J + 1]
    woutT = np.ascontiguousarray(np.asarray(inputs["L0_attn_out_proj_w"], np.float32).T)
    repa = np.ascontiguousarray(np.kron(np.eye(64), np.ones((1, 64))).astype(BF))
    trilq = np.triu(np.ones((64, 64), np.float32))
    tril2 = np.ascontiguousarray(np.tile(trilq, (1, 64)).astype(BF))
    repi = np.ascontiguousarray(np.tile(np.eye(64, dtype=np.float32), (1, 64)).astype(BF))
    repd = np.zeros((64, 4096), BF)
    for J in range(32):
        repd[2 * J, J * 128:J * 128 + 64] = 1
        repd[2 * J + 1, J * 128 + 64:J * 128 + 128] = 1
    trius = np.ascontiguousarray(np.tril(np.ones((64, 64), np.float32), -1).astype(BF))
    maskls = np.ascontiguousarray(np.tril(np.ones((64, 64), np.float32)).astype(BF))
    maps = []
    for c in range(NCORES):
        selv = np.zeros((128, 8), np.float32)
        if c % 2 == 1:
            selv[:, c - 1] = 1.0
        maps.append(dict(xf=xf_percore[c], dtwT=dtwT, dtb=dtb, alog=alog, ddf=ddc,
                         woutT=woutT, selvf=selv, onesf=_ONES, repaf=repa, trilf=tril2,
                         repif=repi, repdf=repd, triusf=trius, maskls=maskls, cstf=_CST))
    return maps


def _moe_maps(inputs, xf_percore):
    gw = np.asarray(inputs["L1_ffn_gate_w"], np.float32)
    gwr = np.ascontiguousarray(gw.reshape(16, 128, 8).transpose(1, 0, 2).reshape(128, 128))
    weg = np.ascontiguousarray(np.asarray(inputs["L1_ffn_w_gate"], np.float32))
    weu = np.ascontiguousarray(np.asarray(inputs["L1_ffn_w_up"], np.float32))
    wed = np.ascontiguousarray(np.asarray(inputs["L1_ffn_w_down"], np.float32))
    swg = np.ascontiguousarray(np.asarray(inputs["L1_ffn_sw_gate"], np.float32))
    swu = np.ascontiguousarray(np.asarray(inputs["L1_ffn_sw_up"], np.float32))
    swd = np.ascontiguousarray(np.asarray(inputs["L1_ffn_sw_down"], np.float32))
    return [dict(xf=xf_percore[c], gwr=gwr, weg=weg, weu=weu, wed=wed,
                 swg=swg, swu=swu, swd=swd, onesf=_ONES, identf=_IDENT, cstf=_CST)
            for c in range(NCORES)]


TOKH = 1027


def emit_l0a(pr, io):
    nc = pr.nc
    pr.prefix = "a_"
    ar = nc.auto_arena()
    for _n in ['d_c', 'd_lx0', 'd_lx1', 'd_ly0', 'd_ly1', 'd_wi', 'd_wx', 'd_wz', 'd_xo0', 'd_xo1', 'd_zo0', 'd_zo1', 'lnA_pe', 'lnA_sq', 'lnC_dv', 'ln_bc', 'ln_dv', 'ln_rt', 'pe_ip', 'pe_xp', 'pe_z', 's_u', 's_z', 'v_cv', 'v_wc', 'v_wx', 'v_xe', 'vq']:
        pr.sem(_n)
    xf = io["xf"]
    winT = io["winT"]
    cwf = io["cwf"]
    cbf = io["cbf"]
    wxT = io["wxT"]
    onesf = io["onesf"]
    cstf = io["cstf"]
    xfm = io["xfm"]
    bcdt = io["bcdt"]
    zsf = io["zsf"]

    xln = ar.sbuf_tensor("a_xln", [128, NF * TOKH], BF16)
    u = ar.sbuf_tensor("a_u", [128, 32 * TOK], BF16)
    xstg = ar.sbuf_tensor("a_xstg", [128, 2 * TOKH], F32)
    sq = ar.sbuf_tensor("a_sq", [128, 2 * TOKH], F32)
    row_m = ar.sbuf_tensor("a_row_m", [1, TOKH], F32)
    row_v = ar.sbuf_tensor("a_row_v", [1, TOKH], F32)
    ones = ar.sbuf_tensor("a_ones", [128, 128], F32)
    cst = ar.sbuf_tensor("a_cst", [128, 8], F32)
    cw = ar.sbuf_tensor("a_cw", [128, 128], F32)
    cb = ar.sbuf_tensor("a_cb", [128, 32], F32)
    wstg = ar.sbuf_tensor("a_wstg", [128, 2048], F32)
    w_b = ar.sbuf_tensor("a_w_b", [128, 4096], BF16)
    xb = ar.sbuf_tensor("a_xb", [128, TOKH], F32)
    acc1 = ar.sbuf_tensor("a_acc1", [128, TOK], F32)
    acc2 = ar.sbuf_tensor("a_acc2", [128, TOK], F32)
    ostg = ar.sbuf_tensor("a_ostg", [128, 2 * TOK], BF16)

    ar.allocate()
    pool = io["pool"]
    ps_sum = pool[0:1, 0:TOKH]
    ps_sq = pool[0:1, 1536:1536 + TOKH]
    ps_bcm = pool[:, 0:TOKH]
    ps_bcr = pool[:, 1536:1536 + TOKH]
    ps_m = pool[:, 0:TOKH]
    ps_x = pool[:, 0:TOK]

    with nc.Block() as blk:
        @blk.sync
        def _(eng):
            pr.inc("d_c", eng.dma_start(ones[:], onesf), 16)
            pr.inc("d_c", eng.dma_start(cst[:], cstf), 16)
            pr.inc("d_c", eng.dma_start(cw[:], cwf), 16)
            pr.inc("d_c", eng.dma_start(cb[:], cbf), 16)

    with nc.Block() as blk:
        @blk.sync
        def _(eng):
            for j in range(NF):
                if j >= 2:
                    pr.wge(eng, "lnA_pe", j - 1)
                pr.inc("d_lx%d" % (j % 2), eng.dma_start(
                    xstg[:, (j % 2) * TOKH:(j % 2 + 1) * TOKH],
                    xf[j * 128:(j + 1) * 128, :]), 16)

        @blk.scalar
        def _(eng):
            for j in range(NF):
                pr.wge(eng, "d_lx%d" % (j % 2), 16 * (j // 2 + 1))
                if j >= 2:
                    pr.wge(eng, "lnA_pe", j - 1)
                pr.inc("lnA_sq", eng.activation(sq[:, (j % 2) * TOKH:(j % 2 + 1) * TOKH],
                                                xstg[:, (j % 2) * TOKH:(j % 2 + 1) * TOKH],
                                                AF.Square))

        @blk.tensor
        def _(eng):
            pr.wait(eng, "d_c")
            for j in range(NF):
                pr.wge(eng, "d_lx%d" % (j % 2), 16 * (j // 2 + 1))
                xt = xstg[:, (j % 2) * TOKH:(j % 2 + 1) * TOKH]
                for c in range(3):
                    c0, c1 = c * 512, min((c + 1) * 512, TOKH)
                    eng.matmul(ps_sum[0:1, c0:c1], ones[:, 0:1], xt[:, c0:c1],
                               start=(j == 0), stop=(j == NF - 1))
                pr.wge(eng, "lnA_sq", j + 1)
                sqt = sq[:, (j % 2) * TOKH:(j % 2 + 1) * TOKH]
                inst = None
                for c in range(3):
                    c0, c1 = c * 512, min((c + 1) * 512, TOKH)
                    inst = eng.matmul(ps_sq[0:1, c0:c1], ones[:, 0:1], sqt[:, c0:c1],
                                      start=(j == 0), stop=(j == NF - 1))
                pr.inc("lnA_pe", inst)

    with nc.Block() as blk:
        @blk.vector
        def _(eng):
            pr.wge(eng, "lnA_pe", NF)
            pr.chain(eng, "vq", eng.tensor_scalar(row_m[0:1, :], ps_sum[0:1, :], 1.0 / D, None, ALU.mult))
            pr.chain(eng, "vq", eng.tensor_scalar(row_v[0:1, :], ps_sq[0:1, :], 1.0 / D, None, ALU.mult))
            pr.chain(eng, "vq", eng.tensor_mul(sq[0:1, 0:TOKH], row_m[0:1, :], row_m[0:1, :]))
            pr.inc("ln_dv", eng.tensor_sub(row_v[0:1, :], row_v[0:1, :], sq[0:1, 0:TOKH]))

        @blk.scalar
        def _(eng):
            pr.wait(eng, "ln_dv")
            pr.wait(eng, "d_c")
            pr.inc("ln_rt", eng.activation(row_v[0:1, :], row_v[0:1, :], AF.Sqrt,
                                           bias=cst[0:1, 0:1]))

    with nc.Block() as blk:
        @blk.vector
        def _(eng):
            pr.wait(eng, "ln_rt")
            pr.inc("ln_rt", eng.reciprocal(row_v[0:1, :], row_v[0:1, :]))

        @blk.tensor
        def _(eng):
            pr.wge(eng, "ln_rt", 2)
            inst = None
            for c in range(3):
                c0, c1 = c * 512, min((c + 1) * 512, TOKH)
                eng.matmul(ps_bcm[:, c0:c1], ones[0:1, :], row_m[0:1, c0:c1],
                           start=True, stop=True)
                inst = eng.matmul(ps_bcr[:, c0:c1], ones[0:1, :],
                                  row_v[0:1, c0:c1], start=True, stop=True)
            pr.inc("ln_bc", inst)

    with nc.Block() as blk:
        @blk.sync
        def _(eng):
            for j in range(NF):
                if j >= 2:
                    pr.wge(eng, "lnC_dv", j - 1)
                pr.inc("d_ly%d" % (j % 2), eng.dma_start(
                    xstg[:, (j % 2) * TOKH:(j % 2 + 1) * TOKH],
                    xf[j * 128:(j + 1) * 128, :]), 16)

        @blk.vector
        def _(eng):
            pr.wait(eng, "ln_bc")
            for j in range(NF):
                pr.wge(eng, "d_ly%d" % (j % 2), 16 * (j // 2 + 1))
                tmp = sq[:, (j % 2) * TOKH:(j % 2 + 1) * TOKH]
                pr.chain(eng, "vq", eng.tensor_sub(tmp, xstg[:, (j % 2) * TOKH:(j % 2 + 1) * TOKH],
                                                   ps_bcm[:]))
                pr.inc("lnC_dv", eng.tensor_mul(xln[:, j * TOKH:(j + 1) * TOKH], tmp,
                                                ps_bcr[:]))

    with nc.Block() as blk:
        @blk.sync
        def _(eng):
            for j in range(32):
                if j > 0:
                    pr.wge(eng, "v_wc", j)
                pr.inc("d_wi", eng.dma_start(
                    wstg[:].rearrange("p (kc f) -> p kc f", kc=16),
                    winT[:, j * 128:(j + 1) * 128].rearrange("(kc kp) f -> kp kc f", kp=128)), 16)

        @blk.vector
        def _(eng):
            for j in range(32):
                pr.wge(eng, "d_wi", 16 * (j + 1))
                if j > 0:
                    pr.wge(eng, "pe_ip", j)
                pr.inc("v_wc", eng.tensor_copy(w_b[:, 0:2048], wstg[:]))
                if j == 0:
                    pr.wait(eng, "d_c")
                pr.wge(eng, "pe_ip", j + 1)
                pr.wge(eng, "s_u", j)
                pr.chain(eng, "vq", eng.tensor_copy(xb[:], ps_m[:]))
                pr.chain(eng, "vq", eng.tensor_scalar(acc1[:], xb[:, 0:TOK], cw[:, 4 * (j % 32):4 * (j % 32) + 1], None, ALU.mult))
                pr.chain(eng, "vq", eng.scalar_tensor_tensor(acc2[:], xb[:, 1:1 + TOK], cw[:, 4 * (j % 32) + 1:4 * (j % 32) + 2], acc1[:], ALU.mult, ALU.add))
                pr.chain(eng, "vq", eng.scalar_tensor_tensor(acc1[:], xb[:, 2:2 + TOK], cw[:, 4 * (j % 32) + 2:4 * (j % 32) + 3], acc2[:], ALU.mult, ALU.add))
                pr.inc("v_cv", eng.scalar_tensor_tensor(acc2[:], xb[:, 3:3 + TOK], cw[:, 4 * (j % 32) + 3:4 * (j % 32) + 4], acc1[:], ALU.mult, ALU.add))

        @blk.scalar
        def _(eng):
            for j in range(32):
                if j == 0:
                    pr.wait(eng, "d_c")
                pr.wge(eng, "v_cv", j + 1)
                pr.inc("s_u", eng.activation(u[:, j * TOK:(j + 1) * TOK], acc2[:], AF.Silu,
                                             bias=cb[:, j:j + 1]))

        @blk.tensor
        def _(eng):
            for j in range(32):
                pr.wge(eng, "v_wc", j + 1)
                if j == 0:
                    pr.wge(eng, "lnC_dv", NF)
                pr.wge(eng, "v_cv", j)
                inst = None
                for c in range(3):
                    c0, c1 = c * 512, min((c + 1) * 512, TOKH)
                    for kc in range(16):
                        inst = eng.matmul(ps_m[:, c0:c1], w_b[:, kc * 128:(kc + 1) * 128],
                                          xln[:, kc * TOKH + c0: kc * TOKH + c1],
                                          start=(kc == 0), stop=(kc == 15))
                pr.inc("pe_ip", inst)

    NXF = 35
    with nc.Block() as blk:
        @blk.sync
        def _(eng):
            for xfc in range(NXF):
                fw = 128 if xfc < 34 else 64
                for piece in range(2):
                    if xfc + piece > 0:
                        pr.wge(eng, "v_wx", 2 * xfc + piece)
                    pr.inc("d_wx", eng.dma_start(
                        wstg[:, 0:16 * fw].rearrange("p (kc f) -> p kc f", kc=16),
                        wxT[piece * 2048:(piece + 1) * 2048,
                            xfc * 128:xfc * 128 + fw].rearrange("(kc kp) f -> kp kc f", kp=128)), 16)

        @blk.vector
        def _(eng):
            for xfc in range(NXF):
                fw = 128 if xfc < 34 else 64
                for piece in range(2):
                    pr.wge(eng, "d_wx", 16 * (2 * xfc + piece + 1))
                    if xfc > 0 and piece == 0:
                        pr.wge(eng, "pe_xp", xfc)
                    pr.inc("v_wx", eng.tensor_copy(
                        w_b[:, piece * 2048:piece * 2048 + 16 * fw], wstg[:, 0:16 * fw]))
                pr.wge(eng, "pe_xp", xfc + 1)
                if xfc >= 2:
                    pr.wge(eng, "d_xo%d" % (xfc % 2), 16 * ((xfc - 2) // 2 + 1))
                pr.inc("v_xe", eng.tensor_copy(ostg[0:fw, (xfc % 2) * TOK:(xfc % 2 + 1) * TOK],
                                               ps_x[0:fw, :]))

        @blk.tensor
        def _(eng):
            for xfc in range(NXF):
                fw = 128 if xfc < 34 else 64
                pr.wge(eng, "v_wx", 2 * (xfc + 1))
                if xfc == 0:
                    pr.wge(eng, "s_u", 32)
                pr.wge(eng, "v_xe", xfc)
                inst = None
                for tc in range(2):
                    ts = slice(tc * 512, (tc + 1) * 512)
                    for kc in range(32):
                        inst = eng.matmul(ps_x[0:fw, ts],
                                          w_b[:, (kc // 16) * 2048 + (kc % 16) * fw:
                                              (kc // 16) * 2048 + (kc % 16) * fw + fw],
                                          u[:, kc * TOK + tc * 512: kc * TOK + (tc + 1) * 512],
                                          start=(kc == 0), stop=(kc == 31))
                pr.inc("pe_xp", inst)

        @blk.gpsimd
        def _(eng):
            for xfc in range(NXF):
                fw = 128 if xfc < 34 else 64
                pr.wge(eng, "v_xe", xfc + 1)
                src = ostg[0:fw, (xfc % 2) * TOK:(xfc % 2) * TOK + TOK]
                if xfc < 32:
                    dst = xfm[xfc * 128:(xfc + 1) * 128, :]
                elif xfc < 34:
                    dst = bcdt[(xfc - 32) * 128:(xfc - 31) * 128, :]
                else:
                    dst = bcdt[256:320, :]
                pr.inc("d_xo%d" % (xfc % 2), eng.dma_start(dst, src), 16)

    with nc.Block() as blk:
        @blk.sync
        def _(eng):
            for j in range(32):
                pr.wge(eng, "v_wc", 32 + j)
                if j == 0:
                    pr.wge(eng, "v_wx", 70)
                pr.inc("d_wz", eng.dma_start(
                    wstg[:].rearrange("p (kc f) -> p kc f", kc=16),
                    winT[:, 4096 + j * 128:4096 + (j + 1) * 128].rearrange("(kc kp) f -> kp kc f", kp=128)), 16)

        @blk.vector
        def _(eng):
            for j in range(32):
                pr.wge(eng, "d_wz", 16 * (j + 1))
                if j > 0:
                    pr.wge(eng, "pe_z", j)
                pr.inc("v_wc", eng.tensor_copy(w_b[:, 0:2048], wstg[:]))

        @blk.scalar
        def _(eng):
            for j in range(32):
                pr.wge(eng, "pe_z", j + 1)
                if j == 0:
                    pr.wge(eng, "d_xo0", pr.val("d_xo0"))
                    pr.wge(eng, "d_xo1", pr.val("d_xo1"))
                if j >= 2:
                    pr.wge(eng, "d_zo%d" % (j % 2), 16 * ((j - 2) // 2 + 1))
                pr.inc("s_z", eng.activation(ostg[:, (j % 2) * TOK:(j % 2 + 1) * TOK],
                                             ps_x[:], AF.Silu))

        @blk.tensor
        def _(eng):
            for j in range(32):
                pr.wge(eng, "v_wc", 33 + j)
                pr.wge(eng, "s_z", j)
                if j == 0:
                    pr.wge(eng, "v_xe", NXF)
                inst = None
                for tc in range(2):
                    ts = slice(tc * 512, (tc + 1) * 512)
                    for kc in range(16):
                        inst = eng.matmul(ps_x[:, ts], w_b[:, kc * 128:(kc + 1) * 128],
                                          xln[:, kc * TOKH + 3 + tc * 512: kc * TOKH + 3 + (tc + 1) * 512],
                                          start=(kc == 0), stop=(kc == 15))
                pr.inc("pe_z", inst)

        @blk.gpsimd
        def _(eng):
            for j in range(32):
                pr.wge(eng, "s_z", j + 1)
                pr.inc("d_zo%d" % (j % 2), eng.dma_start(zsf[j * 128:(j + 1) * 128, :],
                                             ostg[:, (j % 2) * TOK:(j % 2 + 1) * TOK]), 16)

    with nc.Block() as blk:
        @blk.sync
        def _(eng):
            pr.wait(eng, "d_xo0")
            pr.wait(eng, "d_xo1")
            pr.wait(eng, "d_zo0")
            pr.wait(eng, "d_zo1")
    ar.free()


def emit_l0b(pr, io):
    nc = pr.nc
    pr.prefix = "b_"
    ar = nc.auto_arena()
    for _n in ['cc', 'd_blo', 'd_bz', 'd_c', 'd_lam', 'd_or0', 'd_or1', 'd_sg', 'd_sh', 'd_wo', 'd_xl0', 'd_xl1', 'd_xlo', 'd_xo0', 'd_xo1', 'd_xr0', 'd_xr1', 'd_xz', 'd_zo0', 'd_zo1', 'd_zs0', 'd_zs1', 'p_db', 'p_dbl', 'p_dx', 'p_dxl', 'p_s0', 'p_sc', 'p_st', 'p_w', 'p_xw', 'pe_a', 'pe_b', 'pe_do', 'pe_ds', 'pe_lam', 'pe_op', 'pe_pp', 'pe_rs', 'pe_s1', 'pe_st', 'pe_w', 'pe_yd', 'pe_yo', 'pe_z', 's_ge', 's_pp', 's_z', 'sq2', 'v_do', 'v_gf', 'v_lam', 'v_lc', 'v_oe', 'v_pp', 'v_ra', 'v_sc', 'v_sel', 'v_sm', 'v_wo', 'v_xd', 'v_xw', 'v_ye', 'v_yg', 'vq']:
        pr.sem(_n)
    xf = io["xf"]
    xfm = io["xfm"]
    bcdt = io["bcdt"]
    zsf = io["zsf"]
    dtwT = io["dtwT"]
    dtb = io["dtb"]
    alog = io["alog"]
    ddf = io["ddf"]
    woutT = io["woutT"]
    onesf = io["onesf"]
    repaf = io["repaf"]
    trilf = io["trilf"]
    repif = io["repif"]
    repdf = io["repdf"]
    triusf = io["triusf"]
    maskls = io["maskls"]
    cstf = io["cstf"]
    xo = io["xo"]

    Y = ar.sbuf_tensor("b_Y", [128, 32 * 512], BF16)
    S_run = ar.sbuf_tensor("b_S_run", [128, 4096], F32)
    Lam = ar.sbuf_tensor("b_Lam", [128, 1024], F32)
    dt = ar.sbuf_tensor("b_dt", [64, 1024], F32)
    scr1 = ar.sbuf_tensor("b_scr1", [128, 1024], F32)
    scr2 = ar.sbuf_tensor("b_scr2", [128, 1024], F32)
    vbuf = scr1[0:64, :]
    t1 = scr2[0:64, :]
    cs = ar.sbuf_tensor("b_cs", [64, 1024], F32)
    zeros64 = ar.sbuf_tensor("b_zeros64", [64, 64], F32)
    A_bf = ar.sbuf_tensor("b_A_bf", [64, 1024], BF16)
    dt_bf = ar.sbuf_tensor("b_dt_bf", [64, 1024], BF16)
    w_bf = ar.sbuf_tensor("b_w_bf", [64, 1024], BF16)
    decay_bf = ar.sbuf_tensor("b_decay_bf", [64, 1024], BF16)
    dtraw = ar.sbuf_tensor("b_dtraw", [64, 1024], BF16)
    lam = ar.sbuf_tensor("b_lam", [64, 16], F32)
    B_fm = ar.sbuf_tensor("b_B_fm", [128, 1024], BF16)
    C_fm = ar.sbuf_tensor("b_C_fm", [128, 1024], BF16)
    C_f32 = ar.sbuf_tensor("b_C_f32", [128, 1024], F32)
    Bz = ar.sbuf_tensor("b_Bz", [128, 128], BF16)
    Gbuf = ar.sbuf_tensor("b_Gbuf", [128, 4096], BF16)
    Gtmp = ar.sbuf_tensor("b_Gtmp", [128, 4096], BF16)
    rhsA = ar.sbuf_tensor("b_rhsA", [128, 4096], BF16)
    Xz = ar.sbuf_tensor("b_Xz", [128, 4096], BF16)
    Xlo = ar.sbuf_tensor("b_Xlo", [64, 4096], BF16)
    Bzl = ar.sbuf_tensor("b_Bzl", [64, 128], BF16)
    Xdt = ar.sbuf_tensor("b_Xdt", [128, 4096], BF16)
    Xw = ar.sbuf_tensor("b_Xw", [128, 4096], BF16)
    S_m = ar.sbuf_tensor("b_S_m", [128, 64], BF16)
    dorep = ar.sbuf_tensor("b_dorep", [128, 2048], BF16)
    ycat = ar.sbuf_tensor("b_ycat", [128, 1024], BF16)
    tmpv = ar.sbuf_tensor("b_tmpv", [128, 1024], F32)
    lamr = tmpv[0:1, 0:1024]
    repa = ar.sbuf_tensor("b_repa", [64, 4096], BF16)
    tril = ar.sbuf_tensor("b_tril", [64, 4096], BF16)
    repi = ar.sbuf_tensor("b_repi", [64, 4096], BF16)
    repd = ar.sbuf_tensor("b_repd", [64, 4096], BF16)
    trius = ar.sbuf_tensor("b_trius", [64, 64], BF16)
    maskl = ar.sbuf_tensor("b_maskl", [64, 64], BF16)
    ddc = ar.sbuf_tensor("b_ddc", [128, 32], F32)
    ones = ar.sbuf_tensor("b_ones", [128, 128], F32)
    cst = ar.sbuf_tensor("b_cst", [128, 8], F32)
    dtw_b = ar.sbuf_tensor("b_dtw_b", [64, 64], BF16)
    dtb_c = ar.sbuf_tensor("b_dtb_c", [64, 1], F32)
    alog_c = ar.sbuf_tensor("b_alog_c", [64, 1], F32)
    Acol = ar.sbuf_tensor("b_Acol", [64, 1], F32)
    selv = ar.sbuf_tensor("b_selv", [128, 8], F32)
    zstg = ar.sbuf_tensor("b_zstg", [128, 2 * 512], BF16)
    xflstg = ar.sbuf_tensor("b_xflstg", [128, 2 * 512], BF16)
    wstg = ar.sbuf_tensor("b_wstg", [128, 2048], F32)
    wout_b = ar.sbuf_tensor("b_wout_b", [128, 4096], BF16)
    xstg = scr1
    ostg = scr2

    ar.allocate()
    pool = io["pool"]
    psYd = pool[:, 0:1024]
    psYo = pool[:, 1024:2048]
    psSt = pool[:, 2048:4096]
    psDo = pool[:, 0:2048]
    psOP = pool[:, 0:512]

    L64 = pool[0:64, :]

    with nc.Block() as blk:
        @blk.sync
        def _(eng):
            for name, dst, srcp in [("c", ones, onesf), ("c", cst, cstf), ("c", repa, repaf),
                                    ("c", tril, trilf), ("c", repi, repif), ("c", repd, repdf),
                                    ("c", trius, triusf), ("c", maskl, maskls), ("c", ddc, ddf),
                                    ("c", dtb_c, dtb), ("c", alog_c, alog)]:
                pr.inc("d_c", eng.dma_start(dst[:], srcp), 16)
            pr.inc("d_c", eng.dma_start(wstg[0:64, 0:64], dtwT), 16)
            pr.inc("d_c", eng.dma_start(selv[:], io["selvf"]), 16)
            pr.inc("d_c", eng.dma_start(dtraw[:], bcdt[0:64, 0:1024]), 16)
            pr.inc("d_c", eng.dma_start(B_fm[:], bcdt[64:192, 0:1024]), 16)
            pr.inc("d_c", eng.dma_start(C_fm[:], bcdt[192:320, 0:1024]), 16)

    with nc.Block() as blk:
        @blk.vector
        def _(eng):
            pr.wait(eng, "d_c")
            pr.chain(eng, "vq", eng.memset(zeros64[:], 0.0))
            pr.chain(eng, "vq", eng.tensor_copy(C_f32[:], C_fm[:]))
            pr.inc("v_pp", eng.tensor_copy(dtw_b[:], wstg[0:64, 0:64]))

        @blk.tensor
        def _(eng):
            pr.wait(eng, "v_pp")
            inst = None
            for tc in range(2):
                inst = eng.matmul(pool[0:64, tc * 512:(tc + 1) * 512], dtw_b[:],
                                  dtraw[:, tc * 512:(tc + 1) * 512], start=True, stop=True)
            pr.inc("pe_pp", inst)

    with nc.Block() as blk:
        @blk.vector
        def _(eng):
            pr.wait(eng, "pe_pp")
            pr.inc("v_pp", eng.tensor_scalar(vbuf, pool[0:64, 0:1024], dtb_c[:, 0:1],
                                             None, ALU.add))

        @blk.scalar
        def _(eng):
            pr.wge(eng, "v_pp", 2)
            pr.chain(eng, "sq2", eng.activation(t1, vbuf, AF.Abs))
            pr.chain(eng, "sq2", eng.activation(t1, t1, AF.Exp, scale=-1.0))
            pr.inc("s_pp", eng.activation(t1, t1, AF.Ln, bias=1.0))
            pr.inc("s_pp", eng.activation(Acol[:], alog_c[:], AF.Exp))

    with nc.Block() as blk:
        @blk.vector
        def _(eng):
            pr.wge(eng, "s_pp", 2)
            pr.chain(eng, "vq", eng.scalar_tensor_tensor(dt[:], vbuf, 0.0, t1,
                                                         ALU.max, ALU.add))
            pr.chain(eng, "vq", eng.tensor_scalar(Acol[:], Acol[:], -1.0, None, ALU.mult))
            pr.chain(eng, "vq", eng.tensor_scalar(vbuf, dt[:], Acol[:, 0:1], None, ALU.mult))
            pr.chain(eng, "vq", eng.tensor_copy(A_bf[:], vbuf))
            pr.chain(eng, "vq", eng.tensor_copy(dt_bf[:], dt[:]))
            for c in range(16):
                sl = slice(c * 64, (c + 1) * 64)
                pr.chain(eng, "vq", eng.tensor_tensor_scan(cs[:, sl], vbuf[:, sl],
                                                           zeros64[:], 0.0, ALU.add, ALU.add))
            for c in range(16):
                sl = slice(c * 64, (c + 1) * 64)
                pr.chain(eng, "vq", eng.tensor_scalar(t1[:, sl], cs[:, sl],
                                                      cs[:, c * 64 + 63:c * 64 + 64], -1.0,
                                                      ALU.subtract, ALU.mult))
            for c in range(16):
                pr.inc("v_lc", eng.tensor_copy(lam[:, c:c + 1], cs[:, c * 64 + 63:c * 64 + 64]))
            pr.inc("v_pp", eng.tensor_copy(A_bf[:, 0:1], A_bf[:, 0:1]))

        @blk.scalar
        def _(eng):
            pr.wge(eng, "v_pp", 3)
            pr.chain(eng, "sq2", eng.activation(t1, t1, AF.Exp))
            pr.inc("s_pp", eng.activation(decay_bf[:], cs[:], AF.Exp))

    with nc.Block() as blk:
        @blk.scalar
        def _(eng):
            pr.wge(eng, "v_lc", 16)
            pr.inc("s_pp", eng.activation(lam[:], lam[:], AF.Exp))

        @blk.vector
        def _(eng):
            pr.wge(eng, "s_pp", 4)
            pr.inc("v_pp", eng.tensor_mul(w_bf[:], t1, dt[:]))

    with nc.Block() as blk:
        @blk.gpsimd
        def _(eng):
            pr.wge(eng, "s_pp", 4)
            pr.inc("d_lam", eng.dma_start(lamr[:], lam[:]), 16)

        @blk.tensor
        def _(eng):
            pr.wait(eng, "d_lam")
            inst = None
            for tc in range(2):
                inst = eng.matmul(pool[:, tc * 512:(tc + 1) * 512], ones[0:1, :],
                                  lamr[0:1, tc * 512:(tc + 1) * 512], start=True, stop=True)
            pr.inc("pe_lam", inst)

    with nc.Block() as blk:
        @blk.vector
        def _(eng):
            pr.wait(eng, "pe_lam")
            pr.inc("v_lam", eng.tensor_copy(Lam[:], pool[:, 0:1024]))

    with nc.Block() as blk:
        @blk.vector
        def _(eng):
            pr.wait(eng, "v_lam")
            pr.inc("p_s0", eng.memset(S_run[:], 0.0))
    with nc.Block() as blk:
        @blk.sync
        def _(eng):
            for z in range(16):
                ev = z % 2 == 0
                Xc = Xz if ev else Xlo
                Bc = Bz if ev else Bzl
                if ev:
                    pr.wge(eng, "p_xw", z)
                    pr.inc("p_dx", eng.dma_start(Xz[:], xfm[:, z * 64:z * 64 + 128],
                                                 transpose=True), 16)
                    pr.wge(eng, "p_st", 2 * z)
                    pr.inc("p_db", eng.dma_start(Bz[:], bcdt[64:192, z * 64:z * 64 + 128],
                                                 transpose=True), 16)

        @blk.gpsimd
        def _(eng):
            for z in range(16):
                ev = z % 2 == 0
                Xc = Xz if ev else Xlo
                Bc = Bz if ev else Bzl
                if not ev:
                    pr.wge(eng, "p_dx", 16 * (z // 2 + 1))
                    pr.wge(eng, "p_xw", z)
                    pr.inc("p_dxl", eng.dma_start(Xlo[:], Xz[64:128, :]), 16)
                    pr.wge(eng, "p_db", 16 * (z // 2 + 1))
                    pr.wge(eng, "p_st", 2 * z)
                    pr.inc("p_dbl", eng.dma_start(Bzl[:], Bz[64:128, :]), 16)

        @blk.tensor
        def _(eng):
            for z in range(16):
                ev = z % 2 == 0
                Xc = Xz if ev else Xlo
                Bc = Bz if ev else Bzl
                if z == 0:
                    pr.wge(eng, "v_pp", 4)
                    pr.wait(eng, "v_lam")
                    pr.wait(eng, "d_c")
                pr.wge(eng, "p_xw", z)
                pr.wge(eng, "p_sc", 64 * z)
                inst = None
                for mc in range(8):
                    ms = slice(mc * 512, (mc + 1) * 512)
                    inst = eng.matmul(L64[:, ms], w_bf[:, z * 64:(z + 1) * 64],
                                      repa[:, ms], start=True, stop=True)
                pr.inc("p_w", inst)
                for hf in range(2):
                    pr.wge(eng, "p_xw", z + 1)
                    if ev:
                        pr.wge(eng, "p_db", 16 * (z // 2 + 1))
                    else:
                        pr.wge(eng, "p_dbl", 16 * (z // 2 + 1))
                    pr.wge(eng, "p_sc", 64 * z + 32 * hf)
                    inst = None
                    for hl in range(32):
                        h = hf * 32 + hl
                        inst = eng.matmul(psSt[:, hl * 64:(hl + 1) * 64], Bc[0:64, :],
                                          Xw[0:64, h * 64:(h + 1) * 64],
                                          start=True, stop=True)
                    pr.inc("p_st", inst)

        @blk.vector
        def _(eng):
            for z in range(16):
                ev = z % 2 == 0
                Xc = Xz if ev else Xlo
                Bc = Bz if ev else Bzl
                pr.wge(eng, "p_w", z + 1)
                if ev:
                    pr.wge(eng, "p_dx", 16 * (z // 2 + 1))
                else:
                    pr.wge(eng, "p_dxl", 16 * (z // 2 + 1))
                pr.wge(eng, "p_st", 2 * z)
                if z == 0:
                    pr.wait(eng, "p_s0")
                pr.inc("p_xw", eng.tensor_mul(Xw[0:64, :], Xc[0:64, :], L64[:, 0:4096]))
                for hf in range(2):
                    pr.wge(eng, "p_st", 2 * z + hf + 1)
                    for hl in range(32):
                        h = hf * 32 + hl
                        pr.inc("p_sc", eng.scalar_tensor_tensor(
                            S_run[:, h * 64:(h + 1) * 64], S_run[:, h * 64:(h + 1) * 64],
                            Lam[:, h * 16 + z:h * 16 + z + 1],
                            psSt[:, hl * 64:(hl + 1) * 64], ALU.mult, ALU.add))

    with nc.Block() as blk:
        @blk.sync
        def _(eng):
            pr.wge(eng, "p_sc", 64 * 16)
            pr.inc("d_sh", eng.dma_start(io["shd"], S_run[:]), 16)

        @blk.gpsimd
        def _(eng):
            pr.wait(eng, "d_sh")
            pr.inc("cc", eng.collective_compute(
                "AllGather", ALU.bypass, [[0, 1, 2, 3, 4, 5, 6, 7]],
                [io["shd"].opt()], [io["gatd"].opt()]))

    with nc.Block() as blk:
        @blk.gpsimd
        def _(eng):
            for r in range(8):
                for q in range(4):
                    pr.wge(eng, "v_sel", 4 * r + q + 1)
                    pr.inc("d_sg", eng.dma_start(
                        scr1[:], io["gatd"][r * 128:(r + 1) * 128, q * 1024:(q + 1) * 1024]), 16)

        @blk.vector
        def _(eng):
            pr.wait(eng, "cc")
            pr.inc("v_sel", eng.memset(S_run[:], 0.0))
            for r in range(8):
                for q in range(4):
                    pr.wge(eng, "d_sg", 16 * (4 * r + q + 1))
                    pr.inc("v_sel", eng.scalar_tensor_tensor(
                        S_run[:, q * 1024:(q + 1) * 1024], scr1[:], selv[:, r:r + 1],
                        S_run[:, q * 1024:(q + 1) * 1024], ALU.mult, ALU.add))

    with nc.Block() as blk:
        @blk.sync
        def _(eng):
            for th in range(2):
                for zl in range(8):
                    z = th * 8 + zl
                    ev = (z % 2 == 0)
                    Xc = Xz if ev else Xlo
                    Bc = Bz if ev else Bzl
                    if ev:
                        if z == 0:
                            pr.wge(eng, "p_xw", 16)
                            pr.wge(eng, "p_st", 32)
                            pr.wge(eng, "v_sel", 33)
                        pr.wge(eng, "v_xw", z)
                        pr.inc("d_xz", eng.dma_start(Xz[:], xfm[:, z * 64:z * 64 + 128],
                                                     transpose=True), 16)
                        pr.wge(eng, "pe_st", 2 * z)
                        pr.inc("d_bz", eng.dma_start(Bz[:], bcdt[64:192, z * 64:z * 64 + 128],
                                                     transpose=True), 16)
                for od in range(16):
                    odg = th * 16 + od
                    for piece in range(2):
                        if odg + piece > 0:
                            pr.wge(eng, "v_wo", 2 * odg + piece)
                        pr.inc("d_wo", eng.dma_start(
                            wstg[:].rearrange("p (kc f) -> p kc f", kc=16),
                            woutT[piece * 2048:(piece + 1) * 2048,
                                  od * 128:(od + 1) * 128].rearrange(
                                      "(kc kp) f -> kp kc f", kp=128)), 16)

        @blk.gpsimd
        def _(eng):
            for th in range(2):
                for zl in range(8):
                    z = th * 8 + zl
                    ev = (z % 2 == 0)
                    Xc = Xz if ev else Xlo
                    Bc = Bz if ev else Bzl
                    if not ev:
                        pr.wge(eng, "d_xz", 16 * (z // 2 + 1))
                        pr.wge(eng, "v_xw", z)
                        pr.inc("d_xlo", eng.dma_start(Xlo[:], Xz[64:128, :]), 16)
                        pr.wge(eng, "d_bz", 16 * (z // 2 + 1))
                        pr.wge(eng, "pe_st", 2 * z)
                        pr.inc("d_blo", eng.dma_start(Bzl[:], Bz[64:128, :]), 16)
                for J in range(32):
                    if th > 0 or J >= 2:
                        pr.wge(eng, "v_yg", (32 * th + J - 1) if J >= 2 else 32 * th)
                    pr.inc("d_zs%d" % (J % 2), eng.dma_start(
                        zstg[:, (J % 2) * 512:(J % 2 + 1) * 512],
                        zsf[J * 128:(J + 1) * 128, th * 512:(th + 1) * 512]), 16)
                    pr.inc("d_xl%d" % (J % 2), eng.dma_start(
                        xflstg[:, (J % 2) * 512:(J % 2 + 1) * 512],
                        xfm[J * 128:(J + 1) * 128, th * 512:(th + 1) * 512]), 16)
                for od in range(16):
                    odg = th * 16 + od
                    pr.wge(eng, "v_oe", odg)
                    if odg == 0:
                        pr.wge(eng, "v_sel", 33)
                    pr.inc("d_xr%d" % (od % 2), eng.dma_start(
                        xstg[:, (od % 2) * 512:(od % 2 + 1) * 512],
                        xf[od * 128:(od + 1) * 128, th * 512:(th + 1) * 512]), 16)
                    if odg > 0:
                        po = (odg - 1) % 16
                        pt = (odg - 1) // 16
                        pr.inc("d_or%d" % (po % 2), eng.dma_start(
                            xo[po * 128:(po + 1) * 128, pt * 512:(pt + 1) * 512],
                            ostg[:, (po % 2) * 512:(po % 2 + 1) * 512]), 16)

        @blk.tensor
        def _(eng):
            for th in range(2):
                for zl in range(8):
                    z = th * 8 + zl
                    ev = (z % 2 == 0)
                    Xc = Xz if ev else Xlo
                    Bc = Bz if ev else Bzl
                    if z == 0:
                        pr.wge(eng, "v_pp", 4)
                        pr.wait(eng, "v_lam")
                        pr.wait(eng, "d_c")
                        pr.wge(eng, "v_sel", 33)
                        pr.wge(eng, "p_xw", 16)
                        pr.wge(eng, "p_sc", 1024)
                    if z == 8:
                        pr.wge(eng, "v_oe", 16)
                    pr.wge(eng, "v_ye", 2 * z)
                    pr.wge(eng, "v_xw", z)
                    inst = None
                    for J in range(32):
                        inst = eng.matmul(psDo[:, J * 64:(J + 1) * 64],
                                          repd[:, J * 128:(J + 1) * 128],
                                          decay_bf[:, z * 64:(z + 1) * 64], start=True, stop=True)
                    pr.inc("pe_do", inst)
                    pr.wge(eng, "v_do", z + 1)
                    pr.wge(eng, "v_sc", 64 * z)
                    inst = None
                    for mc in range(8):
                        ms = slice(mc * 512, (mc + 1) * 512)
                        inst = eng.matmul(L64[:, ms], A_bf[:, z * 64:(z + 1) * 64],
                                          repa[:, ms], start=True, stop=True)
                    pr.inc("pe_a", inst)
                    pr.wge(eng, "v_ra", z + 1)
                    inst = None
                    for mc in range(8):
                        ms = slice(mc * 512, (mc + 1) * 512)
                        inst = eng.matmul(L64[:, ms], dt_bf[:, z * 64:(z + 1) * 64],
                                          repa[:, ms], start=True, stop=True)
                    pr.inc("pe_b", inst)
                    pr.wge(eng, "v_xd", z + 1)
                    inst = None
                    for mc in range(8):
                        ms = slice(mc * 512, (mc + 1) * 512)
                        inst = eng.matmul(L64[:, ms], trius[:], rhsA[0:64, ms],
                                          start=True, stop=True)
                    pr.inc("pe_ds", inst)
                    pr.wge(eng, "s_ge", z + 1)
                    pr.inc("pe_s1", eng.matmul(L64[:, 0:64], C_fm[:, z * 64:(z + 1) * 64],
                                               B_fm[:, z * 64:(z + 1) * 64], start=True, stop=True))
                    pr.wge(eng, "v_sm", z + 1)
                    inst = None
                    for mc in range(8):
                        ms = slice(mc * 512, (mc + 1) * 512)
                        inst = eng.matmul(L64[:, ms], S_m[0:64, :], repi[:, ms],
                                          start=True, stop=True)
                    pr.inc("pe_rs", inst)
                    pr.wge(eng, "v_gf", z + 1)
                    inst = None
                    for mc in range(8):
                        ms = slice(mc * 512, (mc + 1) * 512)
                        inst = eng.matmul(L64[:, ms], w_bf[:, z * 64:(z + 1) * 64],
                                          repa[:, ms], start=True, stop=True)
                    pr.inc("pe_w", inst)
                    for hf in range(2):
                        pr.wge(eng, "v_xw", z + 1)
                        pr.wge(eng, "v_do", z + 1)
                        pr.wge(eng, "v_ye", 2 * z + hf)
                        pr.wge(eng, "v_sc", 64 * z)
                        if zl == 0 and hf == 0 and th == 1:
                            pr.wge(eng, "pe_op", 16)
                            pr.wge(eng, "v_yg", 32)
                        inst = None
                        for j in range(16):
                            J = hf * 16 + j
                            inst = eng.matmul(psYd[0:64, j * 64:(j + 1) * 64],
                                              Xdt[0:64, (2 * J) * 64:(2 * J + 1) * 64],
                                              Gtmp[0:64, (2 * J) * 64:(2 * J + 1) * 64],
                                              start=True, stop=True)
                            inst = eng.matmul(psYd[64:128, j * 64:(j + 1) * 64],
                                              Xdt[0:64, (2 * J + 1) * 64:(2 * J + 2) * 64],
                                              Gtmp[0:64, (2 * J + 1) * 64:(2 * J + 2) * 64],
                                              start=True, stop=True, tile_position=(0, 64))
                        pr.inc("pe_yd", inst)
                        inst = None
                        for j in range(16):
                            J = hf * 16 + j
                            inst = eng.matmul(psYo[:, j * 64:(j + 1) * 64],
                                              S_run[:, J * 128:(J + 1) * 128],
                                              C_f32[:, z * 64:(z + 1) * 64], start=True, stop=True)
                        pr.inc("pe_yo", inst)
                        if ev:
                            pr.wge(eng, "d_bz", 16 * (z // 2 + 1))
                        else:
                            pr.wge(eng, "d_blo", 16 * (z // 2 + 1))
                        pr.wge(eng, "v_sc", 64 * z + 32 * hf)
                        inst = None
                        for hl in range(32):
                            h = hf * 32 + hl
                            inst = eng.matmul(psSt[:, hl * 64:(hl + 1) * 64], Bc[0:64, :],
                                              Xw[0:64, h * 64:(h + 1) * 64],
                                              start=True, stop=True)
                        pr.inc("pe_st", inst)
                for od in range(16):
                    odg = th * 16 + od
                    pr.wge(eng, "v_wo", 2 * odg + 2)
                    if od == 0:
                        pr.wge(eng, "v_yg", 32 * (th + 1))
                        pr.wge(eng, "v_sc", 64 * 8 * (th + 1))
                    pr.wge(eng, "v_oe", odg)
                    inst = None
                    for fc in range(32):
                        inst = eng.matmul(psOP[:],
                                          wout_b[:, fc * 128:(fc + 1) * 128],
                                          Y[:, fc * 512:(fc + 1) * 512],
                                          start=(fc == 0), stop=(fc == 31))
                    pr.inc("pe_op", inst)

        @blk.scalar
        def _(eng):
            for th in range(2):
                for zl in range(8):
                    z = th * 8 + zl
                    ev = (z % 2 == 0)
                    Xc = Xz if ev else Xlo
                    Bc = Bz if ev else Bzl
                    pr.wge(eng, "pe_ds", z + 1)
                    pr.wge(eng, "v_gf", max(0, z - 1))
                    pr.inc("s_ge", eng.activation(Gbuf[0:64, :], L64[:, 0:4096], AF.Exp))

        @blk.vector
        def _(eng):
            for th in range(2):
                for zl in range(8):
                    z = th * 8 + zl
                    ev = (z % 2 == 0)
                    Xc = Xz if ev else Xlo
                    Bc = Bz if ev else Bzl
                    pr.wge(eng, "pe_do", z + 1)
                    pr.inc("v_do", eng.tensor_copy(dorep[:], psDo[:]))
                    pr.wge(eng, "pe_a", z + 1)
                    pr.wge(eng, "pe_ds", max(0, z - 1) if z > 0 else 0)
                    pr.inc("v_ra", eng.tensor_mul(rhsA[0:64, :], L64[:, 0:4096], tril[:]))
                    pr.wge(eng, "pe_b", z + 1)
                    if ev:
                        pr.wge(eng, "d_xz", 16 * (z // 2 + 1))
                    else:
                        pr.wge(eng, "d_xlo", 16 * (z // 2 + 1))
                    pr.wge(eng, "pe_yd", max(0, 2 * (z - 1)))
                    pr.inc("v_xd", eng.tensor_mul(Xdt[0:64, :], Xc[0:64, :], L64[:, 0:4096]))
                    pr.wge(eng, "pe_s1", z + 1)
                    pr.wge(eng, "pe_rs", max(0, z - 1))
                    pr.inc("v_sm", eng.tensor_mul(S_m[0:64, :], L64[:, 0:64], maskl[:]))
                    pr.wge(eng, "pe_rs", z + 1)
                    pr.wge(eng, "pe_yd", max(0, 2 * (z - 1)))
                    pr.inc("v_gf", eng.tensor_mul(Gtmp[0:64, :], Gbuf[0:64, :], L64[:, 0:4096]))
                    pr.wge(eng, "pe_w", z + 1)
                    pr.wge(eng, "pe_st", max(0, 2 * (z - 1)))
                    pr.inc("v_xw", eng.tensor_mul(Xw[0:64, :], Xc[0:64, :], L64[:, 0:4096]))
                    for hf in range(2):
                        pr.wge(eng, "pe_yd", 2 * z + hf + 1)
                        pr.wge(eng, "pe_yo", 2 * z + hf + 1)
                        pr.chain(eng, "vq", eng.tensor_mul(tmpv[:], psYo[:],
                                                           dorep[:, hf * 1024:(hf + 1) * 1024]))
                        pr.chain(eng, "vq", eng.tensor_add(ycat[:], tmpv[:], psYd[:]))
                        inst = None
                        for j in range(16):
                            J = hf * 16 + j
                            inst = eng.tensor_copy(Y[:, J * 512 + zl * 64:J * 512 + (zl + 1) * 64],
                                                   ycat[:, j * 64:(j + 1) * 64])
                        pr.inc("v_ye", inst)
                        pr.wge(eng, "pe_st", 2 * z + hf + 1)
                        for hl in range(32):
                            h = hf * 32 + hl
                            pr.inc("v_sc", eng.scalar_tensor_tensor(
                                S_run[:, h * 64:(h + 1) * 64], S_run[:, h * 64:(h + 1) * 64],
                                Lam[:, h * 16 + z:h * 16 + z + 1],
                                psSt[:, hl * 64:(hl + 1) * 64], ALU.mult, ALU.add))
                for J in range(32):
                    pr.wge(eng, "d_zs%d" % (J % 2), 16 * (th * 16 + J // 2 + 1))
                    pr.wge(eng, "d_xl%d" % (J % 2), 16 * (th * 16 + J // 2 + 1))
                    pr.wge(eng, "v_ye", 16 * (th + 1))
                    pr.chain(eng, "vq", eng.scalar_tensor_tensor(
                    Y[:, J * 512:(J + 1) * 512],
                    xflstg[:, (J % 2) * 512:(J % 2 + 1) * 512],
                    ddc[:, J:J + 1], Y[:, J * 512:(J + 1) * 512],
                    ALU.mult, ALU.add))
                    pr.inc("v_yg", eng.tensor_mul(Y[:, J * 512:(J + 1) * 512],
                                              Y[:, J * 512:(J + 1) * 512],
                                              zstg[:, (J % 2) * 512:(J % 2 + 1) * 512]))
                for od in range(16):
                    odg = th * 16 + od
                    for piece in range(2):
                        pr.wge(eng, "d_wo", 16 * (2 * odg + piece + 1))
                        if piece == 0 and odg > 0:
                            pr.wge(eng, "pe_op", odg)
                        pr.inc("v_wo", eng.tensor_copy(
                            wout_b[:, piece * 2048:(piece + 1) * 2048], wstg[:]))
                    pr.wge(eng, "pe_op", odg + 1)
                    pr.wge(eng, "d_xr%d" % (od % 2), 16 * (th * 8 + od // 2 + 1))
                    if odg >= 2:
                        pr.wge(eng, "d_or%d" % (od % 2), 16 * (th * 8 + od // 2))
                    pr.inc("v_oe", eng.tensor_add(
                        ostg[:, (od % 2) * 512:(od % 2 + 1) * 512],
                        xstg[:, (od % 2) * 512:(od % 2 + 1) * 512], psOP[:]))

    with nc.Block() as blk:
        @blk.gpsimd
        def _(eng):
            pr.wait(eng, "v_oe")
            pr.inc("d_or1", eng.dma_start(xo[15 * 128:16 * 128, 512:1024],
                                          ostg[:, 512:1024]), 16)

        @blk.sync
        def _(eng):
            pr.wait(eng, "d_or0")
            pr.wait(eng, "d_or1")
    ar.free()


def _shard_x(x):
    outs = []
    for c in range(NCORES):
        seq, half = c // 2, c % 2
        t0 = half * TOK
        blk = np.ascontiguousarray(x[seq, t0:t0 + TOK, :].T)
        if half == 0:
            halo = np.zeros((D, 3), np.float32)
        else:
            halo = np.ascontiguousarray(x[seq, t0 - 3:t0, :].T)
        outs.append((blk, halo))
    return outs


_ONES = np.ones((128, 128), np.float32)
_CST = np.zeros((128, 8), np.float32)
_CST[:, 0] = LN_EPS


def _ffn_maps(inputs, xf_percore):
    wgT = np.ascontiguousarray(np.asarray(inputs["L0_ffn_gate_w"], np.int32).T)
    wuT = np.ascontiguousarray(np.asarray(inputs["L0_ffn_up_w"], np.int32).T)
    wdT = np.ascontiguousarray(np.asarray(inputs["L0_ffn_down_w"], np.int32).T)
    gsc = np.ascontiguousarray(np.asarray(inputs["L0_ffn_gate_scale"], np.float32).reshape(64, 128).T)
    usc = np.ascontiguousarray(np.asarray(inputs["L0_ffn_up_scale"], np.float32).reshape(64, 128).T)
    dsc = np.ascontiguousarray(np.asarray(inputs["L0_ffn_down_scale"], np.float32).reshape(16, 128).T)
    return [dict(xf=xf_percore[c], wgT=wgT, wuT=wuT, wdT=wdT,
                 gsc=gsc, usc=usc, dsc=dsc, onesf=_ONES, cstf=_CST) for c in range(NCORES)]


_ONESB = np.ones((128, 128), BF)
_IDENT = np.eye(128, dtype=np.float32)


def _attn_maps(inputs, xf_percore):
    kv_k = np.asarray(inputs["L1_attn_kv_k"], np.float32).reshape(4096, 16, 128)
    kv_v = np.asarray(inputs["L1_attn_kv_v"], np.float32).reshape(4096, 16, 128)
    kTf = np.ascontiguousarray(kv_k.transpose(1, 2, 0))
    vTf = np.ascontiguousarray(kv_v.transpose(1, 0, 2))
    wqT = np.ascontiguousarray(np.asarray(inputs["L1_attn_Wq"], np.float32).T)
    woT = np.ascontiguousarray(np.asarray(inputs["L1_attn_Wo"], np.float32).T)
    return [dict(xf=xf_percore[c], kTf=kTf, vTf=vTf, wqT=wqT, woT=woT,
                 onesf=_ONES, onesb=_ONESB, cstf=_CST) for c in range(NCORES)]


def build_fused():
    pr = Pr()
    nc = pr.nc
    F = F32
    pool = nc.alloc_psum_tensor("pool", [128, 4096], F32)
    fincell = nc.alloc_sbuf_tensor("fincell", [1, 2], F32)

    def ext(n, shp, dty):
        return nc.dram_tensor(n, shp, dty, kind="ExternalInput").ap()

    def internal(n, shp, dty):
        return nc.dram_tensor(n, shp, dty).ap()

    Da = dict(xf=ext("a_xf", [D, TOKH], F), winT=ext("a_winT", [D, 8192], F),
              cwf=ext("a_cwf", [128, 128], F), cbf=ext("a_cbf", [128, 32], F),
              wxT=ext("a_wxT", [4096, 4416], F), onesf=ext("a_onesf", [128, 128], F),
              cstf=ext("a_cstf", [128, 8], F))
    Db = dict(xf=ext("b_xf", [D, TOK], F), dtwT=ext("b_dtwT", [64, 64], F),
              dtb=ext("b_dtb", [64, 1], F), alog=ext("b_alog", [64, 1], F),
              ddf=ext("b_ddf", [128, 32], F), woutT=ext("b_woutT", [4096, D], F),
              selvf=ext("b_selvf", [128, 8], F), onesf=ext("b_onesf", [128, 128], F),
              repaf=ext("b_repaf", [64, 4096], BF16), trilf=ext("b_trilf", [64, 4096], BF16),
              repif=ext("b_repif", [64, 4096], BF16), repdf=ext("b_repdf", [64, 4096], BF16),
              triusf=ext("b_triusf", [64, 64], BF16), maskls=ext("b_maskls", [64, 64], BF16),
              cstf=ext("b_cstf", [128, 8], F))
    Df = dict(wgT=ext("f_wgT", [D, 8192], I32), wuT=ext("f_wuT", [D, 8192], I32),
              wdT=ext("f_wdT", [8192, D], I32), gsc=ext("f_gsc", [128, 64], F),
              usc=ext("f_usc", [128, 64], F), dsc=ext("f_dsc", [128, 16], F),
              onesf=ext("f_onesf", [128, 128], F), cstf=ext("f_cstf", [128, 8], F))
    Dt = dict(kTf=ext("t_kTf", [16, 128, 4096], F), vTf=ext("t_vTf", [16, 4096, 128], F),
              wqT=ext("t_wqT", [D, D], F), woT=ext("t_woT", [D, D], F),
              onesf=ext("t_onesf", [128, 128], F), onesb=ext("t_onesb", [128, 128], BF16),
              cstf=ext("t_cstf", [128, 8], F))
    Dm = dict(gwr=ext("m_gwr", [128, 128], F), weg=ext("m_weg", [8, D, 1408], F),
              weu=ext("m_weu", [8, D, 1408], F), wed=ext("m_wed", [8, 1408, D], F),
              swg=ext("m_swg", [D, 2816], F), swu=ext("m_swu", [D, 2816], F),
              swd=ext("m_swd", [2816, D], F), onesf=ext("m_onesf", [128, 128], F),
              identf=ext("m_identf", [128, 128], F), cstf=ext("m_cstf", [128, 8], F))
    out_xo = nc.dram_tensor("out_xo", [D, TOK], F32, kind="ExternalOutput").ap()

    xfm = internal("i_xfm", [4096, TOK], BF16)
    bcdt = internal("i_bcdt", [320, TOK], BF16)
    zsf = internal("i_zsf", [4096, TOK], BF16)
    x1 = internal("i_x1", [D, TOK], F32)
    x2 = internal("i_x2", [D, TOK], F32)
    x3 = internal("i_x3", [D, TOK], F32)
    shd = internal("i_shd", [128, 4096], F32)
    gatd = internal("i_gatd", [1024, 4096], F32)

    Da.update(xfm=xfm, bcdt=bcdt, zsf=zsf)
    Db.update(xfm=xfm, bcdt=bcdt, zsf=zsf, xo=x1, shd=shd, gatd=gatd)
    Df.update(xf=x1, xo=x2)
    Dt.update(xf=x2, xo=x3)
    Dm.update(xf=x3, xo=out_xo)
    for Dx in (Da, Db, Df, Dt, Dm):
        Dx["pool"] = pool

    with nc.Block() as blk:
        @blk.vector
        def _(eng):
            pr.prefix = ""
            pr.inc("fz", eng.memset(fincell[:], 0.0))
    with nc.Block() as blk:
        @blk.vector
        def _(eng):
            pr.wait(eng, "/fz")

    with nc.named_scope("L0A"):
        emit_l0a(pr, Da)
        pr.barrier(fincell)
    pr.release_prefix()
    with nc.named_scope("L0B"):
        emit_l0b(pr, Db)
        pr.barrier(fincell)
    pr.release_prefix()
    with nc.named_scope("L0F"):
        emit_l0ffn(pr, Df)
        pr.barrier(fincell)
    pr.release_prefix()
    with nc.named_scope("L1A"):
        emit_l1attn(pr, Dt)
        pr.barrier(fincell)
    pr.release_prefix()
    with nc.named_scope("L1F"):
        emit_l1ffn(pr, Dm)
        pr.barrier(fincell)
    build_fused.last_pr = pr
    return nc


def _fused_maps(inputs):
    x = np.asarray(inputs["L0_attn_x"], np.float32)
    shards = _shard_x(x)
    xf_percore = [s[0] for s in shards]
    maps_a = _l0a_maps(inputs, x)
    maps_b = _l0b_maps(inputs, xf_percore)
    maps_f = _ffn_maps(inputs, [None] * NCORES)
    maps_t = _attn_maps(inputs, [None] * NCORES)
    maps_m = _moe_maps(inputs, [None] * NCORES)
    in_maps = []
    for c in range(NCORES):
        m = {}
        for pfx, mm in (("a_", maps_a[c]), ("b_", maps_b[c]), ("f_", maps_f[c]),
                        ("t_", maps_t[c]), ("m_", maps_m[c])):
            for k, v in mm.items():
                if v is None or (k == "xf" and pfx in ("f_", "t_", "m_")):
                    continue
                m[pfx + k] = v
        in_maps.append(m)
    return in_maps


def kernel(**inputs):
    in_maps = _fused_maps(inputs)
    nc = build_fused()
    res = run_bass_kernel_spmd(nc, in_maps, core_ids=list(range(NCORES)))
    outs = [np.asarray(res.results[c]["out_xo"], np.float32).T for c in range(NCORES)]
    y = np.stack([np.concatenate([outs[2 * s], outs[2 * s + 1]], axis=0) for s in range(4)])
    return y.astype(np.float32)


# version 05 (final)
# speedup vs baseline: 1.4798x; 1.1245x over previous
import numpy as np
import ml_dtypes
import concourse.bass as bass
import concourse.mybir as mybir
from concourse.bass_utils import run_bass_kernel_spmd

F32 = mybir.dt.float32
BF16 = mybir.dt.bfloat16
I32 = mybir.dt.int32
AF = mybir.ActivationFunctionType
ALU = mybir.AluOpType
AX = mybir.AxisListType
BF = ml_dtypes.bfloat16

NCORES = 8
TOK = 1024
D = 2048
NF = D // 128
LN_EPS = 1e-5


class Pr:

    def __init__(self):
        self.nc = bass.Bass("TRN2", target_bir_lowering=False, debug=False)
        self.map = {}
        self.cnt = {}
        self.free = []
        self.prefix = ""

    def _resolve(self, name):
        return name[1:] if name.startswith("/") else self.prefix + name

    def _virgin(self, name):
        if name.endswith("cc"):
            return True
        return (name.startswith("a_") or name.startswith("b_")) and (
            "d_" in name or "p_d" in name)

    def sem(self, name):
        name = self._resolve(name)
        if name not in self.map:
            if self.free and not self._virgin(name):
                h, base = self.free.pop()
            else:
                h, base = self.nc.alloc_semaphore(name), 0
            self.map[name] = (h, base)
            self.cnt[name] = 0
        return self.map[name][0]

    def wge(self, eng, name, v):
        h = self.sem(name)
        base = self.map[self._resolve(name)][1]
        eng.wait_ge(h, base + v)

    def inc(self, name, inst, n=1):
        inst.then_inc(self.sem(name), n)
        self.cnt[self._resolve(name)] += n
        return inst

    def wait(self, eng, name, val=None):
        n = self._resolve(name)
        self.sem(name)
        v = self.cnt[n] if val is None else val
        eng.wait_ge(self.map[n][0], self.map[n][1] + v)

    def val(self, name):
        self.sem(name)
        return self.cnt[self._resolve(name)]

    def chain(self, eng, name, inst):
        self.inc(name, inst)
        self.wait(eng, name)

    def release_prefix(self):
        pfx = self.prefix
        for n in [k for k in self.map if k.startswith(pfx) and pfx]:
            h, base = self.map.pop(n)
            c = self.cnt.pop(n)
            if not self._virgin(n):
                self.free.append((h, base + c))

    def barrier(self, fincell):
        nc = self.nc
        pfx = self.prefix
        names = [n for n, c in self.cnt.items() if n.startswith(pfx) and c > 0]
        with nc.Block() as blk:
            @blk.vector
            def _(eng):
                for n in names:
                    eng.wait_ge(self.map[n][0], self.map[n][1] + self.cnt[n])
                self.inc("/fin", eng.tensor_copy(fincell[0:1, 0:1], fincell[0:1, 1:2]))
        fh, fb = self.map["fin"]
        fv = fb + self.cnt["fin"]
        with nc.Block() as blk:
            @blk.sync
            def _(eng):
                eng.wait_ge(fh, fv)
            @blk.gpsimd
            def _(eng):
                eng.wait_ge(fh, fv)
            @blk.scalar
            def _(eng):
                eng.wait_ge(fh, fv)
            @blk.tensor
            def _(eng):
                eng.wait_ge(fh, fv)


def emit_ln(pr, tag, xres, xln, ones_f, sq, row_m, row_v, ps_sum, ps_sq, ps_bc,
            ncols, waits, eps_ap):
    nc = pr.nc
    nch = (ncols + 511) // 512
    s_sq = tag + "sq"
    s_pe = tag + "pe"
    s_st = tag + "st"
    s_dv = tag + "dv"
    s_rt = tag + "rt"
    s_bc = tag + "bc"
    s_dn = tag + "dn"
    b_sq = pr.val(s_sq)
    b_pe = pr.val(s_pe)

    with nc.Block() as blk:
        @blk.scalar
        def _(eng):
            for s, v in waits:
                pr.wge(eng, s, v)
            for j in range(NF):
                half = j % 2
                if j >= 2:
                    pr.wge(eng, s_pe, b_pe + j - 1)
                pr.inc(s_sq, eng.activation(sq[:, half * ncols:(half + 1) * ncols],
                                            xres[:, j * ncols:(j + 1) * ncols], AF.Square))

        @blk.tensor
        def _(eng):
            for s, v in waits:
                pr.wge(eng, s, v)
            for j in range(NF):
                half = j % 2
                xt = xres[:, j * ncols:(j + 1) * ncols]
                for c in range(nch):
                    c0, c1 = c * 512, min((c + 1) * 512, ncols)
                    eng.matmul(ps_sum[0:1, c0:c1], ones_f[:, 0:1], xt[:, c0:c1],
                               start=(j == 0), stop=(j == NF - 1))
                pr.wge(eng, s_sq, b_sq + j + 1)
                sqt = sq[:, half * ncols:(half + 1) * ncols]
                inst = None
                for c in range(nch):
                    c0, c1 = c * 512, min((c + 1) * 512, ncols)
                    inst = eng.matmul(ps_sq[0:1, c0:c1], ones_f[:, 0:1], sqt[:, c0:c1],
                                      start=(j == 0), stop=(j == NF - 1))
                pr.inc(s_pe, inst)

    with nc.Block() as blk:
        @blk.vector
        def _(eng):
            pr.wge(eng, s_pe, b_pe + NF)
            pr.chain(eng, "vq", eng.tensor_scalar(row_m[0:1, 0:ncols], ps_sum[0:1, 0:ncols], 1.0 / D, None, ALU.mult))
            pr.chain(eng, "vq", eng.tensor_scalar(row_v[0:1, 0:ncols], ps_sq[0:1, 0:ncols], 1.0 / D, None, ALU.mult))
            pr.chain(eng, "vq", eng.tensor_mul(sq[0:1, 0:ncols], row_m[0:1, 0:ncols], row_m[0:1, 0:ncols]))
            pr.inc(s_dv, eng.tensor_sub(row_v[0:1, 0:ncols], row_v[0:1, 0:ncols],
                                        sq[0:1, 0:ncols]))

        @blk.scalar
        def _(eng):
            pr.wait(eng, s_dv)
            pr.inc(s_rt, eng.activation(row_v[0:1, 0:ncols], row_v[0:1, 0:ncols],
                                        AF.Sqrt, bias=eps_ap))

    with nc.Block() as blk:
        @blk.vector
        def _(eng):
            pr.wait(eng, s_rt)
            pr.inc(s_rt, eng.reciprocal(row_v[0:1, 0:ncols], row_v[0:1, 0:ncols]))

        @blk.tensor
        def _(eng):
            pr.wait(eng, s_rt)
            inst = None
            for c in range(nch):
                c0, c1 = c * 512, min((c + 1) * 512, ncols)
                eng.matmul(ps_bc[:, c0:c1], ones_f[0:1, :], row_m[0:1, c0:c1],
                           start=True, stop=True)
                inst = eng.matmul(ps_bc[:, ncols + c0:ncols + c1], ones_f[0:1, :],
                                  row_v[0:1, c0:c1], start=True, stop=True)
            pr.inc(s_bc, inst)

    with nc.Block() as blk:
        @blk.vector
        def _(eng):
            pr.wait(eng, s_bc)
            inst = None
            for j in range(NF):
                xt = xres[:, j * ncols:(j + 1) * ncols]
                ot = xln[:, j * ncols:(j + 1) * ncols]
                tmp = sq[:, (j % 2) * ncols:((j % 2) + 1) * ncols]
                pr.chain(eng, "vq", eng.tensor_sub(tmp, xt, ps_bc[:, 0:ncols]))
                inst = eng.tensor_mul(ot, tmp, ps_bc[:, ncols:2 * ncols])
                if j + 2 < NF:
                    pr.chain(eng, "vq", inst)
            pr.inc(s_dn, inst)


def emit_l0ffn(pr, io):
    nc = pr.nc
    pr.prefix = "f_"
    ar = nc.auto_arena()
    for _n in ['act_g', 'd_c', 'd_out', 'd_wd0', 'd_wd1', 'd_wg0', 'd_wg1', 'd_wu0', 'd_wu1', 'd_x', 'deq', 'dequ', 'deq_d', 'dve_d', 'dve_gu', 'ln0bc', 'ln0dn', 'ln0dv', 'ln0pe', 'ln0rt', 'ln0sq', 'ln0st', 'pe_d', 'pe_g', 'pe_gu', 'vq']:
        pr.sem(_n)
    xf = io["xf"]
    wgT = io["wgT"]
    wuT = io["wuT"]
    wdT = io["wdT"]
    gsc = io["gsc"]
    usc = io["usc"]
    dsc = io["dsc"]
    onesf = io["onesf"]
    cstf = io["cstf"]
    xo = io["xo"]

    xres = ar.sbuf_tensor("f_xres", [128, NF * TOK], F32)
    xln = ar.sbuf_tensor("f_xln", [128, NF * TOK], BF16)
    sq = ar.sbuf_tensor("f_sq", [128, 2 * TOK], F32)
    row_m = ar.sbuf_tensor("f_row_m", [1, TOK], F32)
    row_v = ar.sbuf_tensor("f_row_v", [1, TOK], F32)
    ones = ar.sbuf_tensor("f_ones", [128, 128], F32)
    cst = ar.sbuf_tensor("f_cst", [128, 8], F32)
    g_sc = ar.sbuf_tensor("f_g_sc", [128, 64], F32)
    u_sc = ar.sbuf_tensor("f_u_sc", [128, 64], F32)
    d_sc = ar.sbuf_tensor("f_d_sc", [128, 16], F32)
    FB = 1024
    NFB = 8192 // FB
    FT = FB // 128
    h = ar.sbuf_tensor("f_h", [128, FT * TOK], BF16)
    hg = ar.sbuf_tensor("f_hg", [128, TOK], BF16)
    wg_iA = ar.sbuf_tensor("f_wg_iA", [128, 2048], I32)
    wg_iB = ar.sbuf_tensor("f_wg_iB", [128, 2048], I32)
    wu_iA = ar.sbuf_tensor("f_wu_iA", [128, 2048], I32)
    wu_iB = ar.sbuf_tensor("f_wu_iB", [128, 2048], I32)
    wg_bA = ar.sbuf_tensor("f_wg_bA", [128, 2048], BF16)
    wg_bB = ar.sbuf_tensor("f_wg_bB", [128, 2048], BF16)
    wu_b = ar.sbuf_tensor("f_wu_b", [128, 2048], BF16)
    wd_i = ar.sbuf_tensor("f_wd_i", [128, 2048], I32)
    wd_blk = ar.sbuf_tensor("f_wd_blk", [128, FT * 1024], BF16)

    ar.allocate()
    pool = io["pool"]
    ps_sum = pool[0:1, 0:TOK]
    ps_sq = pool[0:1, TOK:2 * TOK]
    ps_bc = pool[:, 2 * TOK:4 * TOK]
    ps_g = pool[:, 0:TOK]
    ps_u = pool[:, TOK:2 * TOK]

    with nc.Block() as blk:
        @blk.sync
        def _(eng):
            for j in range(NF):
                pr.inc("d_x", eng.dma_start(xres[:, j * TOK:(j + 1) * TOK],
                                            xf[j * 128:(j + 1) * 128, :]), 16)
            pr.inc("d_c", eng.dma_start(ones[:], onesf), 16)
            pr.inc("d_c", eng.dma_start(cst[:], cstf), 16)
            pr.inc("d_c", eng.dma_start(g_sc[:], gsc), 16)
            pr.inc("d_c", eng.dma_start(u_sc[:], usc), 16)
            pr.inc("d_c", eng.dma_start(d_sc[:], dsc), 16)

    emit_ln(pr, "ln0", xres[:], xln[:], ones[:], sq[:], row_m[:], row_v[:],
            ps_sum, ps_sq, ps_bc, TOK,
            [("d_x", pr.val("d_x")), ("d_c", pr.val("d_c"))], cst[0:1, 0:1])

    wg_is = [wg_iA, wg_iB]
    wu_is = [wu_iA, wu_iB]
    wg_bs = [wg_bA, wg_bB]
    NT = NFB * FT

    with nc.Block() as blk:
        @blk.sync
        def _(eng):
            for fb in range(NFB):
                for j in range(FT):
                    ft = fb * FT + j
                    p = ft % 2
                    if ft >= 2:
                        pr.wge(eng, "deq", ft - 1)
                    pr.inc("d_wg0" if p == 0 else "d_wg1", eng.dma_start(
                        wg_is[p][:].rearrange("p (kc f) -> p kc f", kc=16),
                        wgT[:, ft * 128:(ft + 1) * 128].rearrange("(kc kp) f -> kp kc f", kp=128)), 16)
                    if ft >= 2:
                        pr.wge(eng, "dequ", ft - 1)
                    pr.inc("d_wu0" if p == 0 else "d_wu1", eng.dma_start(
                        wu_is[p][:].rearrange("p (kc f) -> p kc f", kc=16),
                        wuT[:, ft * 128:(ft + 1) * 128].rearrange("(kc kp) f -> kp kc f", kp=128)), 16)
                for dh in range(2):
                    for fc in range(FT):
                        g = 16 * fb + 8 * dh + fc
                        q = g % 2
                        if g >= 2:
                            pr.wge(eng, "deq_d", g - 1)
                        pr.inc("d_wd0" if q == 0 else "d_wd1", eng.dma_start(
                            wd_i[:, q * 1024:(q + 1) * 1024],
                            wdT[(fb * FT + fc) * 128:(fb * FT + fc + 1) * 128,
                                dh * 1024:(dh + 1) * 1024]), 16)

        @blk.scalar
        def _(eng):
            for fb in range(NFB):
                for j in range(FT):
                    ft = fb * FT + j
                    if j == 0:
                        p0 = ft % 2
                        pr.wge(eng, "d_wg0" if p0 == 0 else "d_wg1",
                               16 * (ft // 2 + 1))
                        if ft > 0:
                            pr.wge(eng, "pe_g", ft - 1)
                        pr.inc("deq", eng.activation(wg_bs[p0][:], wg_is[p0][:],
                                                     AF.Copy, bias=-128.0))
                    if j + 1 < FT:
                        ft1 = ft + 1
                        p1 = ft1 % 2
                        pr.wge(eng, "d_wg0" if p1 == 0 else "d_wg1",
                               16 * (ft1 // 2 + 1))
                        pr.wge(eng, "pe_g", ft)
                        pr.inc("deq", eng.activation(wg_bs[p1][:], wg_is[p1][:],
                                                     AF.Copy, bias=-128.0))
                    p = ft % 2
                    pr.wge(eng, "d_wu0" if p == 0 else "d_wu1", 16 * (ft // 2 + 1))
                    pr.wge(eng, "pe_gu", ft)
                    pr.inc("dequ", eng.activation(wu_b[:], wu_is[p][:],
                                                  AF.Copy, bias=-128.0))
                    pr.wge(eng, "pe_g", ft + 1)
                    pr.wge(eng, "dve_gu", ft)
                    pr.inc("act_g", eng.activation(hg[:], ps_g[:], AF.Silu,
                                                   scale=g_sc[:, ft:ft + 1]))
                for dh in range(2):
                    for fc in range(FT):
                        g = 16 * fb + 8 * dh + fc
                        q = g % 2
                        pr.wge(eng, "d_wd0" if q == 0 else "d_wd1", 16 * (g // 2 + 1))
                        if fc == 0:
                            pr.wge(eng, "pe_d", 4 * fb + 2 * dh)
                        pr.inc("deq_d", eng.activation(
                            wd_blk[:, fc * 1024:(fc + 1) * 1024],
                            wd_i[:, q * 1024:(q + 1) * 1024], AF.Copy, bias=-128.0))

        @blk.tensor
        def _(eng):
            for fb in range(NFB):
                for j in range(FT):
                    ft = fb * FT + j
                    p = ft % 2
                    pr.wge(eng, "deq", ft + 1)
                    if ft == 0:
                        pr.wait(eng, "ln0dn")
                    if j == 0 and fb > 0:
                        pr.wge(eng, "dve_d", 16 * fb)
                    pr.wge(eng, "act_g", ft)
                    pr.wge(eng, "dve_gu", ft)
                    inst = None
                    for tc in range(2):
                        ts = slice(tc * 512, (tc + 1) * 512)
                        for kc in range(16):
                            inst = eng.matmul(ps_g[:, ts], wg_bs[p][:, kc * 128:(kc + 1) * 128],
                                       xln[:, kc * TOK + tc * 512: kc * TOK + (tc + 1) * 512],
                                       start=(kc == 0), stop=(kc == 15))
                    pr.inc("pe_g", inst)
                    pr.wge(eng, "dequ", ft + 1)
                    for tc in range(2):
                        ts = slice(tc * 512, (tc + 1) * 512)
                        for kc in range(16):
                            inst = eng.matmul(ps_u[:, ts], wu_b[:, kc * 128:(kc + 1) * 128],
                                       xln[:, kc * TOK + tc * 512: kc * TOK + (tc + 1) * 512],
                                       start=(kc == 0), stop=(kc == 15))
                    pr.inc("pe_gu", inst)
                for dh in range(2):
                    for dgl in range(2):
                        dg = 2 * dh + dgl
                        pr.wge(eng, "deq_d", 16 * fb + 8 * (dh + 1))
                        if dg == 0:
                            pr.wge(eng, "dve_gu", 8 * (fb + 1))
                        if dg > 0:
                            pr.wge(eng, "dve_d", 16 * fb + 4 * dg)
                        inst = None
                        for dl in range(4):
                            dt = dg * 4 + dl
                            dloc = dt - 8 * dh
                            psd = pool[:, dl * TOK:(dl + 1) * TOK]
                            for fc in range(FT):
                                for tc in range(2):
                                    ts = slice(tc * 512, (tc + 1) * 512)
                                    inst = eng.matmul(psd[:, ts],
                                               wd_blk[:, fc * 1024 + dloc * 128: fc * 1024 + (dloc + 1) * 128],
                                               h[:, fc * TOK + tc * 512: fc * TOK + (tc + 1) * 512],
                                               start=(fc == 0), stop=(fc == FT - 1))
                        pr.inc("pe_d", inst)

        @blk.vector
        def _(eng):
            for fb in range(NFB):
                for j in range(FT):
                    ft = fb * FT + j
                    pr.wge(eng, "pe_gu", ft + 1)
                    pr.wge(eng, "act_g", ft + 1)
                    if j == 0:
                        pr.wge(eng, "pe_d", 4 * fb)
                    pr.inc("dve_gu", eng.scalar_tensor_tensor(
                        h[:, j * TOK:(j + 1) * TOK], ps_u[:],
                        u_sc[:, ft:ft + 1], hg[:], ALU.mult, ALU.mult))
                for dg in range(4):
                    pr.wge(eng, "pe_d", 4 * fb + dg + 1)
                    for dl in range(4):
                        dt = dg * 4 + dl
                        psd = pool[:, dl * TOK:(dl + 1) * TOK]
                        pr.inc("dve_d", eng.scalar_tensor_tensor(
                            xres[:, dt * TOK:(dt + 1) * TOK], psd,
                            d_sc[:, dt:dt + 1], xres[:, dt * TOK:(dt + 1) * TOK],
                            ALU.mult, ALU.add))

    with nc.Block() as blk:
        @blk.sync
        def _(eng):
            pr.wait(eng, "dve_d")
            for j in range(NF):
                pr.inc("d_out", eng.dma_start(xo[j * 128:(j + 1) * 128, :],
                                              xres[:, j * TOK:(j + 1) * TOK]), 16)
            pr.wait(eng, "d_out")
    ar.free()


SCALE_ATT = 0.08838834764831845


def emit_l1attn(pr, io):
    nc = pr.nc
    pr.prefix = "t_"
    ar = nc.auto_arena()
    for _n in ['d_c', 'd_kv', 'd_out', 'd_wo', 'd_wq', 'd_x', 'ln1bc', 'ln1dn', 'ln1dv', 'ln1pe', 'ln1rt', 'ln1sq', 'ln1st', 'pe_n', 'pe_nb', 'pe_pv', 'pe_qm', 'pe_rr', 'pe_s', 'pe_w', 's_exp', 's_kq', 's_rt2', 'v_ao', 'v_bb', 'v_kc', 'v_n', 'v_qf', 'v_rr', 'v_vc', 'v_wb', 'v_wo', 'vq']:
        pr.sem(_n)
    xf = io["xf"]
    kTf = io["kTf"]
    vTf = io["vTf"]
    wqT = io["wqT"]
    woT = io["woT"]
    onesf = io["onesf"]
    onesb = io["onesb"]
    cstf = io["cstf"]
    xo = io["xo"]

    xres = ar.sbuf_tensor("t_xres", [128, NF * TOK], F32)
    xab = ar.sbuf_tensor("t_xab", [128, NF * TOK], BF16)
    qf = ar.sbuf_tensor("t_qf", [128, NF * TOK], BF16)
    sq = ar.sbuf_tensor("t_sq", [128, 2 * TOK], F32)
    row_m = ar.sbuf_tensor("t_row_m", [1, TOK], F32)
    row_v = ar.sbuf_tensor("t_row_v", [1, TOK], F32)
    srow = ar.sbuf_tensor("t_srow", [1, TOK], F32)
    ones = ar.sbuf_tensor("t_ones", [128, 128], F32)
    onebv = ar.sbuf_tensor("t_onebv", [128, 128], BF16)
    cst = ar.sbuf_tensor("t_cst", [128, 8], F32)
    stg = ar.sbuf_tensor("t_stg", [128, 2048], F32)
    w_b = ar.sbuf_tensor("t_w_b", [128, 2048], BF16)
    kb = ar.sbuf_tensor("t_kb", [128, 4096], BF16)
    vb = ar.sbuf_tensor("t_vb", [128, 4096], BF16)
    ptq = ar.sbuf_tensor("t_ptq", [128, 8 * TOK], BF16)
    rrb = ar.sbuf_tensor("t_rrb", [128, TOK], F32)
    negb = ar.sbuf_tensor("t_negb", [128, 1], F32)

    ar.allocate()
    pool = io["pool"]
    ps_sum = pool[0:1, 0:TOK]
    ps_sq = pool[0:1, TOK:2 * TOK]
    ps_bc = pool[:, 2 * TOK:4 * TOK]
    ps_q = pool[:, 0:TOK]
    ps_o = pool[:, TOK:2 * TOK]
    ps_row = pool[0:1, 2 * TOK:3 * TOK]
    ps_n = pool[0:1, 3 * TOK:4 * TOK]
    ps_nb = pool[:, 3 * TOK:3 * TOK + 1]

    with nc.Block() as blk:
        @blk.sync
        def _(eng):
            for j in range(NF):
                pr.inc("d_x", eng.dma_start(xres[:, j * TOK:(j + 1) * TOK],
                                            xf[j * 128:(j + 1) * 128, :]), 16)
            pr.inc("d_c", eng.dma_start(ones[:], onesf), 16)
            pr.inc("d_c", eng.dma_start(onebv[:], onesb), 16)
            pr.inc("d_c", eng.dma_start(cst[:], cstf), 16)

    emit_ln(pr, "ln1", xres[:], xab[:], ones[:], sq[:], row_m[:], row_v[:],
            ps_sum, ps_sq, ps_bc, TOK,
            [("d_x", pr.val("d_x")), ("d_c", pr.val("d_c"))], cst[0:1, 0:1])

    with nc.Block() as blk:
        @blk.sync
        def _(eng):
            for od in range(16):
                if od > 0:
                    pr.wge(eng, "v_wb", od)
                pr.inc("d_wq", eng.dma_start(
                    stg[:].rearrange("p (kc f) -> p kc f", kc=16),
                    wqT[:, od * 128:(od + 1) * 128].rearrange("(kc kp) f -> kp kc f", kp=128)), 16)

        @blk.vector
        def _(eng):
            for od in range(16):
                pr.wge(eng, "d_wq", 16 * (od + 1))
                if od > 0:
                    pr.wge(eng, "pe_qm", od)
                pr.inc("v_wb", eng.tensor_copy(w_b[:], stg[:]))
                pr.wge(eng, "pe_qm", od + 1)
                pr.inc("v_qf", eng.tensor_copy(qf[:, od * TOK:(od + 1) * TOK], ps_q[:]))

        @blk.tensor
        def _(eng):
            for od in range(16):
                pr.wge(eng, "v_wb", od + 1)
                if od == 0:
                    pr.wait(eng, "ln1dn")
                pr.wge(eng, "v_qf", od)
                inst = None
                for tc in range(2):
                    ts = slice(tc * 512, (tc + 1) * 512)
                    for kc in range(16):
                        inst = eng.matmul(ps_q[:, ts], w_b[:, kc * 128:(kc + 1) * 128],
                                          xab[:, kc * TOK + tc * 512: kc * TOK + (tc + 1) * 512],
                                          start=(kc == 0), stop=(kc == 15))
                pr.inc("pe_qm", inst)

    with nc.Block() as blk:
        @blk.sync
        def _(eng):
            for h in range(16):
                if h == 0:
                    pr.wge(eng, "v_wb", 16)
                else:
                    pr.wge(eng, "v_vc", 2 * h)
                pr.inc("d_kv", eng.dma_start(stg[:], kTf[h, :, 0:2048]), 16)
                pr.wge(eng, "pe_n", 5 * h + 2)
                pr.inc("d_kv", eng.dma_start(stg[:], kTf[h, :, 2048:4096]), 16)
                pr.wge(eng, "pe_n", 5 * h + 4)
                pr.inc("d_kv", eng.dma_start(
                    stg[:].rearrange("p (c d) -> p c d", c=16),
                    vTf[h, 0:2048, :].rearrange("(c p) d -> p c d", p=128)), 16)
                pr.wge(eng, "v_vc", 2 * h + 1)
                pr.inc("d_kv", eng.dma_start(
                    stg[:].rearrange("p (c d) -> p c d", c=16),
                    vTf[h, 2048:4096, :].rearrange("(c p) d -> p c d", p=128)), 16)

        @blk.scalar
        def _(eng):
            for h in range(16):
                if h == 0:
                    pr.wait(eng, "ln1dn")
                for half in range(2):
                    pr.wge(eng, "v_kc", 2 * h + half + 1)
                    pr.inc("s_kq", eng.activation(stg[:], kb[:, half * 2048:(half + 1) * 2048],
                                                  AF.Square))
                pr.wge(eng, "pe_n", 5 * h)
                pr.wge(eng, "v_qf", h + 1)
                pr.inc("s_kq", eng.activation(sq[:, 0:TOK], qf[:, h * TOK:(h + 1) * TOK],
                                              AF.Square))
                pr.wge(eng, "v_n", 7 * h + 6)
                pr.inc("s_rt2", eng.activation(row_m[0:1, 3:4], row_m[0:1, 2:3], AF.Sqrt))
                for qt in range(4):
                    for lc in range(8):
                        ktc = qt * 8 + lc
                        pr.wge(eng, "pe_s", 32 * h + ktc + 1)
                        if ktc == 0:
                            pr.wge(eng, "v_bb", h + 1)
                        if qt > 0:
                            pr.wge(eng, "pe_pv", 32 * h + (qt - 1) * 8 + lc + 1)
                        pr.inc("s_exp", eng.activation(
                            ptq[:, lc * TOK:(lc + 1) * TOK], ps_q[:], AF.Exp,
                            bias=negb[:, 0:1]))

        @blk.tensor
        def _(eng):
            for h in range(16):
                for half in range(2):
                    pr.wge(eng, "s_kq", 3 * h + half + 1)
                    for c in range(2):
                        i = 2 * half + c
                        pr.wge(eng, "v_n", 7 * h + i)
                        inst = None
                        for sc in range(2):
                            cs = slice(c * 1024 + sc * 512, c * 1024 + (sc + 1) * 512)
                            os_ = slice(sc * 512, (sc + 1) * 512)
                            inst = eng.matmul(ps_n[0:1, os_], ones[:, 0:1], stg[:, cs],
                                              start=True, stop=True)
                        pr.inc("pe_n", inst)
                pr.wge(eng, "s_kq", 3 * h + 3)
                pr.wge(eng, "v_n", 7 * h + 4)
                inst = None
                for sc in range(2):
                    os_ = slice(sc * 512, (sc + 1) * 512)
                    inst = eng.matmul(ps_n[0:1, os_], ones[:, 0:1], sq[:, sc * 512:(sc + 1) * 512],
                                      start=True, stop=True)
                pr.inc("pe_n", inst)
                pr.wge(eng, "v_n", 7 * h + 7)
                pr.inc("pe_nb", eng.matmul(ps_nb[:, 0:1], ones[0:1, :], row_m[0:1, 4:5],
                                           start=True, stop=True))
                for ktc in range(32):
                    if ktc == 0:
                        pr.wge(eng, "v_kc", 2 * h + 2)
                        pr.wge(eng, "v_rr", 2 * h)
                        pr.wge(eng, "v_ao", h)
                        if h == 0:
                            pr.wge(eng, "v_qf", 16)
                    pr.wge(eng, "s_exp", 32 * h + ktc)
                    inst = None
                    for tc in range(2):
                        ts = slice(tc * 512, (tc + 1) * 512)
                        inst = eng.matmul(ps_q[:, ts], kb[:, ktc * 128:(ktc + 1) * 128],
                                          qf[:, h * TOK + tc * 512: h * TOK + (tc + 1) * 512],
                                          start=True, stop=True)
                    pr.inc("pe_s", inst)
                    lc = ktc % 8
                    pr.wge(eng, "s_exp", 32 * h + ktc + 1)
                    pr.wge(eng, "v_vc", 2 * h + (1 if ktc < 16 else 2))
                    for tc in range(2):
                        ts = slice(tc * 512, (tc + 1) * 512)
                        eng.matmul(ps_row[0:1, ts], onebv[:, 0:1],
                                   ptq[:, lc * TOK + tc * 512: lc * TOK + (tc + 1) * 512],
                                   start=(ktc == 0), stop=(ktc == 31))
                    inst = None
                    for tc in range(2):
                        ts = slice(tc * 512, (tc + 1) * 512)
                        inst = eng.matmul(ps_o[:, ts], vb[:, ktc * 128:(ktc + 1) * 128],
                                          ptq[:, lc * TOK + tc * 512: lc * TOK + (tc + 1) * 512],
                                          start=(ktc == 0), stop=(ktc == 31))
                    pr.inc("pe_pv", inst)
                pr.wge(eng, "v_rr", 2 * h + 1)
                pr.wge(eng, "s_exp", 32 * (h + 1))
                inst = None
                for sc in range(2):
                    os_ = slice(sc * 512, (sc + 1) * 512)
                    inst = eng.matmul(ps_q[:, os_], ones[0:1, :], srow[0:1, os_],
                                      start=True, stop=True)
                pr.inc("pe_rr", inst)

        @blk.vector
        def _(eng):
            for h in range(16):
                for half in range(2):
                    pr.wge(eng, "d_kv", 64 * h + 16 * (half + 1))
                    if h > 0 and half == 0:
                        pr.wge(eng, "pe_pv", 32 * h)
                    pr.inc("v_kc", eng.tensor_scalar(
                        kb[:, half * 2048:(half + 1) * 2048], stg[:], SCALE_ATT, None, ALU.mult))
                    for c in range(2):
                        i = 2 * half + c
                        pr.wge(eng, "pe_n", 5 * h + i + 1)
                        if i == 0:
                            pr.inc("v_n", eng.reduce_max(row_m[0:1, 0:1], ps_n[0:1, 0:TOK],
                                                         axis=AX.X))
                        else:
                            pr.chain(eng, "vq", eng.reduce_max(row_m[0:1, 5:6], ps_n[0:1, 0:TOK],
                                                               axis=AX.X))
                            pr.wge(eng, "v_n", 7 * h + i)
                            pr.inc("v_n", eng.tensor_max(row_m[0:1, 0:1], row_m[0:1, 0:1],
                                                         row_m[0:1, 5:6]))
                for half in range(2):
                    pr.wge(eng, "d_kv", 64 * h + 32 + 16 * (half + 1))
                    if h > 0 and half == 0:
                        pr.wge(eng, "pe_pv", 32 * h)
                    pr.inc("v_vc", eng.tensor_copy(vb[:, half * 2048:(half + 1) * 2048], stg[:]))
                pr.wge(eng, "pe_n", 5 * h + 5)
                pr.inc("v_n", eng.reduce_max(row_m[0:1, 1:2], ps_n[0:1, 0:TOK], axis=AX.X))
                pr.wge(eng, "v_n", 7 * h + 5)
                pr.inc("v_n", eng.tensor_mul(row_m[0:1, 2:3], row_m[0:1, 0:1], row_m[0:1, 1:2]))
                pr.wge(eng, "s_rt2", h + 1)
                pr.inc("v_n", eng.tensor_scalar(row_m[0:1, 4:5], row_m[0:1, 3:4], -1.0, None,
                                                ALU.mult))
                pr.wge(eng, "pe_nb", h + 1)
                pr.inc("v_bb", eng.tensor_copy(negb[:], ps_nb[:, 0:1]))
                pr.wge(eng, "pe_pv", 32 * (h + 1))
                pr.inc("v_rr", eng.reciprocal(srow[0:1, 0:TOK], ps_row[0:1, 0:TOK]))
                pr.wge(eng, "pe_rr", h + 1)
                pr.inc("v_rr", eng.tensor_copy(rrb[:], ps_q[:]))
                pr.wge(eng, "v_rr", 2 * h + 2)
                pr.inc("v_ao", eng.tensor_mul(xab[:, h * TOK:(h + 1) * TOK], ps_o[:], rrb[:]))

    with nc.Block() as blk:
        @blk.sync
        def _(eng):
            for od in range(16):
                if od == 0:
                    pr.wge(eng, "v_vc", 32)
                else:
                    pr.wge(eng, "v_wo", 2 * od)
                pr.inc("d_wo", eng.dma_start(
                    stg[:].rearrange("p (kc f) -> p kc f", kc=16),
                    woT[:, od * 128:(od + 1) * 128].rearrange("(kc kp) f -> kp kc f", kp=128)), 16)

        @blk.vector
        def _(eng):
            for od in range(16):
                pr.wge(eng, "d_wo", 16 * (od + 1))
                if od > 0:
                    pr.wge(eng, "pe_w", od)
                pr.inc("v_wo", eng.tensor_copy(w_b[:], stg[:]))
                pr.wge(eng, "pe_w", od + 1)
                pr.inc("v_wo", eng.tensor_add(xres[:, od * TOK:(od + 1) * TOK],
                                              xres[:, od * TOK:(od + 1) * TOK], ps_q[:]))

        @blk.tensor
        def _(eng):
            for od in range(16):
                pr.wge(eng, "v_wo", 2 * od + 1)
                if od == 0:
                    pr.wge(eng, "v_ao", 16)
                    pr.wge(eng, "v_rr", 32)
                else:
                    pr.wge(eng, "v_wo", 2 * od)
                inst = None
                for tc in range(2):
                    ts = slice(tc * 512, (tc + 1) * 512)
                    for kc in range(16):
                        inst = eng.matmul(ps_q[:, ts], w_b[:, kc * 128:(kc + 1) * 128],
                                          xab[:, kc * TOK + tc * 512: kc * TOK + (tc + 1) * 512],
                                          start=(kc == 0), stop=(kc == 15))
                pr.inc("pe_w", inst)

    with nc.Block() as blk:
        @blk.sync
        def _(eng):
            pr.wait(eng, "v_wo")
            for j in range(NF):
                pr.inc("d_out", eng.dma_start(xo[j * 128:(j + 1) * 128, :],
                                              xres[:, j * TOK:(j + 1) * TOK]), 16)
            pr.wait(eng, "d_out")
    ar.free()


def emit_l1ffn(pr, io):
    nc = pr.nc
    pr.prefix = "m_"
    ar = nc.auto_arena()
    for _n in ['d_c', 'd_gw', 'd_out', 'd_we0', 'd_we1', 'd_wd0', 'd_wd1', 'd_x', 'a_wc', 'a_wdc', 'ln2bc', 'ln2dn', 'ln2dv', 'ln2pe', 'ln2rt', 'ln2sq', 'ln2st', 'pe_cb', 'pe_ct', 'pe_ed', 'pe_eg', 'pe_egu', 'pe_r', 's_r', 's_sil', 'v_cb', 'v_ct', 'v_ed', 'v_gw', 'v_h', 'v_r', 'vq']:
        pr.sem(_n)
    xf = io["xf"]
    gwr = io["gwr"]
    weg = io["weg"]
    weu = io["weu"]
    wed = io["wed"]
    swg = io["swg"]
    swu = io["swu"]
    swd = io["swd"]
    onesf = io["onesf"]
    identf = io["identf"]
    cstf = io["cstf"]
    xo = io["xo"]

    NFT = 11
    xres = ar.sbuf_tensor("m_xres", [128, NF * TOK], F32)
    xln = ar.sbuf_tensor("m_xln", [128, NF * TOK], BF16)
    sq = ar.sbuf_tensor("m_sq", [128, 2 * TOK], F32)
    row_m = ar.sbuf_tensor("m_row_m", [1, TOK], F32)
    row_v = ar.sbuf_tensor("m_row_v", [1, TOK], F32)
    ones = ar.sbuf_tensor("m_ones", [128, 128], F32)
    ident = ar.sbuf_tensor("m_ident", [128, 128], F32)
    cst = ar.sbuf_tensor("m_cst", [128, 8], F32)
    gwb = ar.sbuf_tensor("m_gwb", [128, 128], BF16)
    h = ar.sbuf_tensor("m_h", [128, NFT * TOK], BF16)
    hg = ar.sbuf_tensor("m_hg", [128, TOK], BF16)
    wstgA = ar.sbuf_tensor("m_wstgA", [128, 2048], F32)
    wstgB = ar.sbuf_tensor("m_wstgB", [128, 2048], F32)
    dstg = ar.sbuf_tensor("m_dstg", [128, 2048], F32)
    wgb = ar.sbuf_tensor("m_wgb", [128, 2048], BF16)
    wub = ar.sbuf_tensor("m_wub", [128, 2048], BF16)
    wd_blk = ar.sbuf_tensor("m_wd_blk", [128, NFT * 1024], BF16)
    comb_tm = ar.sbuf_tensor("m_comb_tm", [128, 64], F32)
    probs = ar.sbuf_tensor("m_probs", [128, 8], F32)
    scs = ar.sbuf_tensor("m_scs", [128, 8], F32)
    colm = ar.sbuf_tensor("m_colm", [128, 4], F32)
    crows = ar.sbuf_tensor("m_crows", [1, TOK], F32)
    combB = ar.sbuf_tensor("m_combB", [128, TOK], F32)
    tmpb = ar.sbuf_tensor("m_tmpb", [128, TOK], F32)

    ar.allocate()
    pool = io["pool"]
    ps_sum = pool[0:1, 0:TOK]
    ps_sq = pool[0:1, TOK:2 * TOK]
    ps_bc = pool[:, 2 * TOK:4 * TOK]
    ps_l = pool[:, 0:8]
    ps_g = pool[:, 0:TOK]
    ps_u = pool[:, TOK:2 * TOK]
    ps_ct = pool[0:1, 3 * TOK:3 * TOK + 128]
    ps_cb = pool[:, 3 * TOK:4 * TOK]

    with nc.Block() as blk:
        @blk.sync
        def _(eng):
            for j in range(NF):
                pr.inc("d_x", eng.dma_start(xres[:, j * TOK:(j + 1) * TOK],
                                            xf[j * 128:(j + 1) * 128, :]), 16)
            pr.inc("d_c", eng.dma_start(ones[:], onesf), 16)
            pr.inc("d_c", eng.dma_start(ident[:], identf), 16)
            pr.inc("d_c", eng.dma_start(cst[:], cstf), 16)

    with nc.Block() as blk:
        @blk.sync
        def _(eng):
            pr.inc("d_gw", eng.dma_start(wstgA[:, 0:128], gwr), 16)

        @blk.vector
        def _(eng):
            pr.wait(eng, "d_gw")
            pr.inc("v_gw", eng.tensor_copy(gwb[:], wstgA[:, 0:128]))

    emit_ln(pr, "ln2", xres[:], xln[:], ones[:], sq[:], row_m[:], row_v[:],
            ps_sum, ps_sq, ps_bc, TOK,
            [("d_x", pr.val("d_x")), ("d_c", pr.val("d_c"))], cst[0:1, 0:1])

    for tt in range(8):
        with nc.Block() as blk:
            @blk.tensor
            def _(eng):
                if tt == 0:
                    pr.wait(eng, "ln2dn")
                    pr.wait(eng, "v_gw")
                pr.wge(eng, "v_r", 2 * tt)
                inst = None
                for kc in range(16):
                    inst = eng.matmul(ps_l[:, 0:8], xln[:, kc * TOK + tt * 128: kc * TOK + (tt + 1) * 128],
                                      gwb[:, kc * 8:(kc + 1) * 8],
                                      start=(kc == 0), stop=(kc == 15))
                pr.inc("pe_r", inst)

            @blk.scalar
            def _(eng):
                pr.wge(eng, "v_r", 2 * tt + 1)
                pr.inc("s_r", eng.activation(probs[:], ps_l[:, 0:8], AF.Exp,
                                             bias=colm[:, 1:2]))

            @blk.vector
            def _(eng):
                pr.wge(eng, "pe_r", tt + 1)
                pr.chain(eng, "vq", eng.reduce_max(colm[:, 0:1], ps_l[:, 0:8], axis=AX.X))
                pr.inc("v_r", eng.tensor_scalar(colm[:, 1:2], colm[:, 0:1], -1.0, None, ALU.mult))
                pr.wge(eng, "s_r", tt + 1)
                pr.chain(eng, "vq", eng.reduce_sum(colm[:, 0:1], probs[:], axis=AX.X))
                pr.chain(eng, "vq", eng.reciprocal(colm[:, 1:2], colm[:, 0:1]))
                pr.chain(eng, "vq", eng.tensor_scalar(scs[:], probs[:], colm[:, 1:2], None, ALU.mult))
                pr.chain(eng, "vq", eng.reduce_max(colm[:, 0:1], scs[:], axis=AX.X))
                pr.chain(eng, "vq", eng.tensor_scalar(probs[:], scs[:], colm[:, 0:1], None, ALU.is_ge))
                pr.chain(eng, "vq", eng.scalar_tensor_tensor(probs[:], probs[:], -1e30, scs[:],
                                                             ALU.mult, ALU.add))
                pr.chain(eng, "vq", eng.reduce_max(colm[:, 2:3], probs[:], axis=AX.X))
                pr.chain(eng, "vq", eng.tensor_scalar(probs[:], scs[:], colm[:, 2:3], None, ALU.is_ge))
                pr.inc("v_r", eng.tensor_mul(comb_tm[:, tt * 8:(tt + 1) * 8], scs[:], probs[:]))

    def g_slab(r, ft):
        if r < 8:
            return weg[r, :, ft * 128:(ft + 1) * 128]
        return swg[:, (r - 8) * 1408 + ft * 128: (r - 8) * 1408 + (ft + 1) * 128]

    def u_slab(r, ft):
        if r < 8:
            return weu[r, :, ft * 128:(ft + 1) * 128]
        return swu[:, (r - 8) * 1408 + ft * 128: (r - 8) * 1408 + (ft + 1) * 128]

    def d_slab(r, fc):
        if r < 8:
            return wed[r, fc * 128:(fc + 1) * 128, :]
        return swd[(r - 8) * 1408 + fc * 128: (r - 8) * 1408 + (fc + 1) * 128, :]

    wstgs = [wstgA, wstgB]

    with nc.Block() as blk:
        @blk.sync
        def _(eng):
            for r in range(10):
                for ft in range(NFT):
                    ftg = r * NFT + ft
                    p = ftg % 2
                    dsem = "d_we0" if p == 0 else "d_we1"
                    if ftg == 0:
                        pr.wge(eng, "v_gw", 1)
                    if ftg >= 2:
                        pr.wge(eng, "a_wc", 2 * ftg - 2)
                    pr.inc(dsem, eng.dma_start(
                        wstgs[p][:].rearrange("p (kc f) -> p kc f", kc=16),
                        g_slab(r, ft).rearrange("(kc kp) f -> kp kc f", kp=128)), 16)
                    pr.wge(eng, "a_wc", 2 * ftg + 1)
                    pr.inc(dsem, eng.dma_start(
                        wstgs[p][:].rearrange("p (kc f) -> p kc f", kc=16),
                        u_slab(r, ft).rearrange("(kc kp) f -> kp kc f", kp=128)), 16)
                for dh in range(2):
                    for fc in range(NFT):
                        g = 22 * r + dh * NFT + fc
                        q = g % 2
                        if g >= 2:
                            pr.wge(eng, "a_wdc", g - 1)
                        pr.inc("d_wd0" if q == 0 else "d_wd1",
                               eng.dma_start(dstg[:, q * 1024:(q + 1) * 1024],
                                             d_slab(r, fc)[:, dh * 1024:(dh + 1) * 1024]), 16)

        @blk.scalar
        def _(eng):
            for r in range(10):
                for ft in range(NFT):
                    ftg = r * NFT + ft
                    p = ftg % 2
                    dsem = "d_we0" if p == 0 else "d_we1"
                    pr.wge(eng, dsem, 16 * (2 * (ftg // 2) + 1))
                    pr.wge(eng, "pe_egu", ftg)
                    pr.inc("a_wc", eng.activation(wgb[:], wstgs[p][:], AF.Copy))
                    pr.wge(eng, dsem, 16 * (2 * (ftg // 2) + 2))
                    pr.inc("a_wc", eng.activation(wub[:], wstgs[p][:], AF.Copy))
                    pr.wge(eng, "pe_eg", ftg + 1)
                    pr.wge(eng, "v_h", ftg)
                    pr.inc("s_sil", eng.activation(hg[:], ps_g[:], AF.Silu))
                for dh in range(2):
                    for fc in range(NFT):
                        g = 22 * r + dh * NFT + fc
                        q = g % 2
                        pr.wge(eng, "d_wd0" if q == 0 else "d_wd1",
                               16 * (g // 2 + 1))
                        if fc == 0:
                            pr.wge(eng, "pe_ed", 4 * r + 2 * dh)
                        pr.inc("a_wdc", eng.activation(
                            wd_blk[:, fc * 1024:(fc + 1) * 1024],
                            dstg[:, q * 1024:(q + 1) * 1024], AF.Copy))

        @blk.tensor
        def _(eng):
            for r in range(10):
                is_exp = r < 8
                if is_exp:
                    pr.wge(eng, "v_ed", 16 * r)
                    pr.wge(eng, "v_r", 16)
                    for tt in range(8):
                        if tt > 0:
                            pr.wge(eng, "v_ct", 8 * r + tt)
                        pr.inc("pe_ct", eng.transpose(ps_ct[0:1, 0:128],
                                                      comb_tm[:, tt * 8 + r:tt * 8 + r + 1],
                                                      ident[:]))
                    pr.wge(eng, "v_ct", 8 * (r + 1))
                    inst = None
                    for sc_ in range(2):
                        os_ = slice(sc_ * 512, (sc_ + 1) * 512)
                        inst = eng.matmul(ps_cb[:, os_], ones[0:1, :], crows[0:1, os_],
                                          start=True, stop=True)
                    pr.inc("pe_cb", inst)
                for ft in range(NFT):
                    ftg = r * NFT + ft
                    pr.wge(eng, "a_wc", 2 * ftg + 1)
                    if ft == 0:
                        pr.wge(eng, "v_ed", 16 * r)
                    if ftg == 0:
                        pr.wait(eng, "ln2dn")
                        pr.wge(eng, "v_r", 16)
                    pr.wge(eng, "s_sil", ftg)
                    pr.wge(eng, "v_h", ftg)
                    inst = None
                    for tc in range(2):
                        ts = slice(tc * 512, (tc + 1) * 512)
                        for kc in range(16):
                            inst = eng.matmul(ps_g[:, ts], wgb[:, kc * 128:(kc + 1) * 128],
                                              xln[:, kc * TOK + tc * 512: kc * TOK + (tc + 1) * 512],
                                              start=(kc == 0), stop=(kc == 15))
                    pr.inc("pe_eg", inst)
                    pr.wge(eng, "a_wc", 2 * ftg + 2)
                    for tc in range(2):
                        ts = slice(tc * 512, (tc + 1) * 512)
                        for kc in range(16):
                            inst = eng.matmul(ps_u[:, ts], wub[:, kc * 128:(kc + 1) * 128],
                                              xln[:, kc * TOK + tc * 512: kc * TOK + (tc + 1) * 512],
                                              start=(kc == 0), stop=(kc == 15))
                    pr.inc("pe_egu", inst)
                for dh in range(2):
                    for dgl in range(2):
                        dg = 2 * dh + dgl
                        pr.wge(eng, "a_wdc", 22 * r + NFT * (dh + 1))
                        pr.wge(eng, "v_h", NFT * (r + 1))
                        pr.wge(eng, "v_ed", 16 * r + 4 * dg)
                        if is_exp and dg == 3:
                            pr.wge(eng, "v_cb", r + 1)
                        inst = None
                        for dl in range(4):
                            dt = dg * 4 + dl
                            dloc = dt - 8 * dh
                            psd = pool[:, dl * TOK:(dl + 1) * TOK]
                            for fc in range(NFT):
                                for tc in range(2):
                                    ts = slice(tc * 512, (tc + 1) * 512)
                                    inst = eng.matmul(
                                        psd[:, ts],
                                        wd_blk[:, fc * 1024 + dloc * 128: fc * 1024 + (dloc + 1) * 128],
                                        h[:, fc * TOK + tc * 512: fc * TOK + (tc + 1) * 512],
                                        start=(fc == 0), stop=(fc == NFT - 1))
                        pr.inc("pe_ed", inst)

        @blk.vector
        def _(eng):
            for r in range(10):
                is_exp = r < 8
                if is_exp:
                    for tt in range(8):
                        pr.wge(eng, "pe_ct", 8 * r + tt + 1)
                        pr.inc("v_ct", eng.tensor_copy(
                            crows[0:1, tt * 128:(tt + 1) * 128], ps_ct[0:1, 0:128]))
                    pr.wge(eng, "pe_cb", r + 1)
                    pr.inc("v_cb", eng.tensor_copy(combB[:], ps_cb[:]))
                for ft in range(NFT):
                    ftg = r * NFT + ft
                    pr.wge(eng, "pe_egu", ftg + 1)
                    pr.wge(eng, "s_sil", ftg + 1)
                    if ft == 0 and r > 0:
                        pr.wge(eng, "pe_ed", 4 * r)
                    pr.inc("v_h", eng.tensor_mul(h[:, ft * TOK:(ft + 1) * TOK], hg[:], ps_u[:]))
                for dh in range(2):
                    for dgl in range(2):
                        dg = 2 * dh + dgl
                        pr.wge(eng, "pe_ed", 4 * r + dg + 1)
                        if is_exp:
                            pr.wge(eng, "v_cb", r + 1)
                        for dl in range(4):
                            dt = dg * 4 + dl
                            psd = pool[:, dl * TOK:(dl + 1) * TOK]
                            if is_exp:
                                pr.wge(eng, "v_ed", 16 * r + 4 * dg + dl)
                                pr.chain(eng, "vq", eng.tensor_mul(tmpb[:], psd, combB[:]))
                                pr.inc("v_ed", eng.tensor_add(xres[:, dt * TOK:(dt + 1) * TOK],
                                                              xres[:, dt * TOK:(dt + 1) * TOK], tmpb[:]))
                            else:
                                pr.inc("v_ed", eng.scalar_tensor_tensor(
                                    xres[:, dt * TOK:(dt + 1) * TOK], psd, 1.0,
                                    xres[:, dt * TOK:(dt + 1) * TOK], ALU.mult, ALU.add))

    with nc.Block() as blk:
        @blk.sync
        def _(eng):
            pr.wait(eng, "v_ed")
            for j in range(NF):
                pr.inc("d_out", eng.dma_start(xo[j * 128:(j + 1) * 128, :],
                                              xres[:, j * TOK:(j + 1) * TOK]), 16)
            pr.wait(eng, "d_out")
    ar.free()


def _l0a_maps(inputs, x):
    winT = np.ascontiguousarray(np.asarray(inputs["L0_attn_in_proj_w"], np.float32).T)
    conv_w = np.asarray(inputs["L0_attn_conv_w"], np.float32)
    cwf = np.ascontiguousarray(conv_w.reshape(32, 128, 4).transpose(1, 0, 2).reshape(128, 128))
    conv_b = np.asarray(inputs["L0_attn_conv_b"], np.float32)
    cbf = np.ascontiguousarray(conv_b.reshape(32, 128).T)
    wxT = np.ascontiguousarray(np.asarray(inputs["L0_attn_x_proj_w"], np.float32).T)
    maps = []
    for c in range(NCORES):
        seq, half = c // 2, c % 2
        t0 = half * TOK
        if half == 0:
            xe = np.concatenate([np.zeros((3, D), np.float32), x[seq, 0:TOK]], axis=0)
        else:
            xe = x[seq, t0 - 3:t0 + TOK]
        xfh = np.ascontiguousarray(xe.T)
        maps.append(dict(xf=xfh, winT=winT, cwf=cwf, cbf=cbf, wxT=wxT,
                         onesf=_ONES, cstf=_CST))
    return maps


def _l0b_maps(inputs, xf_percore):
    dtwT = np.ascontiguousarray(np.asarray(inputs["L0_attn_dt_proj_w"], np.float32).T)
    dtb = np.ascontiguousarray(np.asarray(inputs["L0_attn_dt_proj_b"], np.float32).reshape(64, 1))
    alog = np.ascontiguousarray(np.asarray(inputs["L0_attn_A_log"], np.float32).reshape(64, 1))
    Dv = np.asarray(inputs["L0_attn_D"], np.float32)
    ddc = np.zeros((128, 32), np.float32)
    for J in range(32):
        ddc[0:64, J] = Dv[2 * J]
        ddc[64:128, J] = Dv[2 * J + 1]
    woutT = np.ascontiguousarray(np.asarray(inputs["L0_attn_out_proj_w"], np.float32).T)
    repa = np.ascontiguousarray(np.kron(np.eye(64), np.ones((1, 64))).astype(BF))
    trilq = np.triu(np.ones((64, 64), np.float32))
    tril2 = np.ascontiguousarray(np.tile(trilq, (1, 64)).astype(BF))
    repi = np.ascontiguousarray(np.tile(np.eye(64, dtype=np.float32), (1, 64)).astype(BF))
    repd = np.zeros((64, 4096), BF)
    for J in range(32):
        repd[2 * J, J * 128:J * 128 + 64] = 1
        repd[2 * J + 1, J * 128 + 64:J * 128 + 128] = 1
    trius = np.ascontiguousarray(np.tril(np.ones((64, 64), np.float32), -1).astype(BF))
    maskls = np.ascontiguousarray(np.tril(np.ones((64, 64), np.float32)).astype(BF))
    maps = []
    for c in range(NCORES):
        selv = np.zeros((128, 8), np.float32)
        if c % 2 == 1:
            selv[:, c - 1] = 1.0
        maps.append(dict(xf=xf_percore[c], dtwT=dtwT, dtb=dtb, alog=alog, ddf=ddc,
                         woutT=woutT, selvf=selv, onesf=_ONES, repaf=repa, trilf=tril2,
                         repif=repi, repdf=repd, triusf=trius, maskls=maskls, cstf=_CST))
    return maps


def _moe_maps(inputs, xf_percore):
    gw = np.asarray(inputs["L1_ffn_gate_w"], np.float32)
    gwr = np.ascontiguousarray(gw.reshape(16, 128, 8).transpose(1, 0, 2).reshape(128, 128))
    weg = np.ascontiguousarray(np.asarray(inputs["L1_ffn_w_gate"], np.float32))
    weu = np.ascontiguousarray(np.asarray(inputs["L1_ffn_w_up"], np.float32))
    wed = np.ascontiguousarray(np.asarray(inputs["L1_ffn_w_down"], np.float32))
    swg = np.ascontiguousarray(np.asarray(inputs["L1_ffn_sw_gate"], np.float32))
    swu = np.ascontiguousarray(np.asarray(inputs["L1_ffn_sw_up"], np.float32))
    swd = np.ascontiguousarray(np.asarray(inputs["L1_ffn_sw_down"], np.float32))
    return [dict(xf=xf_percore[c], gwr=gwr, weg=weg, weu=weu, wed=wed,
                 swg=swg, swu=swu, swd=swd, onesf=_ONES, identf=_IDENT, cstf=_CST)
            for c in range(NCORES)]


TOKH = 1027


def emit_l0a(pr, io):
    nc = pr.nc
    pr.prefix = "a_"
    ar = nc.auto_arena()
    for _n in ['d_c', 'd_lx0', 'd_lx1', 'd_ly0', 'd_ly1', 'd_wi', 'd_wx', 'd_wz', 'd_xo0', 'd_xo1', 'd_zo0', 'd_zo1', 'lnA_pe', 'lnA_sq', 'lnC_dv', 'ln_bc', 'ln_dv', 'ln_rt', 'pe_ip', 'pe_xp', 'pe_z', 's_u', 's_z', 'v_cv', 'v_wc', 'v_wx', 'v_xe', 'vq']:
        pr.sem(_n)
    xf = io["xf"]
    winT = io["winT"]
    cwf = io["cwf"]
    cbf = io["cbf"]
    wxT = io["wxT"]
    onesf = io["onesf"]
    cstf = io["cstf"]
    xfm = io["xfm"]
    bcdt = io["bcdt"]
    zsf = io["zsf"]

    xln = ar.sbuf_tensor("a_xln", [128, NF * TOKH], BF16)
    u = ar.sbuf_tensor("a_u", [128, 32 * TOK], BF16)
    xstg = ar.sbuf_tensor("a_xstg", [128, 2 * TOKH], F32)
    sq = ar.sbuf_tensor("a_sq", [128, 2 * TOKH], F32)
    row_m = ar.sbuf_tensor("a_row_m", [1, TOKH], F32)
    row_v = ar.sbuf_tensor("a_row_v", [1, TOKH], F32)
    ones = ar.sbuf_tensor("a_ones", [128, 128], F32)
    cst = ar.sbuf_tensor("a_cst", [128, 8], F32)
    cw = ar.sbuf_tensor("a_cw", [128, 128], F32)
    cb = ar.sbuf_tensor("a_cb", [128, 32], F32)
    wstg = ar.sbuf_tensor("a_wstg", [128, 2048], F32)
    w_b = ar.sbuf_tensor("a_w_b", [128, 4096], BF16)
    xb = ar.sbuf_tensor("a_xb", [128, TOKH], F32)
    acc1 = ar.sbuf_tensor("a_acc1", [128, TOK], F32)
    acc2 = ar.sbuf_tensor("a_acc2", [128, TOK], F32)
    ostg = ar.sbuf_tensor("a_ostg", [128, 2 * TOK], BF16)

    ar.allocate()
    pool = io["pool"]
    ps_sum = pool[0:1, 0:TOKH]
    ps_sq = pool[0:1, 1536:1536 + TOKH]
    ps_bcm = pool[:, 0:TOKH]
    ps_bcr = pool[:, 1536:1536 + TOKH]
    ps_m = pool[:, 0:TOKH]
    ps_x = pool[:, 0:TOK]

    with nc.Block() as blk:
        @blk.sync
        def _(eng):
            pr.inc("d_c", eng.dma_start(ones[:], onesf), 16)
            pr.inc("d_c", eng.dma_start(cst[:], cstf), 16)
            pr.inc("d_c", eng.dma_start(cw[:], cwf), 16)
            pr.inc("d_c", eng.dma_start(cb[:], cbf), 16)

    with nc.Block() as blk:
        @blk.sync
        def _(eng):
            for j in range(NF):
                if j >= 2:
                    pr.wge(eng, "lnA_pe", j - 1)
                pr.inc("d_lx%d" % (j % 2), eng.dma_start(
                    xstg[:, (j % 2) * TOKH:(j % 2 + 1) * TOKH],
                    xf[j * 128:(j + 1) * 128, :]), 16)

        @blk.scalar
        def _(eng):
            for j in range(NF):
                pr.wge(eng, "d_lx%d" % (j % 2), 16 * (j // 2 + 1))
                if j >= 2:
                    pr.wge(eng, "lnA_pe", j - 1)
                pr.inc("lnA_sq", eng.activation(sq[:, (j % 2) * TOKH:(j % 2 + 1) * TOKH],
                                                xstg[:, (j % 2) * TOKH:(j % 2 + 1) * TOKH],
                                                AF.Square))

        @blk.tensor
        def _(eng):
            pr.wait(eng, "d_c")
            for j in range(NF):
                pr.wge(eng, "d_lx%d" % (j % 2), 16 * (j // 2 + 1))
                xt = xstg[:, (j % 2) * TOKH:(j % 2 + 1) * TOKH]
                for c in range(3):
                    c0, c1 = c * 512, min((c + 1) * 512, TOKH)
                    eng.matmul(ps_sum[0:1, c0:c1], ones[:, 0:1], xt[:, c0:c1],
                               start=(j == 0), stop=(j == NF - 1))
                pr.wge(eng, "lnA_sq", j + 1)
                sqt = sq[:, (j % 2) * TOKH:(j % 2 + 1) * TOKH]
                inst = None
                for c in range(3):
                    c0, c1 = c * 512, min((c + 1) * 512, TOKH)
                    inst = eng.matmul(ps_sq[0:1, c0:c1], ones[:, 0:1], sqt[:, c0:c1],
                                      start=(j == 0), stop=(j == NF - 1))
                pr.inc("lnA_pe", inst)

    with nc.Block() as blk:
        @blk.vector
        def _(eng):
            pr.wge(eng, "lnA_pe", NF)
            pr.chain(eng, "vq", eng.tensor_scalar(row_m[0:1, :], ps_sum[0:1, :], 1.0 / D, None, ALU.mult))
            pr.chain(eng, "vq", eng.tensor_scalar(row_v[0:1, :], ps_sq[0:1, :], 1.0 / D, None, ALU.mult))
            pr.chain(eng, "vq", eng.tensor_mul(sq[0:1, 0:TOKH], row_m[0:1, :], row_m[0:1, :]))
            pr.inc("ln_dv", eng.tensor_sub(row_v[0:1, :], row_v[0:1, :], sq[0:1, 0:TOKH]))

        @blk.scalar
        def _(eng):
            pr.wait(eng, "ln_dv")
            pr.wait(eng, "d_c")
            pr.inc("ln_rt", eng.activation(row_v[0:1, :], row_v[0:1, :], AF.Sqrt,
                                           bias=cst[0:1, 0:1]))

    with nc.Block() as blk:
        @blk.vector
        def _(eng):
            pr.wait(eng, "ln_rt")
            pr.inc("ln_rt", eng.reciprocal(row_v[0:1, :], row_v[0:1, :]))

        @blk.tensor
        def _(eng):
            pr.wge(eng, "ln_rt", 2)
            inst = None
            for c in range(3):
                c0, c1 = c * 512, min((c + 1) * 512, TOKH)
                eng.matmul(ps_bcm[:, c0:c1], ones[0:1, :], row_m[0:1, c0:c1],
                           start=True, stop=True)
                inst = eng.matmul(ps_bcr[:, c0:c1], ones[0:1, :],
                                  row_v[0:1, c0:c1], start=True, stop=True)
            pr.inc("ln_bc", inst)

    with nc.Block() as blk:
        @blk.sync
        def _(eng):
            for j in range(NF):
                if j >= 2:
                    pr.wge(eng, "lnC_dv", j - 1)
                pr.inc("d_ly%d" % (j % 2), eng.dma_start(
                    xstg[:, (j % 2) * TOKH:(j % 2 + 1) * TOKH],
                    xf[j * 128:(j + 1) * 128, :]), 16)

        @blk.vector
        def _(eng):
            pr.wait(eng, "ln_bc")
            for j in range(NF):
                pr.wge(eng, "d_ly%d" % (j % 2), 16 * (j // 2 + 1))
                tmp = sq[:, (j % 2) * TOKH:(j % 2 + 1) * TOKH]
                pr.chain(eng, "vq", eng.tensor_sub(tmp, xstg[:, (j % 2) * TOKH:(j % 2 + 1) * TOKH],
                                                   ps_bcm[:]))
                pr.inc("lnC_dv", eng.tensor_mul(xln[:, j * TOKH:(j + 1) * TOKH], tmp,
                                                ps_bcr[:]))

    with nc.Block() as blk:
        @blk.sync
        def _(eng):
            for j in range(32):
                if j > 0:
                    pr.wge(eng, "v_wc", j)
                pr.inc("d_wi", eng.dma_start(
                    wstg[:].rearrange("p (kc f) -> p kc f", kc=16),
                    winT[:, j * 128:(j + 1) * 128].rearrange("(kc kp) f -> kp kc f", kp=128)), 16)

        @blk.vector
        def _(eng):
            for j in range(32):
                pr.wge(eng, "d_wi", 16 * (j + 1))
                if j > 0:
                    pr.wge(eng, "pe_ip", j)
                pr.inc("v_wc", eng.tensor_copy(w_b[:, 0:2048], wstg[:]))
                if j == 0:
                    pr.wait(eng, "d_c")
                pr.wge(eng, "pe_ip", j + 1)
                pr.wge(eng, "s_u", j)
                pr.chain(eng, "vq", eng.tensor_copy(xb[:], ps_m[:]))
                pr.chain(eng, "vq", eng.tensor_scalar(acc1[:], xb[:, 0:TOK], cw[:, 4 * (j % 32):4 * (j % 32) + 1], None, ALU.mult))
                pr.chain(eng, "vq", eng.scalar_tensor_tensor(acc2[:], xb[:, 1:1 + TOK], cw[:, 4 * (j % 32) + 1:4 * (j % 32) + 2], acc1[:], ALU.mult, ALU.add))
                pr.chain(eng, "vq", eng.scalar_tensor_tensor(acc1[:], xb[:, 2:2 + TOK], cw[:, 4 * (j % 32) + 2:4 * (j % 32) + 3], acc2[:], ALU.mult, ALU.add))
                pr.inc("v_cv", eng.scalar_tensor_tensor(acc2[:], xb[:, 3:3 + TOK], cw[:, 4 * (j % 32) + 3:4 * (j % 32) + 4], acc1[:], ALU.mult, ALU.add))

        @blk.scalar
        def _(eng):
            for j in range(32):
                if j == 0:
                    pr.wait(eng, "d_c")
                pr.wge(eng, "v_cv", j + 1)
                pr.inc("s_u", eng.activation(u[:, j * TOK:(j + 1) * TOK], acc2[:], AF.Silu,
                                             bias=cb[:, j:j + 1]))

        @blk.tensor
        def _(eng):
            for j in range(32):
                pr.wge(eng, "v_wc", j + 1)
                if j == 0:
                    pr.wge(eng, "lnC_dv", NF)
                pr.wge(eng, "v_cv", j)
                inst = None
                for c in range(3):
                    c0, c1 = c * 512, min((c + 1) * 512, TOKH)
                    for kc in range(16):
                        inst = eng.matmul(ps_m[:, c0:c1], w_b[:, kc * 128:(kc + 1) * 128],
                                          xln[:, kc * TOKH + c0: kc * TOKH + c1],
                                          start=(kc == 0), stop=(kc == 15))
                pr.inc("pe_ip", inst)

    NXF = 35
    with nc.Block() as blk:
        @blk.sync
        def _(eng):
            for xfc in range(NXF):
                fw = 128 if xfc < 34 else 64
                for piece in range(2):
                    if xfc + piece > 0:
                        pr.wge(eng, "v_wx", 2 * xfc + piece)
                    pr.inc("d_wx", eng.dma_start(
                        wstg[:, 0:16 * fw].rearrange("p (kc f) -> p kc f", kc=16),
                        wxT[piece * 2048:(piece + 1) * 2048,
                            xfc * 128:xfc * 128 + fw].rearrange("(kc kp) f -> kp kc f", kp=128)), 16)

        @blk.vector
        def _(eng):
            for xfc in range(NXF):
                fw = 128 if xfc < 34 else 64
                for piece in range(2):
                    pr.wge(eng, "d_wx", 16 * (2 * xfc + piece + 1))
                    if xfc > 0 and piece == 0:
                        pr.wge(eng, "pe_xp", xfc)
                    pr.inc("v_wx", eng.tensor_copy(
                        w_b[:, piece * 2048:piece * 2048 + 16 * fw], wstg[:, 0:16 * fw]))
                pr.wge(eng, "pe_xp", xfc + 1)
                if xfc >= 2:
                    pr.wge(eng, "d_xo%d" % (xfc % 2), 16 * ((xfc - 2) // 2 + 1))
                pr.inc("v_xe", eng.tensor_copy(ostg[0:fw, (xfc % 2) * TOK:(xfc % 2 + 1) * TOK],
                                               ps_x[0:fw, :]))

        @blk.tensor
        def _(eng):
            for xfc in range(NXF):
                fw = 128 if xfc < 34 else 64
                pr.wge(eng, "v_wx", 2 * (xfc + 1))
                if xfc == 0:
                    pr.wge(eng, "s_u", 32)
                pr.wge(eng, "v_xe", xfc)
                inst = None
                for tc in range(2):
                    ts = slice(tc * 512, (tc + 1) * 512)
                    for kc in range(32):
                        inst = eng.matmul(ps_x[0:fw, ts],
                                          w_b[:, (kc // 16) * 2048 + (kc % 16) * fw:
                                              (kc // 16) * 2048 + (kc % 16) * fw + fw],
                                          u[:, kc * TOK + tc * 512: kc * TOK + (tc + 1) * 512],
                                          start=(kc == 0), stop=(kc == 31))
                pr.inc("pe_xp", inst)

        @blk.gpsimd
        def _(eng):
            for xfc in range(NXF):
                fw = 128 if xfc < 34 else 64
                pr.wge(eng, "v_xe", xfc + 1)
                src = ostg[0:fw, (xfc % 2) * TOK:(xfc % 2) * TOK + TOK]
                if xfc < 32:
                    dst = xfm[xfc * 128:(xfc + 1) * 128, :]
                elif xfc < 34:
                    dst = bcdt[(xfc - 32) * 128:(xfc - 31) * 128, :]
                else:
                    dst = bcdt[256:320, :]
                pr.inc("d_xo%d" % (xfc % 2), eng.dma_start(dst, src), 16)

    with nc.Block() as blk:
        @blk.sync
        def _(eng):
            for j in range(32):
                pr.wge(eng, "v_wc", 32 + j)
                if j == 0:
                    pr.wge(eng, "v_wx", 70)
                pr.inc("d_wz", eng.dma_start(
                    wstg[:].rearrange("p (kc f) -> p kc f", kc=16),
                    winT[:, 4096 + j * 128:4096 + (j + 1) * 128].rearrange("(kc kp) f -> kp kc f", kp=128)), 16)

        @blk.vector
        def _(eng):
            for j in range(32):
                pr.wge(eng, "d_wz", 16 * (j + 1))
                if j > 0:
                    pr.wge(eng, "pe_z", j)
                pr.inc("v_wc", eng.tensor_copy(w_b[:, 0:2048], wstg[:]))

        @blk.scalar
        def _(eng):
            for j in range(32):
                pr.wge(eng, "pe_z", j + 1)
                if j == 0:
                    pr.wge(eng, "d_xo0", pr.val("d_xo0"))
                    pr.wge(eng, "d_xo1", pr.val("d_xo1"))
                if j >= 2:
                    pr.wge(eng, "d_zo%d" % (j % 2), 16 * ((j - 2) // 2 + 1))
                pr.inc("s_z", eng.activation(ostg[:, (j % 2) * TOK:(j % 2 + 1) * TOK],
                                             ps_x[:], AF.Silu))

        @blk.tensor
        def _(eng):
            for j in range(32):
                pr.wge(eng, "v_wc", 33 + j)
                pr.wge(eng, "s_z", j)
                if j == 0:
                    pr.wge(eng, "v_xe", NXF)
                inst = None
                for tc in range(2):
                    ts = slice(tc * 512, (tc + 1) * 512)
                    for kc in range(16):
                        inst = eng.matmul(ps_x[:, ts], w_b[:, kc * 128:(kc + 1) * 128],
                                          xln[:, kc * TOKH + 3 + tc * 512: kc * TOKH + 3 + (tc + 1) * 512],
                                          start=(kc == 0), stop=(kc == 15))
                pr.inc("pe_z", inst)

        @blk.gpsimd
        def _(eng):
            for j in range(32):
                pr.wge(eng, "s_z", j + 1)
                pr.inc("d_zo%d" % (j % 2), eng.dma_start(zsf[j * 128:(j + 1) * 128, :],
                                             ostg[:, (j % 2) * TOK:(j % 2 + 1) * TOK]), 16)

    with nc.Block() as blk:
        @blk.sync
        def _(eng):
            pr.wait(eng, "d_xo0")
            pr.wait(eng, "d_xo1")
            pr.wait(eng, "d_zo0")
            pr.wait(eng, "d_zo1")
    ar.free()


def emit_l0b(pr, io):
    nc = pr.nc
    pr.prefix = "b_"
    ar = nc.auto_arena()
    for _n in ['cc', 'd_blo', 'd_bz', 'd_c', 'd_lam', 'd_or0', 'd_or1', 'd_sg', 'd_sh', 'd_wo', 'd_xl0', 'd_xl1', 'd_xlo', 'd_xo0', 'd_xo1', 'd_xr0', 'd_xr1', 'd_xz', 'd_zo0', 'd_zo1', 'd_zs0', 'd_zs1', 'p_db', 'p_dbl', 'p_dx', 'p_dxl', 'p_s0', 'p_sc', 'p_st', 'p_w', 'p_xw', 'pe_a', 'pe_b', 'pe_do', 'pe_ds', 'pe_lam', 'pe_op', 'pe_pp', 'pe_rs', 'pe_s1', 'pe_st', 'pe_w', 'pe_yd', 'pe_yo', 'pe_z', 's_ge', 's_pp', 's_z', 'sq2', 'v_do', 'v_gf', 'v_lam', 'v_lc', 'v_oe', 'v_pp', 'v_ra', 'v_sc', 'v_sel', 'v_sm', 'v_wo', 'v_xd', 'v_xw', 'v_ye', 'v_yg', 'vq']:
        pr.sem(_n)
    xf = io["xf"]
    xfm = io["xfm"]
    bcdt = io["bcdt"]
    zsf = io["zsf"]
    dtwT = io["dtwT"]
    dtb = io["dtb"]
    alog = io["alog"]
    ddf = io["ddf"]
    woutT = io["woutT"]
    onesf = io["onesf"]
    repaf = io["repaf"]
    trilf = io["trilf"]
    repif = io["repif"]
    repdf = io["repdf"]
    triusf = io["triusf"]
    maskls = io["maskls"]
    cstf = io["cstf"]
    xo = io["xo"]

    Y = ar.sbuf_tensor("b_Y", [128, 32 * 512], BF16)
    S_run = ar.sbuf_tensor("b_S_run", [128, 4096], F32)
    Lam = ar.sbuf_tensor("b_Lam", [128, 1024], F32)
    dt = ar.sbuf_tensor("b_dt", [64, 1024], F32)
    scr1 = ar.sbuf_tensor("b_scr1", [128, 1024], F32)
    scr2 = ar.sbuf_tensor("b_scr2", [128, 1024], F32)
    vbuf = scr1[0:64, :]
    t1 = scr2[0:64, :]
    cs = ar.sbuf_tensor("b_cs", [64, 1024], F32)
    zeros64 = ar.sbuf_tensor("b_zeros64", [64, 64], F32)
    A_bf = ar.sbuf_tensor("b_A_bf", [64, 1024], BF16)
    dt_bf = ar.sbuf_tensor("b_dt_bf", [64, 1024], BF16)
    w_bf = ar.sbuf_tensor("b_w_bf", [64, 1024], BF16)
    decay_bf = ar.sbuf_tensor("b_decay_bf", [64, 1024], BF16)
    dtraw = ar.sbuf_tensor("b_dtraw", [64, 1024], BF16)
    lam = ar.sbuf_tensor("b_lam", [64, 16], F32)
    B_fm = ar.sbuf_tensor("b_B_fm", [128, 1024], BF16)
    C_fm = ar.sbuf_tensor("b_C_fm", [128, 1024], BF16)
    C_f32 = ar.sbuf_tensor("b_C_f32", [128, 1024], F32)
    Bz = ar.sbuf_tensor("b_Bz", [128, 128], BF16)
    Gbuf = ar.sbuf_tensor("b_Gbuf", [128, 4096], BF16)
    Gtmp = ar.sbuf_tensor("b_Gtmp", [128, 4096], BF16)
    rhsA = ar.sbuf_tensor("b_rhsA", [128, 4096], BF16)
    Xz = ar.sbuf_tensor("b_Xz", [128, 4096], BF16)
    Xlo = ar.sbuf_tensor("b_Xlo", [64, 4096], BF16)
    Bzl = ar.sbuf_tensor("b_Bzl", [64, 128], BF16)
    Xdt = ar.sbuf_tensor("b_Xdt", [128, 4096], BF16)
    Xw = ar.sbuf_tensor("b_Xw", [128, 4096], BF16)
    S_m = ar.sbuf_tensor("b_S_m", [128, 64], BF16)
    dorep = ar.sbuf_tensor("b_dorep", [128, 2048], BF16)
    ycat = ar.sbuf_tensor("b_ycat", [128, 1024], BF16)
    tmpv = ar.sbuf_tensor("b_tmpv", [128, 1024], F32)
    lamr = tmpv[0:1, 0:1024]
    repa = ar.sbuf_tensor("b_repa", [64, 4096], BF16)
    tril = ar.sbuf_tensor("b_tril", [64, 4096], BF16)
    repi = ar.sbuf_tensor("b_repi", [64, 4096], BF16)
    repd = ar.sbuf_tensor("b_repd", [64, 4096], BF16)
    trius = ar.sbuf_tensor("b_trius", [64, 64], BF16)
    maskl = ar.sbuf_tensor("b_maskl", [64, 64], BF16)
    ddc = ar.sbuf_tensor("b_ddc", [128, 32], F32)
    ones = ar.sbuf_tensor("b_ones", [128, 128], F32)
    cst = ar.sbuf_tensor("b_cst", [128, 8], F32)
    dtw_b = ar.sbuf_tensor("b_dtw_b", [64, 64], BF16)
    dtb_c = ar.sbuf_tensor("b_dtb_c", [64, 1], F32)
    alog_c = ar.sbuf_tensor("b_alog_c", [64, 1], F32)
    Acol = ar.sbuf_tensor("b_Acol", [64, 1], F32)
    selv = ar.sbuf_tensor("b_selv", [128, 8], F32)
    zstg = ar.sbuf_tensor("b_zstg", [128, 2 * 512], BF16)
    xflstg = ar.sbuf_tensor("b_xflstg", [128, 2 * 512], BF16)
    wstg = ar.sbuf_tensor("b_wstg", [128, 2048], F32)
    wout_b = ar.sbuf_tensor("b_wout_b", [128, 4096], BF16)
    xstg = scr1
    ostg = scr2

    ar.allocate()
    pool = io["pool"]
    psYd = pool[:, 0:1024]
    psYo = pool[:, 1024:2048]
    psSt = pool[:, 2048:4096]
    psDo = pool[:, 0:2048]
    psOP = pool[:, 0:512]

    L64 = pool[0:64, :]

    with nc.Block() as blk:
        @blk.sync
        def _(eng):
            for name, dst, srcp in [("c", ones, onesf), ("c", cst, cstf), ("c", repa, repaf),
                                    ("c", tril, trilf), ("c", repi, repif), ("c", repd, repdf),
                                    ("c", trius, triusf), ("c", maskl, maskls), ("c", ddc, ddf),
                                    ("c", dtb_c, dtb), ("c", alog_c, alog)]:
                pr.inc("d_c", eng.dma_start(dst[:], srcp), 16)
            pr.inc("d_c", eng.dma_start(wstg[0:64, 0:64], dtwT), 16)
            pr.inc("d_c", eng.dma_start(selv[:], io["selvf"]), 16)
            pr.inc("d_c", eng.dma_start(dtraw[:], bcdt[0:64, 0:1024]), 16)
            pr.inc("d_c", eng.dma_start(B_fm[:], bcdt[64:192, 0:1024]), 16)
            pr.inc("d_c", eng.dma_start(C_fm[:], bcdt[192:320, 0:1024]), 16)

    with nc.Block() as blk:
        @blk.vector
        def _(eng):
            pr.wait(eng, "d_c")
            pr.chain(eng, "vq", eng.memset(zeros64[:], 0.0))
            pr.chain(eng, "vq", eng.tensor_copy(C_f32[:], C_fm[:]))
            pr.inc("v_pp", eng.tensor_copy(dtw_b[:], wstg[0:64, 0:64]))

        @blk.tensor
        def _(eng):
            pr.wait(eng, "v_pp")
            inst = None
            for tc in range(2):
                inst = eng.matmul(pool[0:64, tc * 512:(tc + 1) * 512], dtw_b[:],
                                  dtraw[:, tc * 512:(tc + 1) * 512], start=True, stop=True)
            pr.inc("pe_pp", inst)

    with nc.Block() as blk:
        @blk.vector
        def _(eng):
            pr.wait(eng, "pe_pp")
            pr.inc("v_pp", eng.tensor_scalar(vbuf, pool[0:64, 0:1024], dtb_c[:, 0:1],
                                             None, ALU.add))

        @blk.scalar
        def _(eng):
            pr.wge(eng, "v_pp", 2)
            pr.chain(eng, "sq2", eng.activation(t1, vbuf, AF.Abs))
            pr.chain(eng, "sq2", eng.activation(t1, t1, AF.Exp, scale=-1.0))
            pr.inc("s_pp", eng.activation(t1, t1, AF.Ln, bias=1.0))
            pr.inc("s_pp", eng.activation(Acol[:], alog_c[:], AF.Exp))

    with nc.Block() as blk:
        @blk.vector
        def _(eng):
            pr.wge(eng, "s_pp", 2)
            pr.chain(eng, "vq", eng.scalar_tensor_tensor(dt[:], vbuf, 0.0, t1,
                                                         ALU.max, ALU.add))
            pr.chain(eng, "vq", eng.tensor_scalar(Acol[:], Acol[:], -1.0, None, ALU.mult))
            pr.chain(eng, "vq", eng.tensor_scalar(vbuf, dt[:], Acol[:, 0:1], None, ALU.mult))
            pr.chain(eng, "vq", eng.tensor_copy(A_bf[:], vbuf))
            pr.chain(eng, "vq", eng.tensor_copy(dt_bf[:], dt[:]))
            for c in range(16):
                sl = slice(c * 64, (c + 1) * 64)
                pr.chain(eng, "vq", eng.tensor_tensor_scan(cs[:, sl], vbuf[:, sl],
                                                           zeros64[:], 0.0, ALU.add, ALU.add))
            for c in range(16):
                sl = slice(c * 64, (c + 1) * 64)
                pr.chain(eng, "vq", eng.tensor_scalar(t1[:, sl], cs[:, sl],
                                                      cs[:, c * 64 + 63:c * 64 + 64], -1.0,
                                                      ALU.subtract, ALU.mult))
            for c in range(16):
                pr.inc("v_lc", eng.tensor_copy(lam[:, c:c + 1], cs[:, c * 64 + 63:c * 64 + 64]))
            pr.inc("v_pp", eng.tensor_copy(A_bf[:, 0:1], A_bf[:, 0:1]))

        @blk.scalar
        def _(eng):
            pr.wge(eng, "v_pp", 3)
            pr.chain(eng, "sq2", eng.activation(t1, t1, AF.Exp))
            pr.inc("s_pp", eng.activation(decay_bf[:], cs[:], AF.Exp))

    with nc.Block() as blk:
        @blk.scalar
        def _(eng):
            pr.wge(eng, "v_lc", 16)
            pr.inc("s_pp", eng.activation(lam[:], lam[:], AF.Exp))

        @blk.vector
        def _(eng):
            pr.wge(eng, "s_pp", 4)
            pr.inc("v_pp", eng.tensor_mul(w_bf[:], t1, dt[:]))

    with nc.Block() as blk:
        @blk.gpsimd
        def _(eng):
            pr.wge(eng, "s_pp", 4)
            pr.inc("d_lam", eng.dma_start(lamr[:], lam[:]), 16)

        @blk.tensor
        def _(eng):
            pr.wait(eng, "d_lam")
            inst = None
            for tc in range(2):
                inst = eng.matmul(pool[:, tc * 512:(tc + 1) * 512], ones[0:1, :],
                                  lamr[0:1, tc * 512:(tc + 1) * 512], start=True, stop=True)
            pr.inc("pe_lam", inst)

    with nc.Block() as blk:
        @blk.vector
        def _(eng):
            pr.wait(eng, "pe_lam")
            pr.inc("v_lam", eng.tensor_copy(Lam[:], pool[:, 0:1024]))

    with nc.Block() as blk:
        @blk.vector
        def _(eng):
            pr.wait(eng, "v_lam")
            pr.inc("p_s0", eng.memset(S_run[:], 0.0))
    with nc.Block() as blk:
        @blk.sync
        def _(eng):
            for z in range(16):
                ev = z % 2 == 0
                Xc = Xz if ev else Xlo
                Bc = Bz if ev else Bzl
                if ev:
                    pr.wge(eng, "p_xw", z)
                    pr.inc("p_dx", eng.dma_start(Xz[:], xfm[:, z * 64:z * 64 + 128],
                                                 transpose=True), 16)
                    pr.wge(eng, "p_st", 2 * z)
                    pr.inc("p_db", eng.dma_start(Bz[:], bcdt[64:192, z * 64:z * 64 + 128],
                                                 transpose=True), 16)

        @blk.gpsimd
        def _(eng):
            for z in range(16):
                ev = z % 2 == 0
                Xc = Xz if ev else Xlo
                Bc = Bz if ev else Bzl
                if not ev:
                    pr.wge(eng, "p_dx", 16 * (z // 2 + 1))
                    pr.wge(eng, "p_xw", z)
                    pr.inc("p_dxl", eng.dma_start(Xlo[:], Xz[64:128, :]), 16)
                    pr.wge(eng, "p_db", 16 * (z // 2 + 1))
                    pr.wge(eng, "p_st", 2 * z)
                    pr.inc("p_dbl", eng.dma_start(Bzl[:], Bz[64:128, :]), 16)

        @blk.tensor
        def _(eng):
            for z in range(16):
                ev = z % 2 == 0
                Xc = Xz if ev else Xlo
                Bc = Bz if ev else Bzl
                if z == 0:
                    pr.wge(eng, "v_pp", 4)
                    pr.wait(eng, "v_lam")
                    pr.wait(eng, "d_c")
                pr.wge(eng, "p_xw", z)
                pr.wge(eng, "p_sc", 64 * z)
                inst = None
                for mc in range(8):
                    ms = slice(mc * 512, (mc + 1) * 512)
                    inst = eng.matmul(L64[:, ms], w_bf[:, z * 64:(z + 1) * 64],
                                      repa[:, ms], start=True, stop=True)
                pr.inc("p_w", inst)
                for hf in range(2):
                    pr.wge(eng, "p_xw", z + 1)
                    if ev:
                        pr.wge(eng, "p_db", 16 * (z // 2 + 1))
                    else:
                        pr.wge(eng, "p_dbl", 16 * (z // 2 + 1))
                    pr.wge(eng, "p_sc", 64 * z + 32 * hf)
                    inst = None
                    for hl in range(32):
                        h = hf * 32 + hl
                        inst = eng.matmul(psSt[:, hl * 64:(hl + 1) * 64], Bc[0:64, :],
                                          Xw[0:64, h * 64:(h + 1) * 64],
                                          start=True, stop=True)
                    pr.inc("p_st", inst)

        @blk.vector
        def _(eng):
            for z in range(16):
                ev = z % 2 == 0
                Xc = Xz if ev else Xlo
                Bc = Bz if ev else Bzl
                pr.wge(eng, "p_w", z + 1)
                if ev:
                    pr.wge(eng, "p_dx", 16 * (z // 2 + 1))
                else:
                    pr.wge(eng, "p_dxl", 16 * (z // 2 + 1))
                pr.wge(eng, "p_st", 2 * z)
                if z == 0:
                    pr.wait(eng, "p_s0")
                pr.inc("p_xw", eng.tensor_mul(Xw[0:64, :], Xc[0:64, :], L64[:, 0:4096]))
                for hf in range(2):
                    pr.wge(eng, "p_st", 2 * z + hf + 1)
                    for hl in range(32):
                        h = hf * 32 + hl
                        pr.inc("p_sc", eng.scalar_tensor_tensor(
                            S_run[:, h * 64:(h + 1) * 64], S_run[:, h * 64:(h + 1) * 64],
                            Lam[:, h * 16 + z:h * 16 + z + 1],
                            psSt[:, hl * 64:(hl + 1) * 64], ALU.mult, ALU.add))

    with nc.Block() as blk:
        @blk.sync
        def _(eng):
            pr.wge(eng, "p_sc", 64 * 16)
            pr.inc("d_sh", eng.dma_start(io["shd"], S_run[:]), 16)

        @blk.gpsimd
        def _(eng):
            pr.wait(eng, "d_sh")
            pr.inc("cc", eng.collective_compute(
                "AllGather", ALU.bypass, [[0, 1, 2, 3, 4, 5, 6, 7]],
                [io["shd"].opt()], [io["gatd"].opt()]))

    with nc.Block() as blk:
        @blk.gpsimd
        def _(eng):
            for r in range(8):
                for q in range(4):
                    pr.wge(eng, "v_sel", 4 * r + q + 1)
                    pr.inc("d_sg", eng.dma_start(
                        scr1[:], io["gatd"][r * 128:(r + 1) * 128, q * 1024:(q + 1) * 1024]), 16)

        @blk.vector
        def _(eng):
            pr.wait(eng, "cc")
            pr.inc("v_sel", eng.memset(S_run[:], 0.0))
            for r in range(8):
                for q in range(4):
                    pr.wge(eng, "d_sg", 16 * (4 * r + q + 1))
                    pr.inc("v_sel", eng.scalar_tensor_tensor(
                        S_run[:, q * 1024:(q + 1) * 1024], scr1[:], selv[:, r:r + 1],
                        S_run[:, q * 1024:(q + 1) * 1024], ALU.mult, ALU.add))

    with nc.Block() as blk:
        @blk.sync
        def _(eng):
            for th in range(2):
                for zl in range(8):
                    z = th * 8 + zl
                    ev = (z % 2 == 0)
                    Xc = Xz if ev else Xlo
                    Bc = Bz if ev else Bzl
                    if ev:
                        if z == 0:
                            pr.wge(eng, "p_xw", 16)
                            pr.wge(eng, "p_st", 32)
                            pr.wge(eng, "v_sel", 33)
                        pr.wge(eng, "v_xw", z)
                        pr.inc("d_xz", eng.dma_start(Xz[:], xfm[:, z * 64:z * 64 + 128],
                                                     transpose=True), 16)
                        pr.wge(eng, "pe_st", 2 * z)
                        pr.inc("d_bz", eng.dma_start(Bz[:], bcdt[64:192, z * 64:z * 64 + 128],
                                                     transpose=True), 16)
                for od in range(16):
                    odg = th * 16 + od
                    for piece in range(2):
                        if odg + piece > 0:
                            pr.wge(eng, "v_wo", 2 * odg + piece)
                        pr.inc("d_wo", eng.dma_start(
                            wstg[:].rearrange("p (kc f) -> p kc f", kc=16),
                            woutT[piece * 2048:(piece + 1) * 2048,
                                  od * 128:(od + 1) * 128].rearrange(
                                      "(kc kp) f -> kp kc f", kp=128)), 16)

        @blk.gpsimd
        def _(eng):
            for th in range(2):
                for zl in range(8):
                    z = th * 8 + zl
                    ev = (z % 2 == 0)
                    Xc = Xz if ev else Xlo
                    Bc = Bz if ev else Bzl
                    if not ev:
                        pr.wge(eng, "d_xz", 16 * (z // 2 + 1))
                        pr.wge(eng, "v_xw", z)
                        pr.inc("d_xlo", eng.dma_start(Xlo[:], Xz[64:128, :]), 16)
                        pr.wge(eng, "d_bz", 16 * (z // 2 + 1))
                        pr.wge(eng, "pe_st", 2 * z)
                        pr.inc("d_blo", eng.dma_start(Bzl[:], Bz[64:128, :]), 16)
                for J in range(32):
                    if th > 0 or J >= 2:
                        pr.wge(eng, "v_yg", (32 * th + J - 1) if J >= 2 else 32 * th)
                    pr.inc("d_zs%d" % (J % 2), eng.dma_start(
                        zstg[:, (J % 2) * 512:(J % 2 + 1) * 512],
                        zsf[J * 128:(J + 1) * 128, th * 512:(th + 1) * 512]), 16)
                    pr.inc("d_xl%d" % (J % 2), eng.dma_start(
                        xflstg[:, (J % 2) * 512:(J % 2 + 1) * 512],
                        xfm[J * 128:(J + 1) * 128, th * 512:(th + 1) * 512]), 16)
                for od in range(16):
                    odg = th * 16 + od
                    pr.wge(eng, "v_oe", odg)
                    if odg == 0:
                        pr.wge(eng, "v_sel", 33)
                    pr.inc("d_xr%d" % (od % 2), eng.dma_start(
                        xstg[:, (od % 2) * 512:(od % 2 + 1) * 512],
                        xf[od * 128:(od + 1) * 128, th * 512:(th + 1) * 512]), 16)
                    if odg > 0:
                        po = (odg - 1) % 16
                        pt = (odg - 1) // 16
                        pr.inc("d_or%d" % (po % 2), eng.dma_start(
                            xo[po * 128:(po + 1) * 128, pt * 512:(pt + 1) * 512],
                            ostg[:, (po % 2) * 512:(po % 2 + 1) * 512]), 16)

        @blk.tensor
        def _(eng):
            for th in range(2):
                for zl in range(8):
                    z = th * 8 + zl
                    ev = (z % 2 == 0)
                    Xc = Xz if ev else Xlo
                    Bc = Bz if ev else Bzl
                    if z == 0:
                        pr.wge(eng, "v_pp", 4)
                        pr.wait(eng, "v_lam")
                        pr.wait(eng, "d_c")
                        pr.wge(eng, "v_sel", 33)
                        pr.wge(eng, "p_xw", 16)
                        pr.wge(eng, "p_sc", 1024)
                    if z == 8:
                        pr.wge(eng, "v_oe", 16)
                    pr.wge(eng, "v_ye", 2 * z)
                    pr.wge(eng, "v_xw", z)
                    inst = None
                    for J in range(32):
                        inst = eng.matmul(psDo[:, J * 64:(J + 1) * 64],
                                          repd[:, J * 128:(J + 1) * 128],
                                          decay_bf[:, z * 64:(z + 1) * 64], start=True, stop=True)
                    pr.inc("pe_do", inst)
                    pr.wge(eng, "v_do", z + 1)
                    pr.wge(eng, "v_sc", 64 * z)
                    inst = None
                    for mc in range(8):
                        ms = slice(mc * 512, (mc + 1) * 512)
                        inst = eng.matmul(L64[:, ms], A_bf[:, z * 64:(z + 1) * 64],
                                          repa[:, ms], start=True, stop=True)
                    pr.inc("pe_a", inst)
                    pr.wge(eng, "v_ra", z + 1)
                    inst = None
                    for mc in range(8):
                        ms = slice(mc * 512, (mc + 1) * 512)
                        inst = eng.matmul(L64[:, ms], dt_bf[:, z * 64:(z + 1) * 64],
                                          repa[:, ms], start=True, stop=True)
                    pr.inc("pe_b", inst)
                    pr.wge(eng, "v_xd", z + 1)
                    inst = None
                    for mc in range(8):
                        ms = slice(mc * 512, (mc + 1) * 512)
                        inst = eng.matmul(L64[:, ms], trius[:], rhsA[0:64, ms],
                                          start=True, stop=True)
                    pr.inc("pe_ds", inst)
                    pr.wge(eng, "s_ge", z + 1)
                    pr.inc("pe_s1", eng.matmul(L64[:, 0:64], C_fm[:, z * 64:(z + 1) * 64],
                                               B_fm[:, z * 64:(z + 1) * 64], start=True, stop=True))
                    pr.wge(eng, "v_sm", z + 1)
                    inst = None
                    for mc in range(8):
                        ms = slice(mc * 512, (mc + 1) * 512)
                        inst = eng.matmul(L64[:, ms], S_m[0:64, :], repi[:, ms],
                                          start=True, stop=True)
                    pr.inc("pe_rs", inst)
                    pr.wge(eng, "v_gf", z + 1)
                    inst = None
                    for mc in range(8):
                        ms = slice(mc * 512, (mc + 1) * 512)
                        inst = eng.matmul(L64[:, ms], w_bf[:, z * 64:(z + 1) * 64],
                                          repa[:, ms], start=True, stop=True)
                    pr.inc("pe_w", inst)
                    for hf in range(2):
                        pr.wge(eng, "v_xw", z + 1)
                        pr.wge(eng, "v_do", z + 1)
                        pr.wge(eng, "v_ye", 2 * z + hf)
                        pr.wge(eng, "v_sc", 64 * z)
                        if zl == 0 and hf == 0 and th == 1:
                            pr.wge(eng, "pe_op", 16)
                            pr.wge(eng, "v_yg", 32)
                        inst = None
                        for j in range(16):
                            J = hf * 16 + j
                            inst = eng.matmul(psYd[0:64, j * 64:(j + 1) * 64],
                                              Xdt[0:64, (2 * J) * 64:(2 * J + 1) * 64],
                                              Gtmp[0:64, (2 * J) * 64:(2 * J + 1) * 64],
                                              start=True, stop=True)
                            inst = eng.matmul(psYd[64:128, j * 64:(j + 1) * 64],
                                              Xdt[0:64, (2 * J + 1) * 64:(2 * J + 2) * 64],
                                              Gtmp[0:64, (2 * J + 1) * 64:(2 * J + 2) * 64],
                                              start=True, stop=True, tile_position=(0, 64))
                        pr.inc("pe_yd", inst)
                        inst = None
                        for j in range(16):
                            J = hf * 16 + j
                            inst = eng.matmul(psYo[:, j * 64:(j + 1) * 64],
                                              S_run[:, J * 128:(J + 1) * 128],
                                              C_f32[:, z * 64:(z + 1) * 64], start=True, stop=True)
                        pr.inc("pe_yo", inst)
                        if ev:
                            pr.wge(eng, "d_bz", 16 * (z // 2 + 1))
                        else:
                            pr.wge(eng, "d_blo", 16 * (z // 2 + 1))
                        pr.wge(eng, "v_sc", 64 * z + 32 * hf)
                        inst = None
                        for hl in range(32):
                            h = hf * 32 + hl
                            inst = eng.matmul(psSt[:, hl * 64:(hl + 1) * 64], Bc[0:64, :],
                                              Xw[0:64, h * 64:(h + 1) * 64],
                                              start=True, stop=True)
                        pr.inc("pe_st", inst)
                for od in range(16):
                    odg = th * 16 + od
                    pr.wge(eng, "v_wo", 2 * odg + 2)
                    if od == 0:
                        pr.wge(eng, "v_yg", 32 * (th + 1))
                        pr.wge(eng, "v_sc", 64 * 8 * (th + 1))
                    pr.wge(eng, "v_oe", odg)
                    inst = None
                    for fc in range(32):
                        inst = eng.matmul(psOP[:],
                                          wout_b[:, fc * 128:(fc + 1) * 128],
                                          Y[:, fc * 512:(fc + 1) * 512],
                                          start=(fc == 0), stop=(fc == 31))
                    pr.inc("pe_op", inst)

        @blk.scalar
        def _(eng):
            for th in range(2):
                for zl in range(8):
                    z = th * 8 + zl
                    ev = (z % 2 == 0)
                    Xc = Xz if ev else Xlo
                    Bc = Bz if ev else Bzl
                    pr.wge(eng, "pe_ds", z + 1)
                    pr.wge(eng, "v_gf", max(0, z - 1))
                    pr.inc("s_ge", eng.activation(Gbuf[0:64, :], L64[:, 0:4096], AF.Exp))

        @blk.vector
        def _(eng):
            for th in range(2):
                for zl in range(8):
                    z = th * 8 + zl
                    ev = (z % 2 == 0)
                    Xc = Xz if ev else Xlo
                    Bc = Bz if ev else Bzl
                    pr.wge(eng, "pe_do", z + 1)
                    pr.inc("v_do", eng.tensor_copy(dorep[:], psDo[:]))
                    pr.wge(eng, "pe_a", z + 1)
                    pr.wge(eng, "pe_ds", max(0, z - 1) if z > 0 else 0)
                    pr.inc("v_ra", eng.tensor_mul(rhsA[0:64, :], L64[:, 0:4096], tril[:]))
                    pr.wge(eng, "pe_b", z + 1)
                    if ev:
                        pr.wge(eng, "d_xz", 16 * (z // 2 + 1))
                    else:
                        pr.wge(eng, "d_xlo", 16 * (z // 2 + 1))
                    pr.wge(eng, "pe_yd", max(0, 2 * (z - 1)))
                    pr.inc("v_xd", eng.tensor_mul(Xdt[0:64, :], Xc[0:64, :], L64[:, 0:4096]))
                    pr.wge(eng, "pe_s1", z + 1)
                    pr.wge(eng, "pe_rs", max(0, z - 1))
                    pr.inc("v_sm", eng.tensor_mul(S_m[0:64, :], L64[:, 0:64], maskl[:]))
                    pr.wge(eng, "pe_rs", z + 1)
                    pr.wge(eng, "pe_yd", max(0, 2 * (z - 1)))
                    pr.inc("v_gf", eng.tensor_mul(Gtmp[0:64, :], Gbuf[0:64, :], L64[:, 0:4096]))
                    pr.wge(eng, "pe_w", z + 1)
                    pr.wge(eng, "pe_st", max(0, 2 * (z - 1)))
                    pr.inc("v_xw", eng.tensor_mul(Xw[0:64, :], Xc[0:64, :], L64[:, 0:4096]))
                    for hf in range(2):
                        pr.wge(eng, "pe_yd", 2 * z + hf + 1)
                        pr.wge(eng, "pe_yo", 2 * z + hf + 1)
                        pr.chain(eng, "vq", eng.tensor_mul(tmpv[:], psYo[:],
                                                           dorep[:, hf * 1024:(hf + 1) * 1024]))
                        pr.chain(eng, "vq", eng.tensor_add(ycat[:], tmpv[:], psYd[:]))
                        inst = None
                        for j in range(16):
                            J = hf * 16 + j
                            inst = eng.tensor_copy(Y[:, J * 512 + zl * 64:J * 512 + (zl + 1) * 64],
                                                   ycat[:, j * 64:(j + 1) * 64])
                        pr.inc("v_ye", inst)
                        pr.wge(eng, "pe_st", 2 * z + hf + 1)
                        for hl in range(32):
                            h = hf * 32 + hl
                            pr.inc("v_sc", eng.scalar_tensor_tensor(
                                S_run[:, h * 64:(h + 1) * 64], S_run[:, h * 64:(h + 1) * 64],
                                Lam[:, h * 16 + z:h * 16 + z + 1],
                                psSt[:, hl * 64:(hl + 1) * 64], ALU.mult, ALU.add))
                for J in range(32):
                    pr.wge(eng, "d_zs%d" % (J % 2), 16 * (th * 16 + J // 2 + 1))
                    pr.wge(eng, "d_xl%d" % (J % 2), 16 * (th * 16 + J // 2 + 1))
                    pr.wge(eng, "v_ye", 16 * (th + 1))
                    pr.chain(eng, "vq", eng.scalar_tensor_tensor(
                    Y[:, J * 512:(J + 1) * 512],
                    xflstg[:, (J % 2) * 512:(J % 2 + 1) * 512],
                    ddc[:, J:J + 1], Y[:, J * 512:(J + 1) * 512],
                    ALU.mult, ALU.add))
                    pr.inc("v_yg", eng.tensor_mul(Y[:, J * 512:(J + 1) * 512],
                                              Y[:, J * 512:(J + 1) * 512],
                                              zstg[:, (J % 2) * 512:(J % 2 + 1) * 512]))
                for od in range(16):
                    odg = th * 16 + od
                    for piece in range(2):
                        pr.wge(eng, "d_wo", 16 * (2 * odg + piece + 1))
                        if piece == 0 and odg > 0:
                            pr.wge(eng, "pe_op", odg)
                        pr.inc("v_wo", eng.tensor_copy(
                            wout_b[:, piece * 2048:(piece + 1) * 2048], wstg[:]))
                    pr.wge(eng, "pe_op", odg + 1)
                    pr.wge(eng, "d_xr%d" % (od % 2), 16 * (th * 8 + od // 2 + 1))
                    if odg >= 2:
                        pr.wge(eng, "d_or%d" % (od % 2), 16 * (th * 8 + od // 2))
                    pr.inc("v_oe", eng.tensor_add(
                        ostg[:, (od % 2) * 512:(od % 2 + 1) * 512],
                        xstg[:, (od % 2) * 512:(od % 2 + 1) * 512], psOP[:]))

    with nc.Block() as blk:
        @blk.gpsimd
        def _(eng):
            pr.wait(eng, "v_oe")
            pr.inc("d_or1", eng.dma_start(xo[15 * 128:16 * 128, 512:1024],
                                          ostg[:, 512:1024]), 16)

        @blk.sync
        def _(eng):
            pr.wait(eng, "d_or0")
            pr.wait(eng, "d_or1")
    ar.free()


def _shard_x(x):
    outs = []
    for c in range(NCORES):
        seq, half = c // 2, c % 2
        t0 = half * TOK
        blk = np.ascontiguousarray(x[seq, t0:t0 + TOK, :].T)
        if half == 0:
            halo = np.zeros((D, 3), np.float32)
        else:
            halo = np.ascontiguousarray(x[seq, t0 - 3:t0, :].T)
        outs.append((blk, halo))
    return outs


_ONES = np.ones((128, 128), np.float32)
_CST = np.zeros((128, 8), np.float32)
_CST[:, 0] = LN_EPS


def _ffn_maps(inputs, xf_percore):
    wgT = np.ascontiguousarray(np.asarray(inputs["L0_ffn_gate_w"], np.int32).T)
    wuT = np.ascontiguousarray(np.asarray(inputs["L0_ffn_up_w"], np.int32).T)
    wdT = np.ascontiguousarray(np.asarray(inputs["L0_ffn_down_w"], np.int32).T)
    gsc = np.ascontiguousarray(np.asarray(inputs["L0_ffn_gate_scale"], np.float32).reshape(64, 128).T)
    usc = np.ascontiguousarray(np.asarray(inputs["L0_ffn_up_scale"], np.float32).reshape(64, 128).T)
    dsc = np.ascontiguousarray(np.asarray(inputs["L0_ffn_down_scale"], np.float32).reshape(16, 128).T)
    return [dict(xf=xf_percore[c], wgT=wgT, wuT=wuT, wdT=wdT,
                 gsc=gsc, usc=usc, dsc=dsc, onesf=_ONES, cstf=_CST) for c in range(NCORES)]


_ONESB = np.ones((128, 128), BF)
_IDENT = np.eye(128, dtype=np.float32)


def _attn_maps(inputs, xf_percore):
    kv_k = np.asarray(inputs["L1_attn_kv_k"], np.float32).reshape(4096, 16, 128)
    kv_v = np.asarray(inputs["L1_attn_kv_v"], np.float32).reshape(4096, 16, 128)
    kTf = np.ascontiguousarray(kv_k.transpose(1, 2, 0))
    vTf = np.ascontiguousarray(kv_v.transpose(1, 0, 2))
    wqT = np.ascontiguousarray(np.asarray(inputs["L1_attn_Wq"], np.float32).T)
    woT = np.ascontiguousarray(np.asarray(inputs["L1_attn_Wo"], np.float32).T)
    return [dict(xf=xf_percore[c], kTf=kTf, vTf=vTf, wqT=wqT, woT=woT,
                 onesf=_ONES, onesb=_ONESB, cstf=_CST) for c in range(NCORES)]


def build_fused():
    pr = Pr()
    nc = pr.nc
    F = F32
    pool = nc.alloc_psum_tensor("pool", [128, 4096], F32)
    fincell = nc.alloc_sbuf_tensor("fincell", [1, 2], F32)

    def ext(n, shp, dty):
        return nc.dram_tensor(n, shp, dty, kind="ExternalInput").ap()

    def internal(n, shp, dty):
        return nc.dram_tensor(n, shp, dty).ap()

    Da = dict(xf=ext("a_xf", [D, TOKH], F), winT=ext("a_winT", [D, 8192], F),
              cwf=ext("a_cwf", [128, 128], F), cbf=ext("a_cbf", [128, 32], F),
              wxT=ext("a_wxT", [4096, 4416], F), onesf=ext("a_onesf", [128, 128], F),
              cstf=ext("a_cstf", [128, 8], F))
    Db = dict(xf=ext("b_xf", [D, TOK], F), dtwT=ext("b_dtwT", [64, 64], F),
              dtb=ext("b_dtb", [64, 1], F), alog=ext("b_alog", [64, 1], F),
              ddf=ext("b_ddf", [128, 32], F), woutT=ext("b_woutT", [4096, D], F),
              selvf=ext("b_selvf", [128, 8], F), onesf=ext("b_onesf", [128, 128], F),
              repaf=ext("b_repaf", [64, 4096], BF16), trilf=ext("b_trilf", [64, 4096], BF16),
              repif=ext("b_repif", [64, 4096], BF16), repdf=ext("b_repdf", [64, 4096], BF16),
              triusf=ext("b_triusf", [64, 64], BF16), maskls=ext("b_maskls", [64, 64], BF16),
              cstf=ext("b_cstf", [128, 8], F))
    Df = dict(wgT=ext("f_wgT", [D, 8192], I32), wuT=ext("f_wuT", [D, 8192], I32),
              wdT=ext("f_wdT", [8192, D], I32), gsc=ext("f_gsc", [128, 64], F),
              usc=ext("f_usc", [128, 64], F), dsc=ext("f_dsc", [128, 16], F),
              onesf=ext("f_onesf", [128, 128], F), cstf=ext("f_cstf", [128, 8], F))
    Dt = dict(kTf=ext("t_kTf", [16, 128, 4096], F), vTf=ext("t_vTf", [16, 4096, 128], F),
              wqT=ext("t_wqT", [D, D], F), woT=ext("t_woT", [D, D], F),
              onesf=ext("t_onesf", [128, 128], F), onesb=ext("t_onesb", [128, 128], BF16),
              cstf=ext("t_cstf", [128, 8], F))
    Dm = dict(gwr=ext("m_gwr", [128, 128], F), weg=ext("m_weg", [8, D, 1408], F),
              weu=ext("m_weu", [8, D, 1408], F), wed=ext("m_wed", [8, 1408, D], F),
              swg=ext("m_swg", [D, 2816], F), swu=ext("m_swu", [D, 2816], F),
              swd=ext("m_swd", [2816, D], F), onesf=ext("m_onesf", [128, 128], F),
              identf=ext("m_identf", [128, 128], F), cstf=ext("m_cstf", [128, 8], F))
    out_xo = nc.dram_tensor("out_xo", [D, TOK], F32, kind="ExternalOutput").ap()

    xfm = internal("i_xfm", [4096, TOK], BF16)
    bcdt = internal("i_bcdt", [320, TOK], BF16)
    zsf = internal("i_zsf", [4096, TOK], BF16)
    x1 = internal("i_x1", [D, TOK], F32)
    x2 = internal("i_x2", [D, TOK], F32)
    x3 = internal("i_x3", [D, TOK], F32)
    shd = internal("i_shd", [128, 4096], F32)
    gatd = internal("i_gatd", [1024, 4096], F32)

    Da.update(xfm=xfm, bcdt=bcdt, zsf=zsf)
    Db.update(xfm=xfm, bcdt=bcdt, zsf=zsf, xo=x1, shd=shd, gatd=gatd)
    Df.update(xf=x1, xo=x2)
    Dt.update(xf=x2, xo=x3)
    Dm.update(xf=x3, xo=out_xo)
    for Dx in (Da, Db, Df, Dt, Dm):
        Dx["pool"] = pool

    with nc.Block() as blk:
        @blk.vector
        def _(eng):
            pr.prefix = ""
            pr.inc("fz", eng.memset(fincell[:], 0.0))
    with nc.Block() as blk:
        @blk.vector
        def _(eng):
            pr.wait(eng, "/fz")

    with nc.named_scope("L0A"):
        emit_l0a(pr, Da)
        pr.barrier(fincell)
    pr.release_prefix()
    with nc.named_scope("L0B"):
        emit_l0b(pr, Db)
        pr.barrier(fincell)
    pr.release_prefix()
    with nc.named_scope("L0F"):
        emit_l0ffn(pr, Df)
        pr.barrier(fincell)
    pr.release_prefix()
    with nc.named_scope("L1A"):
        emit_l1attn(pr, Dt)
        pr.barrier(fincell)
    pr.release_prefix()
    with nc.named_scope("L1F"):
        emit_l1ffn(pr, Dm)
        pr.barrier(fincell)
    build_fused.last_pr = pr
    return nc


def _fused_maps(inputs):
    x = np.asarray(inputs["L0_attn_x"], np.float32)
    shards = _shard_x(x)
    xf_percore = [s[0] for s in shards]
    maps_a = _l0a_maps(inputs, x)
    maps_b = _l0b_maps(inputs, xf_percore)
    maps_f = _ffn_maps(inputs, [None] * NCORES)
    maps_t = _attn_maps(inputs, [None] * NCORES)
    maps_m = _moe_maps(inputs, [None] * NCORES)
    in_maps = []
    for c in range(NCORES):
        m = {}
        for pfx, mm in (("a_", maps_a[c]), ("b_", maps_b[c]), ("f_", maps_f[c]),
                        ("t_", maps_t[c]), ("m_", maps_m[c])):
            for k, v in mm.items():
                if v is None or (k == "xf" and pfx in ("f_", "t_", "m_")):
                    continue
                m[pfx + k] = v
        in_maps.append(m)
    return in_maps


def kernel(**inputs):
    in_maps = _fused_maps(inputs)
    nc = build_fused()
    res = run_bass_kernel_spmd(nc, in_maps, core_ids=list(range(NCORES)))
    outs = [np.asarray(res.results[c]["out_xo"], np.float32).T for c in range(NCORES)]
    y = np.stack([np.concatenate([outs[2 * s], outs[2 * s + 1]], axis=0) for s in range(4)])
    return y.astype(np.float32)
